# version 01
# baseline (speedup 1.0000x reference)
import numpy as np
from contextlib import ExitStack

import concourse.bass as bass
import concourse.mybir as mybir
from concourse.bass_utils import run_bass_kernel_spmd

F32 = mybir.dt.float32
BF16 = mybir.dt.bfloat16
AF = mybir.ActivationFunctionType
ALU = mybir.AluOpType
AX = mybir.AxisListType

P = 128
D = 1024
H = 16
HD = 64
NT = 16
HALF = 8
SB = 128
LTAB = 8
RB = 16
E = 16
ES = 256
HFF = 256
EPS = 1e-5

DBG_NAMES = ()
A_CFG = dict(bs=(0, 1), halves=(0, 1), tiles=8, buckets=True, stages=99, sub9=99)


class Sm:
    def __init__(self, es, nc, name):
        self.h = es.enter_context(nc.semaphore(name))
        self.n = 0

    def inc(self, instr, k=1):
        instr.then_inc(self.h, k)
        self.n += k
        return self.n


def build_program(dbg_names=(), phases=("A", "B", "F")):
    nc = bass.Bass("TRN2", target_bir_lowering=False, detect_race_conditions=False)
    dbg_outs = {}

    x_d = nc.dram_tensor("x", [2, 2048, D], F32, kind="ExternalInput").ap()
    wq_d = nc.dram_tensor("wq", [128, 8192], F32, kind="ExternalInput").ap()
    wk_d = nc.dram_tensor("wk", [128, 8192], F32, kind="ExternalInput").ap()
    wv_d = nc.dram_tensor("wv", [128, 8192], F32, kind="ExternalInput").ap()
    wo_d = nc.dram_tensor("wo", [128, 8192], F32, kind="ExternalInput").ap()
    bo_d = nc.dram_tensor("bo", [1, D], F32, kind="ExternalInput").ap()
    planes_d = nc.dram_tensor("planes2", [128, 32], F32, kind="ExternalInput").ap()
    protos_d = nc.dram_tensor("protos", [4, 16], F32, kind="ExternalInput").ap()
    ltemp_d = nc.dram_tensor("ltemp", [1, 1], F32, kind="ExternalInput").ap()
    negA_d = nc.dram_tensor("negindA", [64, 1024], F32, kind="ExternalInput").ap()
    negB_d = nc.dram_tensor("negindB", [8, 1024], F32, kind="ExternalInput").ap()
    ident_d = nc.dram_tensor("ident", [128, 128], F32, kind="ExternalInput").ap()
    ones1_d = nc.dram_tensor("ones1", [1, 128], F32, kind="ExternalInput").ap()
    onescol_d = nc.dram_tensor("onescol", [128, 1], F32, kind="ExternalInput").ap()
    mu_d = nc.dram_tensor("mul", [128, 2048], F32, kind="ExternalInput").ap()
    gam_d = nc.dram_tensor("gam", [128, D], F32, kind="ExternalInput").ap()
    bet_d = nc.dram_tensor("bet", [128, D], F32, kind="ExternalInput").ap()
    scl_d = nc.dram_tensor("scl", [128, 1], F32, kind="ExternalInput").ap()
    w1_d = nc.dram_tensor("w1l", [E, 128, 2048], F32, kind="ExternalInput").ap()
    w2_d = nc.dram_tensor("w2l", [E, 128, 2048], F32, kind="ExternalInput").ap()
    b1_d = nc.dram_tensor("b1f", [128, 32], F32, kind="ExternalInput").ap()
    b2_d = nc.dram_tensor("b2r", [256, D], F32, kind="ExternalInput").ap()

    out_d = nc.dram_tensor("out", [2, 2048, D], F32, kind="ExternalOutput").ap()
    x2scr = nc.dram_tensor("x2scr", [2, 2048, D], F32, kind="Internal").ap()

    x_half_v = x_d.rearrange("b (hh t p) d -> b hh p t d", hh=2, t=HALF, p=P)
    x_tile_v = x_d.rearrange("b (t p) d -> b t p d", t=NT, p=P)
    x2_tile_v = x2scr.rearrange("b (t p) d -> b t p d", t=NT, p=P)
    out_tile_v = out_d.rearrange("b (t p) d -> b t p d", t=NT, p=P)

    es = ExitStack()
    with es:
        sP = Sm(es, nc, "sP")
        sV = Sm(es, nc, "sV")
        sA = Sm(es, nc, "sA")
        sG = Sm(es, nc, "sG")
        dXA = Sm(es, nc, "dXA")
        dXp = [Sm(es, nc, "dX0"), Sm(es, nc, "dX1")]
        dCon = Sm(es, nc, "dCon")
        dPro = Sm(es, nc, "dPro")
        dWkv = Sm(es, nc, "dWkv")
        dWqo = Sm(es, nc, "dWqo")
        dE = [Sm(es, nc, "dE0"), Sm(es, nc, "dE1")]
        dSp = [Sm(es, nc, "dS0"), Sm(es, nc, "dS1")]
        dSh = [Sm(es, nc, "dSh0"), Sm(es, nc, "dSh1")]
        dO = Sm(es, nc, "dO")
        dOp = [Sm(es, nc, "dO0"), Sm(es, nc, "dO1")]
        dDbg = Sm(es, nc, "dDbg")

        ptrb = es.enter_context(nc.psum_tensor("ptrb", [P, 1024], BF16))
        ptrf = es.enter_context(nc.psum_tensor("ptrf", [P, 256], F32))
        pA = es.enter_context(nc.psum_tensor("pA", [P, 512], F32))
        pB = es.enter_context(nc.psum_tensor("pB", [P, 512], F32))
        pC = es.enter_context(nc.psum_tensor("pC", [P, 512], F32))
        pD = es.enter_context(nc.psum_tensor("pD", [P, 512], F32))
        pE = es.enter_context(nc.psum_tensor("pE", [P, 1024], F32))

        identb = es.enter_context(nc.sbuf_tensor("identb", [128, 128], BF16))
        identf = es.enter_context(nc.sbuf_tensor("identf", [128, 128], F32))
        onesb = es.enter_context(nc.sbuf_tensor("onesb", [1, 128], BF16))
        ones1f = es.enter_context(nc.sbuf_tensor("ones1f", [1, 128], F32))
        onescolb = es.enter_context(nc.sbuf_tensor("onescolb", [128, 1], BF16))
        negA = es.enter_context(nc.sbuf_tensor("negA", [64, 1024], BF16))
        negB = es.enter_context(nc.sbuf_tensor("negB", [8, 1024], BF16))
        planes = es.enter_context(nc.sbuf_tensor("planes", [128, 32], BF16))
        PbigF = es.enter_context(nc.sbuf_tensor("PbigF", [32, 128], F32))
        PbigS = es.enter_context(nc.sbuf_tensor("PbigS", [128, 128], BF16))
        bo_s = es.enter_context(nc.sbuf_tensor("bo_s", [1, D], BF16))
        lt_s = es.enter_context(nc.sbuf_tensor("lt_s", [1, 8], F32))
        invs32 = es.enter_context(nc.sbuf_tensor("invs32", [32, 1], F32))

        block = es.enter_context(nc.Block())
        T = block.tensor
        V = block.vector
        A = block.scalar
        G = block.gpsimd
        Q = block.sync

        def dump(name, ap, sem, val):
            if name not in dbg_names:
                return
            dd = nc.dram_tensor("dbg_" + name, list(ap.shape), F32, kind="ExternalOutput").ap()
            dbg_outs[name] = dd

            def _(g):
                g.wait_ge(sem.h, val)
                dDbg.inc(g.dma_start(out=dd, in_=ap), 16)
            G(_)

        def _(g):
            g.dma_start(out=identb[:], in_=ident_d).then_inc(dCon.h, 16)
            g.dma_start(out=identf[:], in_=ident_d).then_inc(dCon.h, 16)
            g.dma_start(out=onesb[:], in_=ones1_d).then_inc(dCon.h, 16)
            g.dma_start(out=ones1f[:], in_=ones1_d).then_inc(dCon.h, 16)
            g.dma_start(out=onescolb[:], in_=onescol_d).then_inc(dCon.h, 16)
            g.dma_start(out=negA[:], in_=negA_d).then_inc(dCon.h, 16)
            g.dma_start(out=negB[:], in_=negB_d).then_inc(dCon.h, 16)
            g.dma_start(out=planes[:], in_=planes_d).then_inc(dCon.h, 16)
            g.dma_start(out=bo_s[:], in_=bo_d).then_inc(dCon.h, 16)
            g.dma_start(out=lt_s[0:1, 0:1], in_=ltemp_d).then_inc(dCon.h, 16)
            dCon.n += 160
        G(_)
        w_pre = dCon.n

        def _(v):
            sV.inc(v.memset(PbigF[:], 0.0))
        V(_)
        v_pz = sV.n

        def _(g):
            g.wait_ge(sV.h, v_pz)
            for l in range(LTAB):
                g.dma_start(out=PbigF[4 * l:4 * l + 4, 16 * l:16 * l + 16], in_=protos_d).then_inc(dPro.h, 16)
            dPro.n += 16 * LTAB
        G(_)
        w_pbig = dPro.n

        def _(s):
            s.wait_ge(dCon.h, w_pre)
            sA.inc(s.activation(lt_s[0:1, 1:2], lt_s[0:1, 0:1], AF.Exp))
        A(_)
        a_e = sA.n

        def _(v):
            v.wait_ge(sA.h, a_e)
            sV.inc(v.tensor_scalar(lt_s[0:1, 2:3], lt_s[0:1, 1:2], 20.0, 0.01, op0=ALU.min, op1=ALU.max))
            v.wait_ge(sV.h, sV.n)
            sV.inc(v.reciprocal(lt_s[0:1, 3:4], lt_s[0:1, 2:3]))
        V(_)
        v_invs = sV.n

        def _(t):
            t.wait_ge(sV.h, v_invs)
            sP.inc(t.matmul(pD[0:32, 0:1], ones1f[0:1, 0:32], lt_s[0:1, 3:4], start=True, stop=True))
        T(_)
        p_inv = sP.n

        def _(s):
            s.wait_ge(sP.h, p_inv)
            sA.inc(s.copy(invs32[:], pD[0:32, 0:1]))
        A(_)
        a_inv = sA.n

        def _(v):
            v.wait_ge(sA.h, a_inv)
            v.wait_ge(dPro.h, w_pbig)
            sV.inc(v.tensor_scalar(PbigS[0:32, :], PbigF[:], invs32[:], None, op0=ALU.mult))
            sV.inc(v.tensor_scalar(PbigS[64:96, :], PbigF[:], invs32[:], None, op0=ALU.mult))
        V(_)
        v_pbig = sV.n
        dump("PbigS", PbigS[0:32, :], sV, v_pbig)

        wq_s = es.enter_context(nc.sbuf_tensor("wq_s", [128, 8192], BF16))
        wo_s = es.enter_context(nc.sbuf_tensor("wo_s", [128, 8192], BF16))

        def _(g):
            g.dma_start(out=wq_s[:], in_=wq_d).then_inc(dWqo.h, 16)
            g.dma_start(out=wo_s[:], in_=wo_d).then_inc(dWqo.h, 16)
            dWqo.n += 32
        G(_)

        stats = es.enter_context(nc.sbuf_tensor("stats", [P, 64], F32))
        negm1 = es.enter_context(nc.sbuf_tensor("negm1", [P, 32], F32))
        r1s = es.enter_context(nc.sbuf_tensor("r1s", [P, 32], F32))
        xn1 = es.enter_context(nc.sbuf_tensor("xn1", [P, D], BF16))
        xn1f = es.enter_context(nc.sbuf_tensor("xn1f", [P, D], BF16))

        esAB = es.enter_context(ExitStack())
        bKTa = [esAB.enter_context(nc.sbuf_tensor(f"bKTa{b}", [64, 2048], F32)) for b in range(2)]
        bVa = [esAB.enter_context(nc.sbuf_tensor(f"bVa{b}", [P, 1024], F32)) for b in range(2)]

        esA = ExitStack()
        wk_s = esA.enter_context(nc.sbuf_tensor("wk_s", [128, 8192], BF16))
        wv_s = esA.enter_context(nc.sbuf_tensor("wv_s", [128, 8192], BF16))
        x_sb = esA.enter_context(nc.sbuf_tensor("x_sb", [P, HALF, D], F32))
        Ktm = esA.enter_context(nc.sbuf_tensor("Ktm", [P, HALF, D], BF16))
        Vtm = esA.enter_context(nc.sbuf_tensor("Vtm", [P, HALF, D], BF16))
        probs = esA.enter_context(nc.sbuf_tensor("probs", [P, HALF, 2048], BF16))
        bnbuf = esA.enter_context(nc.sbuf_tensor("bnbuf", [P, 96], F32))
        tsb = esA.enter_context(nc.sbuf_tensor("tsb", [P, 8], F32))
        tsb2 = esA.enter_context(nc.sbuf_tensor("tsb2", [P, 8], F32))
        Kfm = esA.enter_context(nc.sbuf_tensor("Kfm", [P, D], BF16))
        tanhp = esA.enter_context(nc.sbuf_tensor("tanhp", [P, D], BF16))
        nmaxA = esA.enter_context(nc.sbuf_tensor("nmaxA", [P, 64], F32))
        cTA = esA.enter_context(nc.sbuf_tensor("cTA", [64, 128], BF16))
        expA = esA.enter_context(nc.sbuf_tensor("expA", [P, 1024], F32))
        ZA = esA.enter_context(nc.sbuf_tensor("ZA", [P, 64], F32))
        RA = esA.enter_context(nc.sbuf_tensor("RA", [P, 64], F32))

        def _(g):
            g.dma_start(out=wk_s[:], in_=wk_d).then_inc(dWkv.h, 16)
            g.dma_start(out=wv_s[:], in_=wv_d).then_inc(dWkv.h, 16)
            dWkv.n += 32
        G(_)
        w_attn = dWkv.n

        trk = {
            "x_free": 0,
            "ptrb_free": 0,
            "ptrf_free": 0,
            "pAB_free_pe": 0,
            "pCD_free": 0,
            "pE_free": 0,
            "prod_free_pe": 0,
            "bk_ready": 0,
            "x2_store": 0,
        }

        def emit_A(b):
            for half in A_CFG["halves"]:
                g0 = half * HALF
                x_free = trk["x_free"]
                prod_free = trk["prod_free_pe"]

                def _(q):
                    if x_free:
                        q.wait_ge(sV.h, x_free)
                    dXA.inc(q.dma_start(out=x_sb[:], in_=x_half_v[b, half]), 16)
                Q(_)
                x_ld = dXA.n

                def _(v):
                    v.wait_ge(dXA.h, x_ld)
                    for tt in range(HALF):
                        sV.inc(v.bn_stats(bnbuf[:, tt * 12:tt * 12 + 6], x_sb[:, tt, 0:512]))
                        sV.inc(v.bn_stats(bnbuf[:, tt * 12 + 6:tt * 12 + 12], x_sb[:, tt, 512:1024]))
                    v.wait_ge(sV.h, sV.n)
                    for tt in range(HALF):
                        col = b * 32 + (g0 + tt) * 2
                        sV.inc(v.bn_aggr(stats[:, col:col + 2], bnbuf[:, tt * 12:tt * 12 + 12]))
                    v.wait_ge(sV.h, sV.n)
                    vsl = stats[:, b * 32 + g0 * 2 + 1: b * 32 + (g0 + HALF) * 2: 2]
                    msl = stats[:, b * 32 + g0 * 2: b * 32 + (g0 + HALF) * 2: 2]
                    sV.inc(v.tensor_scalar(tsb[:], vsl, EPS, None, op0=ALU.add))
                    v.wait_ge(sV.h, sV.n)
                    sV.inc(v.reciprocal(tsb2[:], tsb[:]))
                    sV.inc(v.tensor_scalar(negm1[:, b * 16 + g0: b * 16 + g0 + HALF], msl, -1.0, None, op0=ALU.mult))
                V(_)
                v_stats = sV.n
                dump(f"xsb_b{b}h{half}", x_sb[:, 0, :], dXA, x_ld)
                dump(f"stats_b{b}h{half}", stats[:, b * 32 + g0 * 2: b * 32 + (g0 + HALF) * 2], sV, v_stats)

                def _(s):
                    s.wait_ge(sV.h, v_stats)
                    sA.inc(s.activation(r1s[:, b * 16 + g0: b * 16 + g0 + HALF], tsb2[:], AF.Sqrt))
                A(_)
                a_r1 = sA.n
                dump(f"r1s_b{b}h{half}", r1s[:, b * 16 + g0:b * 16 + g0 + HALF], sA, a_r1)

                for tt in range(A_CFG["tiles"]):
                    gt = g0 + tt
                    ncol = b * 16 + gt

                    p_xn1T_prev = trk.get("p_xn1T", 0)

                    def _(v, tt=tt, ncol=ncol, p_xn1T_prev=p_xn1T_prev, a_r1=a_r1):
                        v.wait_ge(sA.h, a_r1)
                        if p_xn1T_prev:
                            v.wait_ge(sP.h, p_xn1T_prev)
                        sV.inc(v.tensor_scalar(xn1[:], x_sb[:, tt, :], negm1[:, ncol:ncol + 1],
                                               r1s[:, ncol:ncol + 1], op0=ALU.add, op1=ALU.mult))
                    V(_)
                    v_xn1 = sV.n

                    if A_CFG['stages'] < 2:
                        continue
                    ptrb_free = trk["ptrb_free"]

                    def _(t, v_xn1=v_xn1, ptrb_free=ptrb_free, prod_free=prod_free):
                        t.wait_ge(sV.h, v_xn1)
                        t.wait_ge(dCon.h, 160)
                        t.wait_ge(dWkv.h, 32)
                        if ptrb_free:
                            t.wait_ge(sA.h, ptrb_free)
                        for kc in range(8):
                            ii = t.transpose(ptrb[:, kc * 128:kc * 128 + 128], xn1[:, kc * 128:kc * 128 + 128], identb[:])
                        sP.inc(ii)
                    T(_)
                    trk["p_xn1T"] = sP.n
                    p_xn1T = sP.n

                    if A_CFG['stages'] < 3:
                        continue
                    def _(s, p_xn1T=p_xn1T):
                        s.wait_ge(sP.h, p_xn1T)
                        sA.inc(s.copy(xn1f[:], ptrb[:]))
                    A(_)
                    a_xn1f = sA.n

                    if A_CFG['stages'] < 4:
                        continue
                    pab_free = trk["pAB_free_pe"]

                    def _(t, a_xn1f=a_xn1f, pab_free=pab_free):
                        t.wait_ge(sA.h, a_xn1f)
                        if pab_free:
                            t.wait_ge(sA.h, pab_free)
                        for kc in range(8):
                            for oc in range(8):
                                bank = pA if oc < 4 else pB
                                ii = t.matmul(bank[:, (oc % 4) * 128:(oc % 4) * 128 + 128],
                                              wk_s[:, kc * 1024 + oc * 128:kc * 1024 + oc * 128 + 128],
                                              xn1f[:, kc * 128:kc * 128 + 128],
                                              start=(kc == 0 and oc % 4 == 0),
                                              stop=(kc == 7 and oc % 4 == 3),
                                              skip_group_check=True)
                        sP.inc(ii)
                    T(_)
                    p_kfm = sP.n

                    if A_CFG['stages'] < 5:
                        continue
                    def _(s, p_kfm=p_kfm):
                        s.wait_ge(sP.h, p_kfm)
                        sA.inc(s.copy(Kfm[:, 0:512], pA[:]))
                        sA.inc(s.copy(Kfm[:, 512:1024], pB[:]))
                    A(_)
                    a_kfm = sA.n

                    if A_CFG['stages'] < 6:
                        continue
                    def _(t, a_kfm=a_kfm):
                        t.wait_ge(sA.h, a_kfm)
                        for oc in range(8):
                            ii = t.transpose(ptrb[:, oc * 128:oc * 128 + 128], Kfm[:, oc * 128:oc * 128 + 128], identb[:])
                        sP.inc(ii)
                    T(_)
                    p_ktT = sP.n

                    def _(s, p_ktT=p_ktT, tt=tt):
                        s.wait_ge(sP.h, p_ktT)
                        sA.inc(s.copy(Ktm[:, tt, :], ptrb[:]))
                    A(_)
                    a_ktm = sA.n
                    trk["ptrb_free"] = a_ktm

                    if A_CFG['stages'] < 7:
                        continue
                    def _(t, a_kfm=a_kfm, v_pbig=v_pbig):
                        t.wait_ge(sV.h, v_pbig)
                        t.wait_ge(sA.h, a_kfm)
                        for h in range(H):
                            bank = pA if h < 8 else pB
                            base = 64 * (h % 2)
                            cs = (h % 8) // 2
                            ii = t.matmul(bank[base:base + 32, cs * 128:cs * 128 + 128],
                                          planes[base:base + 64, :],
                                          Kfm[base:base + 64, (h // 2) * 128:(h // 2) * 128 + 128],
                                          start=(h % 8 in (0, 1)), stop=(h % 8 in (6, 7)),
                                          skip_group_check=True)
                        sP.inc(ii)
                    T(_)
                    p_pj = sP.n

                    def _(s, p_pj=p_pj):
                        s.wait_ge(sP.h, p_pj)
                        sA.inc(s.activation(tanhp[:, 0:512], pA[:], AF.Tanh))
                        sA.inc(s.activation(tanhp[:, 512:1024], pB[:], AF.Tanh))
                    A(_)
                    a_tanh = sA.n
                    trk["pAB_free_pe"] = a_tanh

                    if A_CFG['stages'] < 8:
                        continue
                    pcd_free = trk["pCD_free"]

                    def _(t, a_xn1f=a_xn1f, pcd_free=pcd_free):
                        t.wait_ge(sA.h, a_xn1f)
                        if pcd_free:
                            t.wait_ge(sA.h, pcd_free)
                        for dh in range(2):
                            bank = pC if dh == 0 else pD
                            for kc in range(8):
                                ii = t.matmul(bank[:, 0:512],
                                              xn1f[:, kc * 128:kc * 128 + 128],
                                              wv_s[:, kc * 1024 + dh * 512:kc * 1024 + dh * 512 + 512],
                                              start=(kc == 0), stop=(kc == 7))
                        sP.inc(ii)
                    T(_)
                    p_v = sP.n

                    def _(s, p_v=p_v, tt=tt):
                        s.wait_ge(sP.h, p_v)
                        sA.inc(s.copy(Vtm[:, tt, 0:512], pC[:]))
                        sA.inc(s.copy(Vtm[:, tt, 512:1024], pD[:]))
                    A(_)
                    a_v = sA.n
                    trk["pCD_free"] = a_v

                    if A_CFG['stages'] < 9:
                        continue
                    for grp in range(2):
                        pE_free = trk["pE_free"]

                        def _(t, grp=grp, a_tanh=a_tanh, pE_free=pE_free):
                            t.wait_ge(sA.h, a_tanh)
                            if pE_free:
                                t.wait_ge(sV.h, pE_free)
                            for hl in range(8):
                                h = grp * 8 + hl
                                base = 64 * (h % 2)
                                col = (h // 8) * 512 + ((h % 8) // 2) * 128
                                oc = (hl // 2) * 128 + (hl % 2) * 512
                                ii = t.matmul(pE[:, oc:oc + 128],
                                              tanhp[base:base + 32, col:col + 128],
                                              PbigS[base:base + 32, :],
                                              start=(hl in (0, 1)), stop=(hl in (6, 7)),
                                              skip_group_check=True)
                            sP.inc(ii)
                        T(_)
                        p_lg = sP.n

                        if A_CFG['sub9'] < 2:
                            continue
                        def _(v, p_lg=p_lg):
                            v.wait_ge(sP.h, p_lg)
                            sV.inc(v.tensor_reduce(nmaxA[:], pE[:].rearrange("p (s r) -> p s r", r=RB),
                                                   axis=AX.X, op=ALU.max, negate=False))
                        V(_)
                        v_nm = sV.n

                        if A_CFG['sub9'] < 3:
                            continue
                        ptrf_free = trk["ptrf_free"]

                        def _(t, v_nm=v_nm, ptrf_free=ptrf_free):
                            t.wait_ge(sV.h, v_nm)
                            if ptrf_free:
                                t.wait_ge(sA.h, ptrf_free)
                            sP.inc(t.transpose(ptrf[0:64, 0:128], nmaxA[:], identf[:]))
                        T(_)
                        p_nt = sP.n

                        if A_CFG['sub9'] < 4:
                            continue
                        def _(s, p_nt=p_nt):
                            s.wait_ge(sP.h, p_nt)
                            sA.inc(s.copy(cTA[:], ptrf[0:64, 0:128]))
                        A(_)
                        a_ct = sA.n
                        trk["ptrf_free"] = a_ct

                        if A_CFG['sub9'] < 5:
                            continue
                        def _(t, a_ct=a_ct):
                            t.wait_ge(sA.h, a_ct)
                            for j in range(2):
                                ii = t.matmul(pE[:, j * 512:j * 512 + 512], cTA[:], negA[:, j * 512:j * 512 + 512],
                                              start=False, stop=True, skip_group_check=True)
                            sP.inc(ii)
                        T(_)
                        p_bm = sP.n

                        if A_CFG['sub9'] < 6:
                            continue
                        def _(s, p_bm=p_bm):
                            s.wait_ge(sP.h, p_bm)
                            sA.inc(s.activation(expA[:], pE[:], AF.Exp))
                        A(_)
                        a_exp = sA.n

                        if A_CFG['sub9'] < 7:
                            continue
                        def _(v, a_exp=a_exp):
                            v.wait_ge(sA.h, a_exp)
                            sV.inc(v.tensor_reduce(ZA[:], expA[:].rearrange("p (s r) -> p s r", r=RB),
                                                   axis=AX.X, op=ALU.add))
                            v.wait_ge(sV.h, sV.n)
                            sV.inc(v.reciprocal(RA[:], ZA[:]))
                        V(_)
                        v_r = sV.n

                        if A_CFG['sub9'] < 8:
                            continue
                        def _(v, v_r=v_r, tt=tt, grp=grp):
                            v.wait_ge(sV.h, v_r)
                            dst = probs[:, tt, grp * 1024:grp * 1024 + 1024].rearrange("p (s r) -> p s r", r=RB)
                            src = expA[:].rearrange("p (s r) -> p s r", r=RB)
                            bc = RA[:].unsqueeze(2).broadcast_to([P, 64, RB])
                            sV.inc(v.tensor_tensor(dst, src, bc, op=ALU.mult))
                        V(_)
                        trk["pE_free"] = sV.n
                        v_pm = sV.n

                    if b == 0 and half == 0 and tt == 0 and dbg_names:
                        dump("xn1_t0", xn1[:], sV, v_xn1)
                        dump("xn1f_t0", xn1f[:], sA, a_xn1f)
                        dump("Kfm_t0", Kfm[:], sA, a_kfm)
                        dump("tanhp_t0", tanhp[:], sA, a_tanh)
                        dump("expA_t0", expA[:], sA, a_exp)
                        dump("probs_t0", probs[:, 0, :], sV, v_pm)
                        dump("Ktm_t0", Ktm[:, 0, :], sA, a_ktm)
                        dump("Vtm_t0", Vtm[:, 0, :], sA, a_v)

                trk["x_free"] = sV.n

                a_tanh_last = trk["pAB_free_pe"]
                a_v_last = trk["pCD_free"]
                v_prod = sV.n
                a_prod = sA.n
                prev_bk = 0
                for h in (range(H) if A_CFG["buckets"] else ()):
                    kb = pA if h % 2 == 0 else pC
                    vb = pB if h % 2 == 0 else pD

                    def _(t, h=h, kb=kb, vb=vb, v_prod=v_prod, a_prod=a_prod, prev_bk=prev_bk):
                        if h == 0:
                            t.wait_ge(sV.h, v_prod)
                            t.wait_ge(sA.h, a_prod)
                        if prev_bk:
                            t.wait_ge(sV.h, prev_bk)
                        pcol = (h // 8) * 1024 + ((h % 8) // 2) * 128 + (h % 2) * 512
                        for tt in range(HALF):
                            t.matmul(kb[0:64, 0:128], Ktm[:, tt, h * 64:h * 64 + 64],
                                     probs[:, tt, pcol:pcol + 128],
                                     start=(tt == 0), stop=(tt == HALF - 1), skip_group_check=True)
                        for tt in range(HALF):
                            ii = t.matmul(vb[0:128, 0:64], probs[:, tt, pcol:pcol + 128],
                                          Vtm[:, tt, h * 64:h * 64 + 64],
                                          start=(tt == 0), stop=(tt == HALF - 1), skip_group_check=True)
                        sP.inc(ii)
                    T(_)
                    p_bk = sP.n

                    def _(v, h=h, kb=kb, vb=vb, half=half, p_bk=p_bk, b=b):
                        v.wait_ge(sP.h, p_bk)
                        if half == 0:
                            sV.inc(v.tensor_copy(bKTa[b][:, h * 128:h * 128 + 128], kb[0:64, 0:128]))
                            sV.inc(v.tensor_copy(bVa[b][:, h * 64:h * 64 + 64], vb[0:128, 0:64]))
                        else:
                            sV.inc(v.tensor_tensor(bKTa[b][:, h * 128:h * 128 + 128], bKTa[b][:, h * 128:h * 128 + 128],
                                                   kb[0:64, 0:128], op=ALU.add))
                            sV.inc(v.tensor_tensor(bVa[b][:, h * 64:h * 64 + 64], bVa[b][:, h * 64:h * 64 + 64],
                                                   vb[0:128, 0:64], op=ALU.add))
                    V(_)
                    prev_bk = sV.n
                trk["pAB_free_pe"] = 0
                trk["pCD_free"] = 0
                trk["bk_evac"] = prev_bk
                trk["prod_free_pe"] = sP.n

                def _(t, prev_bk=prev_bk):
                    t.wait_ge(sV.h, prev_bk)
                    sP.inc(t.nop())
                T(_)

            trk[f"bk_evac{b}"] = trk["bk_evac"]
            dump(f"bKT_b{b}", bKTa[b][:], sV, trk["bk_evac"])
            dump(f"bV_b{b}", bVa[b][:], sV, trk["bk_evac"])

        def emit_B(b):
            def _(v, b=b):
                v.wait_ge(sV.h, trk[f"bk_evac{b}"])
                sV.inc(v.tensor_copy(bKTb[b][0:64, :], bKTa[b][:]))
                sV.inc(v.tensor_copy(bKTb[b][64:128, :], bKTa[b][:]))
                sV.inc(v.tensor_copy(bVb[b][:], bVa[b][:]))
            V(_)
            bk_ready = sV.n
            for tt in range(NT):
                par = tt % 2
                x2_free = trk.get(f"x2st_{par}", 0)
                xin_free = trk.get(f"xinfree_{par}", 0)

                def _(q, b=b, tt=tt, par=par, xin_free=xin_free):
                    if xin_free:
                        q.wait_ge(sV.h, xin_free)
                    dXp[par].inc(q.dma_start(out=xin[par][:], in_=x_tile_v[b, tt]), 16)
                Q(_)
                x_ld = dXp[par].n

                p_xn1T_prev = trk.get("p_xn1T_B", 0)

                def _(v, b=b, tt=tt, par=par, x_ld=x_ld, p_xn1T_prev=p_xn1T_prev):
                    v.wait_ge(dXp[par].h, x_ld)
                    if p_xn1T_prev:
                        v.wait_ge(sP.h, p_xn1T_prev)
                    ncol = b * 16 + tt
                    sV.inc(v.tensor_scalar(xn1[:], xin[par][:], negm1[:, ncol:ncol + 1],
                                           r1s[:, ncol:ncol + 1], op0=ALU.add, op1=ALU.mult))
                V(_)
                v_xn1 = sV.n

                ptrb_free = trk["ptrb_free"]

                def _(t, v_xn1=v_xn1, ptrb_free=ptrb_free):
                    t.wait_ge(sV.h, v_xn1)
                    if ptrb_free:
                        t.wait_ge(sA.h, ptrb_free)
                    for kc in range(8):
                        ii = t.transpose(ptrb[:, kc * 128:kc * 128 + 128], xn1[:, kc * 128:kc * 128 + 128], identb[:])
                    sP.inc(ii)
                T(_)
                trk["p_xn1T_B"] = sP.n
                p_xn1T = sP.n

                def _(s, p_xn1T=p_xn1T):
                    s.wait_ge(sP.h, p_xn1T)
                    sA.inc(s.copy(xn1f[:], ptrb[:]))
                A(_)
                a_xn1f = sA.n

                pab_free = trk.get("pAB_free_B", 0)

                def _(t, a_xn1f=a_xn1f, pab_free=pab_free):
                    t.wait_ge(sA.h, a_xn1f)
                    t.wait_ge(dWqo.h, 32)
                    if pab_free:
                        t.wait_ge(sV.h, pab_free)
                    for kc in range(8):
                        for oc in range(8):
                            bank = pA if oc < 4 else pB
                            ii = t.matmul(bank[:, (oc % 4) * 128:(oc % 4) * 128 + 128],
                                          wq_s[:, kc * 1024 + oc * 128:kc * 1024 + oc * 128 + 128],
                                          xn1f[:, kc * 128:kc * 128 + 128],
                                          start=(kc == 0 and oc % 4 == 0),
                                          stop=(kc == 7 and oc % 4 == 3),
                                          skip_group_check=True)
                    sP.inc(ii)
                T(_)
                p_qm = sP.n

                def _(s, p_qm=p_qm):
                    s.wait_ge(sP.h, p_qm)
                    sA.inc(s.copy(Qt[:, 0:512], pA[:]))
                    sA.inc(s.copy(Qt[:, 512:1024], pB[:]))
                A(_)
                a_q = sA.n

                for grp in range(2):
                    pE_free = trk["pE_free"]

                    def _(t, grp=grp, a_q=a_q, bk_ready=bk_ready, pE_free=pE_free, b=b):
                        t.wait_ge(sA.h, a_q)
                        t.wait_ge(sV.h, bk_ready)
                        if pE_free:
                            t.wait_ge(sV.h, pE_free)
                        for hl in range(8):
                            h = grp * 8 + hl
                            base = 64 * (h % 2)
                            oc = (hl // 2) * 128 + (hl % 2) * 512
                            ii = t.matmul(pE[:, oc:oc + 128],
                                          Qt[base:base + 64, (h // 2) * 128:(h // 2) * 128 + 128],
                                          bKTb[b][base:base + 64, h * 128:h * 128 + 128],
                                          start=(hl in (0, 1)), stop=(hl in (6, 7)),
                                          skip_group_check=True)
                        sP.inc(ii)
                    T(_)
                    p_sc = sP.n

                    def _(v, p_sc=p_sc):
                        v.wait_ge(sP.h, p_sc)
                        sV.inc(v.tensor_reduce(nmaxB[:], pE[:].rearrange("p (h s) -> p h s", s=128),
                                               axis=AX.X, op=ALU.max, negate=False))
                    V(_)
                    v_nm = sV.n

                    ptrf_free = trk["ptrf_free"]

                    def _(t, v_nm=v_nm, ptrf_free=ptrf_free):
                        t.wait_ge(sV.h, v_nm)
                        if ptrf_free:
                            t.wait_ge(sA.h, ptrf_free)
                        sP.inc(t.transpose(ptrf[0:8, 0:128], nmaxB[:], identf[:]))
                    T(_)
                    p_nt = sP.n

                    def _(s, p_nt=p_nt):
                        s.wait_ge(sP.h, p_nt)
                        sA.inc(s.copy(cTB[:], ptrf[0:8, 0:128]))
                    A(_)
                    a_ct = sA.n
                    trk["ptrf_free"] = a_ct

                    def _(t, a_ct=a_ct):
                        t.wait_ge(sA.h, a_ct)
                        for j in range(2):
                            ii = t.matmul(pE[:, j * 512:j * 512 + 512], cTB[:], negB[:, j * 512:j * 512 + 512],
                                          start=False, stop=True, skip_group_check=True)
                        sP.inc(ii)
                    T(_)
                    p_bm = sP.n

                    def _(s, p_bm=p_bm):
                        s.wait_ge(sP.h, p_bm)
                        sA.inc(s.activation(expB[:], pE[:], AF.Exp, scale=0.125))
                    A(_)
                    a_exp = sA.n

                    def _(v, a_exp=a_exp):
                        v.wait_ge(sA.h, a_exp)
                        sV.inc(v.tensor_reduce(ZB[:], expB[:].rearrange("p (h s) -> p h s", s=128),
                                               axis=AX.X, op=ALU.add))
                        v.wait_ge(sV.h, sV.n)
                        sV.inc(v.reciprocal(RBs[:], ZB[:]))
                        v.wait_ge(sV.h, sV.n)
                        sV.inc(v.tensor_tensor(attn_t[:].rearrange("p (h s) -> p h s", s=128),
                                               expB[:].rearrange("p (h s) -> p h s", s=128),
                                               RBs[:].unsqueeze(2).broadcast_to([P, 8, 128]), op=ALU.mult))
                    V(_)
                    v_attn = sV.n
                    trk["pE_free"] = v_attn

                    atb_free = trk.get("ptrb_free_B", a_xn1f)

                    def _(t, v_attn=v_attn, atb_free=atb_free):
                        t.wait_ge(sV.h, v_attn)
                        t.wait_ge(sA.h, atb_free)
                        for hl in range(8):
                            ii = t.transpose(ptrb[:, hl * 128:hl * 128 + 128], attn_t[:, hl * 128:hl * 128 + 128], identb[:])
                        sP.inc(ii)
                    T(_)
                    p_at = sP.n

                    def _(s, p_at=p_at):
                        s.wait_ge(sP.h, p_at)
                        sA.inc(s.copy(expT[:], ptrb[:]))
                    A(_)
                    a_expT = sA.n
                    trk["ptrb_free_B"] = a_expT
                    trk["ptrb_free"] = a_expT

                    pc_free = trk.get("pC_free_B", 0)

                    def _(t, grp=grp, a_expT=a_expT, pc_free=pc_free, b=b):
                        t.wait_ge(sA.h, a_expT)
                        if pc_free:
                            t.wait_ge(sA.h, pc_free)
                        for hl in range(8):
                            h = grp * 8 + hl
                            base = 64 * (h % 2)
                            cs = (h // 2) % 4
                            ec = (hl // 2) * 128 + (hl % 2) * 512
                            ii = t.matmul(pC[base:base + 64, cs * 128:cs * 128 + 128],
                                          bVb[b][:, h * 64:h * 64 + 64],
                                          expT[:, ec:ec + 128],
                                          start=(hl in (0, 1)), stop=(hl in (6, 7)), skip_group_check=True)
                        sP.inc(ii)
                    T(_)
                    p_om = sP.n

                    def _(s, grp=grp, p_om=p_om):
                        s.wait_ge(sP.h, p_om)
                        sA.inc(s.copy(aout[:, grp * 512:grp * 512 + 512], pC[:]))
                    A(_)
                    trk["pC_free_B"] = sA.n
                a_out2 = sA.n

                def _(t, a_out2=a_out2, a_q=a_q):
                    t.wait_ge(sA.h, a_out2)
                    for dh in range(2):
                        bank = pA if dh == 0 else pB
                        for kc in range(8):
                            t.matmul(bank[:, 0:512], aout[:, kc * 128:kc * 128 + 128],
                                     wo_s[:, kc * 1024 + dh * 512:kc * 1024 + dh * 512 + 512],
                                     start=(kc == 0), stop=False, skip_group_check=True)
                        ii = t.matmul(bank[:, 0:512], onesb[:], bo_s[0:1, dh * 512:dh * 512 + 512],
                                      start=False, stop=True, skip_group_check=True)
                    sP.inc(ii)
                T(_)
                p_wo = sP.n

                def _(v, p_wo=p_wo):
                    v.wait_ge(sP.h, p_wo)
                    sV.inc(v.bn_stats(bnb2[:, 0:6], pA[:]))
                    sV.inc(v.bn_stats(bnb2[:, 6:12], pB[:]))
                    v.wait_ge(sV.h, sV.n)
                    sV.inc(v.bn_aggr(mv2[:], bnb2[:]))
                    v.wait_ge(sV.h, sV.n)
                    sV.inc(v.tensor_scalar(sc2[:, 0:1], mv2[:, 1:2], EPS, None, op0=ALU.add))
                    v.wait_ge(sV.h, sV.n)
                    sV.inc(v.reciprocal(sc2[:, 1:2], sc2[:, 0:1]))
                V(_)
                v_u2 = sV.n

                def _(s, v_u2=v_u2):
                    s.wait_ge(sV.h, v_u2)
                    sA.inc(s.activation(sc2[:, 2:3], sc2[:, 1:2], AF.Ln))
                    s.wait_ge(sA.h, sA.n)
                    sA.inc(s.activation(sc2[:, 3:4], sc2[:, 2:3], AF.Exp, scale=0.5))
                A(_)
                a_r2 = sA.n

                x2_free = trk.get(f"x2free_{par}", 0)

                def _(v, par=par, a_r2=a_r2, x2_free=x2_free):
                    v.wait_ge(sA.h, a_r2)
                    if x2_free:
                        v.wait_ge(dSp[par].h, x2_free)
                    sV.inc(v.tensor_scalar(sc2[:, 0:1], mv2[:, 0:1], sc2[:, 3:4], -1.0, op0=ALU.mult, op1=ALU.mult))
                    v.wait_ge(sV.h, sV.n)
                    sV.inc(v.tensor_scalar(ln2t[:, 0:512], pA[:], sc2[:, 3:4], sc2[:, 0:1], op0=ALU.mult, op1=ALU.add))
                    sV.inc(v.tensor_scalar(ln2t[:, 512:1024], pB[:], sc2[:, 3:4], sc2[:, 0:1], op0=ALU.mult, op1=ALU.add))
                    v.wait_ge(sV.h, sV.n)
                    sV.inc(v.tensor_tensor(x2t[par][:], xin[par][:], ln2t[:], op=ALU.add))
                V(_)
                v_x2 = sV.n
                trk["pAB_free_B"] = v_x2
                trk[f"xinfree_{par}"] = v_x2

                def _(q, b=b, tt=tt, par=par, v_x2=v_x2):
                    q.wait_ge(sV.h, v_x2)
                    dSp[par].inc(q.dma_start(out=x2_tile_v[b, tt], in_=x2t[par][:]), 16)
                Q(_)
                trk[f"x2free_{par}"] = dSp[par].n

                if b == 0 and tt == 0:
                    dump("Qt_t0", Qt[:], sA, a_q)
                    dump("expT_t0", expT[:], sA, trk["ptrb_free_B"])
                    dump("aout_t0", aout[:], sA, a_out2)
                    dump("x2_t0", x2t[0][:], sV, v_x2)

        def barrier():
            vals = [(sP, sP.n), (sV, sV.n), (sA, sA.n), (sG, sG.n),
                    (dXA, dXA.n), (dXp[0], dXp[0].n), (dXp[1], dXp[1].n),
                    (dCon, dCon.n), (dPro, dPro.n), (dWkv, dWkv.n), (dWqo, dWqo.n),
                    (dSp[0], dSp[0].n), (dSp[1], dSp[1].n)]

            def _(e):
                for s, vv in vals:
                    if vv:
                        e.wait_ge(s.h, vv)
            for body in (T, V, A, G, Q):
                body(_)

        if "A" in phases:
            for _b in A_CFG["bs"]:
                emit_A(_b)
        barrier()
        esA.close()

        esB = ExitStack()
        Qt = esB.enter_context(nc.sbuf_tensor("Qt", [P, D], BF16))
        nmaxB = esB.enter_context(nc.sbuf_tensor("nmaxB", [P, 8], F32))
        cTB = esB.enter_context(nc.sbuf_tensor("cTB", [8, 128], BF16))
        expB = esB.enter_context(nc.sbuf_tensor("expB", [P, 1024], F32))
        ZB = esB.enter_context(nc.sbuf_tensor("ZB", [P, 8], F32))
        RBs = esB.enter_context(nc.sbuf_tensor("RBs", [P, 8], F32))
        attn_t = esB.enter_context(nc.sbuf_tensor("attn_t", [P, 1024], BF16))
        expT = esB.enter_context(nc.sbuf_tensor("expT", [P, 1024], BF16))
        aout = esB.enter_context(nc.sbuf_tensor("aout", [P, D], BF16))
        ln2t = esB.enter_context(nc.sbuf_tensor("ln2t", [P, D], F32))
        bnb2 = esB.enter_context(nc.sbuf_tensor("bnb2", [P, 12], F32))
        mv2 = esB.enter_context(nc.sbuf_tensor("mv2", [P, 2], F32))
        sc2 = esB.enter_context(nc.sbuf_tensor("sc2", [P, 4], F32))
        bKTb = [esB.enter_context(nc.sbuf_tensor(f"bKTb{b}", [P, 2048], BF16)) for b in range(2)]
        bVb = [esB.enter_context(nc.sbuf_tensor(f"bVb{b}", [P, 1024], BF16)) for b in range(2)]
        xin = [esB.enter_context(nc.sbuf_tensor(f"xin{i}", [P, D], F32)) for i in range(2)]
        x2t = [esB.enter_context(nc.sbuf_tensor(f"x2t{i}", [P, D], F32)) for i in range(2)]

        if "B" in phases:
            emit_B(0)
            emit_B(1)
        barrier()
        esB.close()
        esAB.close()

        if "F" in phases:
            dFc = Sm(es, nc, "dFc")
            esF = ExitStack()
            muu = esF.enter_context(nc.sbuf_tensor("muu", [P, 2048], BF16))
            gam_s = esF.enter_context(nc.sbuf_tensor("gam_s", [P, D], F32))
            bet_s = esF.enter_context(nc.sbuf_tensor("bet_s", [P, D], F32))
            scl_s = esF.enter_context(nc.sbuf_tensor("scl_s", [P, 1], F32))
            b1f_s = esF.enter_context(nc.sbuf_tensor("b1f_s", [P, 32], F32))
            b2r_s = esF.enter_context(nc.sbuf_tensor("b2r_s", [P, 2, D], BF16))
            esFpre = ExitStack()
            mu_sb = esFpre.enter_context(nc.sbuf_tensor("mu_sb", [P, 2048], F32))
            musq = esFpre.enter_context(nc.sbuf_tensor("musq", [P, 2048], BF16))
            ssmu = esFpre.enter_context(nc.sbuf_tensor("ssmu", [1, 1024], F32))
            rmub = esFpre.enter_context(nc.sbuf_tensor("rmub", [P, 256], F32))
            def _(g):
                g.dma_start(out=mu_sb[:], in_=mu_d).then_inc(dFc.h, 16)
                g.dma_start(out=gam_s[:], in_=gam_d).then_inc(dFc.h, 16)
                g.dma_start(out=bet_s[:], in_=bet_d).then_inc(dFc.h, 16)
                g.dma_start(out=scl_s[:], in_=scl_d).then_inc(dFc.h, 16)
                g.dma_start(out=b1f_s[:], in_=b1_d).then_inc(dFc.h, 16)
                g.dma_start(out=b2r_s[:], in_=b2_d.rearrange("(c p) d -> p c d", p=P)).then_inc(dFc.h, 16)
                dFc.n += 96
            G(_)
            w_fc = dFc.n

            def _(s):
                s.wait_ge(dFc.h, w_fc)
                sA.inc(s.activation(musq[:], mu_sb[:], AF.Square))
            A(_)
            a_musq = sA.n

            def _(t):
                t.wait_ge(sA.h, a_musq)
                for kc in range(8):
                    ii = t.matmul(pD[0:1, 0:256], onescolb[:], musq[:, kc * 256:kc * 256 + 256],
                                  start=(kc == 0), stop=(kc == 7))
                sP.inc(ii)
            T(_)
            p_mu = sP.n

            def _(s):
                s.wait_ge(sP.h, p_mu)
                sA.inc(s.copy(ssmu[0:1, 0:256], pD[0:1, 0:256]))
                s.wait_ge(sA.h, sA.n)
                sA.inc(s.activation(ssmu[0:1, 256:512], ssmu[0:1, 0:256], AF.Ln))
                s.wait_ge(sA.h, sA.n)
                sA.inc(s.activation(ssmu[0:1, 512:768], ssmu[0:1, 256:512], AF.Exp, scale=0.5))
            A(_)
            a_nmu = sA.n

            def _(v):
                v.wait_ge(sA.h, a_nmu)
                sV.inc(v.tensor_scalar(ssmu[0:1, 768:1024], ssmu[0:1, 512:768], 1e-12, None, op0=ALU.max))
                v.wait_ge(sV.h, sV.n)
                sV.inc(v.reciprocal(ssmu[0:1, 0:256], ssmu[0:1, 768:1024]))
            V(_)
            v_rmu = sV.n

            def _(t):
                t.wait_ge(sV.h, v_rmu)
                sP.inc(t.matmul(pD[0:128, 0:256], ones1f[:], ssmu[0:1, 0:256], start=True, stop=True))
            T(_)
            p_rmub = sP.n

            def _(s):
                s.wait_ge(sP.h, p_rmub)
                sA.inc(s.copy(rmub[:], pD[0:128, 0:256]))
            A(_)
            a_rmub = sA.n

            def _(v):
                v.wait_ge(sA.h, a_rmub)
                for kc in range(8):
                    sV.inc(v.tensor_tensor(muu[:, kc * 256:kc * 256 + 256], mu_sb[:, kc * 256:kc * 256 + 256],
                                           rmub[:], op=ALU.mult))
            V(_)
            v_muu = sV.n

            esFpre.close()
            xnn_sb = esF.enter_context(nc.sbuf_tensor("xnn_sb", [P, NT, D], BF16))
            lgsm = esF.enter_context(nc.sbuf_tensor("lgsm", [P, 2, 2048], BF16))
            comb = esF.enter_context(nc.sbuf_tensor("comb", [P, 2, NT, 256], BF16))
            si_sb = esF.enter_context(nc.sbuf_tensor("si_sb", [P, 2, 8, 256], BF16))
            so_stg = esF.enter_context(nc.sbuf_tensor("so_stg", [P, 2, 2, D], BF16))
            so_bf = esF.enter_context(nc.sbuf_tensor("so_bf", [P, 2, 2, D], BF16))
            w1buf = [esF.enter_context(nc.sbuf_tensor(f"w1b{i}", [P, 2048], BF16)) for i in range(2)]
            w2buf = [esF.enter_context(nc.sbuf_tensor(f"w2b{i}", [P, 2048], BF16)) for i in range(2)]
            so_st = [esF.enter_context(nc.sbuf_tensor(f"sost{i}", [32, 1024], F32)) for i in range(2)]
            h_bf = esF.enter_context(nc.sbuf_tensor("h_bf", [P, 64], BF16))
            x2f = [esF.enter_context(nc.sbuf_tensor(f"x2f{i}", [P, D], F32)) for i in range(2)]
            t1f = esF.enter_context(nc.sbuf_tensor("t1f", [P, D], F32))
            xnn_f = esF.enter_context(nc.sbuf_tensor("xnn_f", [P, D], F32))
            xu_t = esF.enter_context(nc.sbuf_tensor("xu_t", [P, D], BF16))
            xuf = esF.enter_context(nc.sbuf_tensor("xuf", [P, D], BF16))
            lg_t = esF.enter_context(nc.sbuf_tensor("lg_t", [P, 256], F32))
            expc = esF.enter_context(nc.sbuf_tensor("expc", [P, 256], F32))
            bnF = esF.enter_context(nc.sbuf_tensor("bnF", [P, 12], F32))
            mvF = esF.enter_context(nc.sbuf_tensor("mvF", [P, 2], F32))
            scF = esF.enter_context(nc.sbuf_tensor("scF", [P, 16], F32))
            dspb = esF.enter_context(nc.sbuf_tensor("dspb", [P, 2048], BF16))
            esFd = ExitStack()
            dtm = esFd.enter_context(nc.sbuf_tensor("dtm", [P, NT, 256], BF16))

            ftrk = {"ptrb": 0, "ptrf": 0, "pA": 0, "si": {}, "comb": {}}

            def emit_F1(b):
                for tt in range(NT):
                    par = tt % 2
                    xfree = ftrk.get(f"x2f{par}", 0)

                    def _(q, b=b, tt=tt, par=par, xfree=xfree):
                        if xfree:
                            q.wait_ge(sV.h, xfree)
                        if b == 0 and tt < 2:
                            q.wait_ge(sV.h, v_muu)
                        dXp[par].inc(q.dma_start(out=x2f[par][:], in_=x2_tile_v[b, tt]), 16)
                    Q(_)
                    x_ld = dXp[par].n

                    def _(v, par=par, x_ld=x_ld):
                        v.wait_ge(dXp[par].h, x_ld)
                        sV.inc(v.bn_stats(bnF[:, 0:6], x2f[par][:, 0:512]))
                        sV.inc(v.bn_stats(bnF[:, 6:12], x2f[par][:, 512:1024]))
                        v.wait_ge(sV.h, sV.n)
                        sV.inc(v.bn_aggr(mvF[:], bnF[:]))
                        v.wait_ge(sV.h, sV.n)
                        sV.inc(v.tensor_scalar(scF[:, 0:1], mvF[:, 1:2], EPS, None, op0=ALU.add))
                    V(_)
                    v_u3 = sV.n

                    def _(s, v_u3=v_u3):
                        s.wait_ge(sV.h, v_u3)
                        sA.inc(s.activation(scF[:, 1:2], scF[:, 0:1], AF.Ln))
                        s.wait_ge(sA.h, sA.n)
                        sA.inc(s.activation(scF[:, 2:3], scF[:, 1:2], AF.Exp, scale=-0.5))
                    A(_)
                    a_r3 = sA.n

                    def _(v, a_r3=a_r3):
                        v.wait_ge(sA.h, a_r3)
                        sV.inc(v.tensor_scalar(scF[:, 3:4], mvF[:, 1:2], scF[:, 2:3], scF[:, 2:3], op0=ALU.mult, op1=ALU.mult))
                        v.wait_ge(sV.h, sV.n)
                        sV.inc(v.tensor_scalar(scF[:, 4:5], scF[:, 3:4], EPS, None, op0=ALU.add))
                    V(_)
                    v_v4 = sV.n

                    def _(s, v_v4=v_v4):
                        s.wait_ge(sV.h, v_v4)
                        sA.inc(s.activation(scF[:, 5:6], scF[:, 4:5], AF.Ln))
                        s.wait_ge(sA.h, sA.n)
                        sA.inc(s.activation(scF[:, 6:7], scF[:, 5:6], AF.Exp, scale=-0.5))
                    A(_)
                    a_r4 = sA.n

                    p_xuT_prev = ftrk.get("p_xuT", 0)
                    g_xnn_prev = ftrk.get("g_xnn", 0)

                    def _(v, par=par, a_r4=a_r4, p_xuT_prev=p_xuT_prev, g_xnn_prev=g_xnn_prev):
                        v.wait_ge(sA.h, a_r4)
                        if p_xuT_prev:
                            v.wait_ge(sP.h, p_xuT_prev)
                        if g_xnn_prev:
                            v.wait_ge(sG.h, g_xnn_prev)
                        sV.inc(v.tensor_tensor(scF[:, 7:8], scF[:, 2:3], scF[:, 6:7], op=ALU.mult))
                        v.wait_ge(sV.h, sV.n)
                        sV.inc(v.tensor_scalar(scF[:, 8:9], mvF[:, 0:1], scF[:, 7:8], -1.0, op0=ALU.mult, op1=ALU.mult))
                        v.wait_ge(sV.h, sV.n)
                        sV.inc(v.tensor_scalar(t1f[:], x2f[par][:], scF[:, 7:8], scF[:, 8:9], op0=ALU.mult, op1=ALU.add))
                        v.wait_ge(sV.h, sV.n)
                        sV.inc(v.tensor_tensor(t1f[:], t1f[:], gam_s[:], op=ALU.mult))
                        v.wait_ge(sV.h, sV.n)
                        sV.inc(v.tensor_tensor(xnn_f[:], t1f[:], bet_s[:], op=ALU.add))
                        v.wait_ge(sV.h, sV.n)
                        sV.inc(v.bn_stats(bnF[:, 0:6], xnn_f[:, 0:512]))
                        sV.inc(v.bn_stats(bnF[:, 6:12], xnn_f[:, 512:1024]))
                        v.wait_ge(sV.h, sV.n)
                        sV.inc(v.bn_aggr(mvF[:], bnF[:]))
                        v.wait_ge(sV.h, sV.n)
                        sV.inc(v.tensor_tensor(scF[:, 9:10], mvF[:, 0:1], mvF[:, 0:1], op=ALU.mult))
                        v.wait_ge(sV.h, sV.n)
                        sV.inc(v.tensor_tensor(scF[:, 10:11], scF[:, 9:10], mvF[:, 1:2], op=ALU.add))
                    V(_)
                    v_ss = sV.n

                    def _(s, v_ss=v_ss):
                        s.wait_ge(sV.h, v_ss)
                        sA.inc(s.activation(scF[:, 11:12], scF[:, 10:11], AF.Ln, scale=1024.0))
                        s.wait_ge(sA.h, sA.n)
                        sA.inc(s.activation(scF[:, 12:13], scF[:, 11:12], AF.Exp, scale=0.5))
                    A(_)
                    a_n5 = sA.n

                    def _(v, tt=tt, a_n5=a_n5):
                        v.wait_ge(sA.h, a_n5)
                        sV.inc(v.tensor_scalar(scF[:, 13:14], scF[:, 12:13], 1e-12, None, op0=ALU.max))
                        v.wait_ge(sV.h, sV.n)
                        sV.inc(v.reciprocal(scF[:, 14:15], scF[:, 13:14]))
                        v.wait_ge(sV.h, sV.n)
                        sV.inc(v.tensor_tensor(scF[:, 15:16], scF[:, 14:15], scl_s[:], op=ALU.mult))
                        v.wait_ge(sV.h, sV.n)
                        sV.inc(v.tensor_scalar(xu_t[:], xnn_f[:], scF[:, 15:16], None, op0=ALU.mult))
                    V(_)
                    v_xu = sV.n

                    si_mm_done = ftrk.get("si_mm_done", 0)

                    def _(g, tt=tt, v_ss=v_ss, si_mm_done=si_mm_done):
                        g.wait_ge(sV.h, v_ss)
                        if si_mm_done:
                            g.wait_ge(sP.h, si_mm_done)
                        sG.inc(g.tensor_copy(xnn_sb[:, tt, :], xnn_f[:]))
                    G(_)
                    g_xnn = sG.n
                    ftrk["g_xnn"] = g_xnn
                    if dbg_names and b == 1 and tt in (0, 7, 13, 14, 15):
                        dump(f"w14_t{tt}", xnn_sb[:, 14, :], sG, g_xnn)
                    if dbg_names and b == 1 and tt == 14:
                        dump("xnnf14", xnn_f[:], sV, v_ss)
                        dump("x2f14", x2f[par][:], sV, v_ss)
                        dump("w14_vss", xnn_sb[:, 14, :], sV, v_ss)
                        dump("w14_an5", xnn_sb[:, 14, :], sA, a_n5)
                        dump("w14_vxu", xnn_sb[:, 14, :], sV, v_xu)
                        dump("w14_xld", xnn_sb[:, 14, :], dXp[par], x_ld)

                    ptrb_free = ftrk["ptrb"]

                    def _(t, v_xu=v_xu, ptrb_free=ptrb_free):
                        t.wait_ge(sV.h, v_xu)
                        if ptrb_free:
                            t.wait_ge(sA.h, ptrb_free)
                        for kc in range(8):
                            ii = t.transpose(ptrb[:, kc * 128:kc * 128 + 128], xu_t[:, kc * 128:kc * 128 + 128], identb[:])
                        sP.inc(ii)
                    T(_)
                    ftrk["p_xuT"] = sP.n
                    p_xuT = sP.n

                    def _(s, p_xuT=p_xuT):
                        s.wait_ge(sP.h, p_xuT)
                        sA.inc(s.copy(xuf[:], ptrb[:]))
                    A(_)
                    a_xuf = sA.n
                    ftrk["ptrb"] = a_xuf

                    pa_free = ftrk["pA"]

                    def _(t, a_xuf=a_xuf, pa_free=pa_free, v_muu=v_muu):
                        t.wait_ge(sA.h, a_xuf)
                        t.wait_ge(sV.h, v_muu)
                        if pa_free:
                            t.wait_ge(sA.h, pa_free)
                        for kc in range(8):
                            ii = t.matmul(pA[:, 0:256], xuf[:, kc * 128:kc * 128 + 128],
                                          muu[:, kc * 256:kc * 256 + 256],
                                          start=(kc == 0), stop=(kc == 7))
                        sP.inc(ii)
                    T(_)
                    p_lg2 = sP.n

                    def _(s, p_lg2=p_lg2):
                        s.wait_ge(sP.h, p_lg2)
                        sA.inc(s.copy(lg_t[:], pA[:, 0:256]))
                    A(_)
                    a_lg = sA.n
                    ftrk["pA"] = a_lg

                    def _(v, a_lg=a_lg):
                        v.wait_ge(sA.h, a_lg)
                        sV.inc(v.tensor_reduce(scF[:, 0:1], lg_t[:], axis=AX.X, op=ALU.max, negate=True))
                    V(_)
                    v_nmc = sV.n

                    def _(s, v_nmc=v_nmc):
                        s.wait_ge(sV.h, v_nmc)
                        sA.inc(s.activation(expc[:], lg_t[:], AF.Exp, bias=scF[:, 0:1], accum_out=scF[:, 1:2]))
                    A(_)
                    a_expc = sA.n

                    def _(v, b=b, tt=tt, a_expc=a_expc):
                        v.wait_ge(sA.h, a_expc)
                        sV.inc(v.reciprocal(scF[:, 2:3], scF[:, 1:2]))
                        v.wait_ge(sV.h, sV.n)
                        sV.inc(v.tensor_scalar(comb[:, b, tt, :], expc[:], scF[:, 2:3], None, op0=ALU.mult))
                    V(_)
                    ftrk["comb"][b] = sV.n

                    ptrf_free = ftrk["ptrf"]

                    def _(t, a_lg=a_lg, ptrf_free=ptrf_free):
                        t.wait_ge(sA.h, a_lg)
                        if ptrf_free:
                            t.wait_ge(sA.h, ptrf_free)
                        t.transpose(ptrf[:, 0:128], lg_t[:, 0:128], identf[:])
                        sP.inc(t.transpose(ptrf[:, 128:256], lg_t[:, 128:256], identf[:]))
                    T(_)
                    p_lgT = sP.n

                    disp_done = ftrk.get("disp_done_v", 0)

                    def _(s, b=b, tt=tt, p_lgT=p_lgT, disp_done=disp_done):
                        s.wait_ge(sP.h, p_lgT)
                        if disp_done:
                            s.wait_ge(sV.h, disp_done)
                        sA.inc(s.copy(lgsm[:, :, tt * 128:tt * 128 + 128],
                                      ptrf[:, 0:256].rearrange("p (c t) -> p c t", t=128)))
                    A(_)
                    a_lgsm = sA.n
                    ftrk["ptrf"] = a_lgsm
                    ftrk[f"x2f{par}"] = v_ss
                    if tt == 0 and dbg_names:
                        dump(f"xnnf_b{b}", xnn_f[:], sV, v_ss)
                        dump(f"lg_b{b}", lg_t[:], sA, a_lg)
                        dump(f"comb0_b{b}", comb[:, b, 0, :], sV, ftrk["comb"][b])

                ftrk[f"lgsm_done"] = sA.n
                ftrk[f"xnn_done"] = sG.n
                if dbg_names:
                    dump(f"xnn14_a_b{b}", xnn_sb[:, 14, :], sG, sG.n)

                for c in range(2):
                    def _(v, c=c):
                        v.wait_ge(sA.h, ftrk["lgsm_done"])
                        sV.inc(v.tensor_reduce(scF[:, 0:1], lgsm[:, c, :], axis=AX.X, op=ALU.max, negate=True))
                    V(_)
                    v_nmd = sV.n

                    dspb_free = ftrk.get("dspb_free", 0)

                    def _(s, c=c, v_nmd=v_nmd, dspb_free=dspb_free):
                        s.wait_ge(sV.h, v_nmd)
                        if dspb_free:
                            s.wait_ge(sP.h, dspb_free)
                        sA.inc(s.activation(dspb[:], lgsm[:, c, :], AF.Exp, bias=scF[:, 0:1], accum_out=scF[:, 1:2]))
                        s.wait_ge(sA.h, sA.n)
                        sA.inc(s.activation(scF[:, 2:3], scF[:, 1:2], AF.Ln))
                    A(_)
                    a_lnz = sA.n

                    def _(v, a_lnz=a_lnz):
                        v.wait_ge(sA.h, a_lnz)
                        sV.inc(v.tensor_scalar(scF[:, 3:4], scF[:, 2:3], -1.0, scF[:, 0:1], op0=ALU.mult, op1=ALU.add))
                    V(_)
                    v_b2x = sV.n

                    def _(s, c=c, v_b2x=v_b2x):
                        s.wait_ge(sV.h, v_b2x)
                        sA.inc(s.activation(dspb[:], lgsm[:, c, :], AF.Exp, bias=scF[:, 3:4]))
                    A(_)
                    v_dsp = sV.n
                    a_dsp = sA.n

                    for rnd in range(2):
                        ptrb_free = ftrk["ptrb"]

                        def _(t, c=c, rnd=rnd, a_dsp=a_dsp, ptrb_free=ptrb_free):
                            t.wait_ge(sA.h, a_dsp)
                            if ptrb_free:
                                t.wait_ge(sA.h, ptrb_free)
                            for j in range(8):
                                tt = rnd * 8 + j
                                ii = t.transpose(ptrb[:, j * 128:j * 128 + 128],
                                                 dspb[:, tt * 128:tt * 128 + 128], identb[:])
                            sP.inc(ii)
                        T(_)
                        p_dT = sP.n
                        ftrk["dspb_free"] = p_dT
                        si_mm_done = ftrk.get("si_mm_done", 0)

                        def _(s, c=c, rnd=rnd, p_dT=p_dT, si_mm_done=si_mm_done):
                            s.wait_ge(sP.h, p_dT)
                            if si_mm_done:
                                s.wait_ge(sP.h, si_mm_done)
                            sA.inc(s.copy(dtm[:, rnd * 8:rnd * 8 + 8, c * 128:c * 128 + 128],
                                          ptrb[:].rearrange("p (j t) -> p j t", t=128)))
                        A(_)
                        ftrk["ptrb"] = sA.n
                v_dtm = sA.n
                ftrk["disp_done_v"] = sV.n

                prev_si = 0
                for dc in range(8):
                    bank = pB if dc % 2 == 0 else pC

                    def _(t, b=b, dc=dc, bank=bank, v_dtm=v_dtm, prev_si=prev_si):
                        t.wait_ge(sA.h, v_dtm)
                        t.wait_ge(sG.h, ftrk["xnn_done"])
                        if prev_si:
                            t.wait_ge(sA.h, prev_si)
                        for tt in range(NT):
                            ii = t.matmul(bank[:, 0:256], xnn_sb[:, tt, dc * 128:dc * 128 + 128],
                                          dtm[:, tt, :], start=(tt == 0), stop=(tt == NT - 1))
                        sP.inc(ii)
                    T(_)
                    p_si = sP.n

                    def _(s, b=b, dc=dc, bank=bank, p_si=p_si):
                        s.wait_ge(sP.h, p_si)
                        sA.inc(s.copy(si_sb[:, b, dc, :], bank[:, 0:256]))
                    A(_)
                    prev_si = sA.n if dc >= 1 else 0
                    if dc >= 1:
                        prev_si = ftrk["si"].get(dc - 1, 0)
                    ftrk["si"][dc] = sA.n
                ftrk[f"si_done{b}"] = sA.n
                ftrk["si_mm_done"] = sP.n
                if dbg_names:
                    dump(f"xnn14_b_b{b}", xnn_sb[:, 14, :], sP, sP.n)
                if dbg_names:
                    dump(f"si_b{b}", si_sb[:, b], sA, sA.n)
                    dump(f"dtm_b{b}", dtm[:, 0, :], sA, sA.n)
                    dump(f"dtm7_b{b}", dtm[:, 7, :], sA, sA.n)
                    dump(f"dtm8_b{b}", dtm[:, 8, :], sA, sA.n)
                    dump(f"dtm15_b{b}", dtm[:, 15, :], sA, sA.n)

            def emit_experts():
                si_done = max(ftrk["si_done0"], ftrk["si_done1"])
                for e in range(E):
                    par = e % 2
                    wfree = ftrk.get(f"wbuf{par}", 0)

                    def _(g, e=e, par=par, wfree=wfree):
                        if wfree:
                            g.wait_ge(sP.h, wfree)
                        if e < 2:
                            g.wait_ge(sV.h, v_muu)
                        g.dma_start(out=w1buf[par][:], in_=w1_d[e]).then_inc(dE[par].h, 16)
                        g.dma_start(out=w2buf[par][:], in_=w2_d[e]).then_inc(dE[par].h, 16)
                        dE[par].n += 32
                    G(_)
                    w_e = dE[par].n

                    pd_free = ftrk.get("pD_exp", 0)

                    def _(t, e=e, par=par, w_e=w_e, si_done=si_done, pd_free=pd_free):
                        t.wait_ge(dE[par].h, w_e)
                        t.wait_ge(sA.h, si_done)
                        if pd_free:
                            t.wait_ge(sA.h, pd_free)
                        for dc in range(8):
                            for hc in range(2):
                                ii = t.matmul(pD[:, hc * 32:hc * 32 + 32],
                                              w1buf[par][:, dc * 256 + hc * 128:dc * 256 + hc * 128 + 128],
                                              si_sb[:, :, dc, e * 16:e * 16 + 16],
                                              start=(dc == 0 and hc == 0), stop=(dc == 7 and hc == 1),
                                              skip_group_check=True)
                        sP.inc(ii)
                    T(_)
                    p_h = sP.n

                    def _(s, e=e, p_h=p_h):
                        s.wait_ge(sP.h, p_h)
                        sA.inc(s.activation(h_bf[:, 0:32], pD[:, 0:32], AF.Gelu, bias=b1f_s[:, 2 * e:2 * e + 1]))
                        sA.inc(s.activation(h_bf[:, 32:64], pD[:, 32:64], AF.Gelu, bias=b1f_s[:, 2 * e + 1:2 * e + 2]))
                    A(_)
                    a_h = sA.n
                    ftrk["pD_exp"] = a_h

                    pe_free = ftrk.get("pE_exp", 0)

                    def _(t, e=e, par=par, a_h=a_h, pe_free=pe_free):
                        t.wait_ge(sA.h, a_h)
                        if pe_free:
                            t.wait_ge(sA.h, pe_free)
                        for dh in range(2):
                            for hc in range(2):
                                ii = t.matmul(pE[0:32, dh * 512:dh * 512 + 512],
                                              h_bf[:, hc * 32:hc * 32 + 32],
                                              w2buf[par][:, hc * 1024 + dh * 512:hc * 1024 + dh * 512 + 512],
                                              start=(hc == 0), stop=(hc == 1), skip_group_check=True)
                        sP.inc(ii)
                    T(_)
                    p_so = sP.n
                    ftrk[f"wbuf{par}"] = p_so

                    spar = e % 2
                    stfree = ftrk.get(f"sost{spar}", 0)

                    def _(s, e=e, spar=spar, p_so=p_so, stfree=stfree):
                        s.wait_ge(sP.h, p_so)
                        if stfree:
                            s.wait_ge(dSh[spar].h, stfree)
                        sA.inc(s.copy(so_st[spar][0:32, 0:512], pE[0:32, 0:512]))
                        sA.inc(s.copy(so_st[spar][0:32, 512:1024], pE[0:32, 512:1024]))
                    A(_)
                    a_so = sA.n
                    ftrk["pE_exp"] = a_so

                    def _(g, e=e, spar=spar, a_so=a_so):
                        g.wait_ge(sA.h, a_so)
                        r0 = (e % 8) * 16
                        g.dma_start(out=so_stg[r0:r0 + 16, 0, e // 8, :], in_=so_st[spar][0:16, :]).then_inc(dSh[spar].h, 16)
                        g.dma_start(out=so_stg[r0:r0 + 16, 1, e // 8, :], in_=so_st[spar][16:32, :]).then_inc(dSh[spar].h, 16)
                        dSh[spar].n += 32
                    G(_)
                    ftrk[f"sost{spar}"] = dSh[spar].n

                def _(v):
                    v.wait_ge(dSh[0].h, dSh[0].n)
                    v.wait_ge(dSh[1].h, dSh[1].n)
                    for bb in range(2):
                        for c in range(2):
                            sV.inc(v.tensor_tensor(so_bf[:, bb, c, :], so_stg[:, bb, c, :],
                                                   b2r_s[:, c, :], op=ALU.add))
                V(_)
                ftrk["so_ready"] = sV.n
                if dbg_names:
                    dump("sobf_b0", so_bf[:, 0], sV, sV.n)
                    dump("sobf_b1", so_bf[:, 1], sV, sV.n)

            def emit_F3(b):
                for tt in range(NT):
                    par = tt % 2
                    xfree = ftrk.get(f"x2r{par}", 0) or ftrk.get(f"x2f{par}", 0)

                    def _(q, b=b, tt=tt, par=par, xfree=xfree):
                        if xfree:
                            q.wait_ge(sV.h, xfree)
                        dXp[par].inc(q.dma_start(out=x2f[par][:], in_=x2_tile_v[b, tt]), 16)
                    Q(_)
                    x_ld = dXp[par].n

                    ptrb_free = ftrk["ptrb"]

                    def _(t, b=b, tt=tt, ptrb_free=ptrb_free):
                        t.wait_ge(sV.h, ftrk["comb"][b])
                        if ptrb_free:
                            t.wait_ge(sA.h, ptrb_free)
                        t.transpose(ptrb[:, 0:128], comb[:, b, tt, 0:128], identb[:])
                        sP.inc(t.transpose(ptrb[:, 128:256], comb[:, b, tt, 128:256], identb[:]))
                    T(_)
                    p_cT = sP.n

                    def _(s, p_cT=p_cT):
                        s.wait_ge(sP.h, p_cT)
                        sA.inc(s.copy(cb_sm[:], ptrb[:, 0:256]))
                    A(_)
                    a_cb = sA.n
                    ftrk["ptrb"] = a_cb

                    pbc_free = ftrk.get("pBC_F3", 0)

                    def _(t, b=b, a_cb=a_cb, pbc_free=pbc_free):
                        t.wait_ge(sA.h, a_cb)
                        t.wait_ge(sV.h, ftrk["so_ready"])
                        if pbc_free:
                            t.wait_ge(sV.h, pbc_free)
                        for dh in range(2):
                            bank = pB if dh == 0 else pC
                            for c in range(2):
                                ii = t.matmul(bank[:, 0:512], cb_sm[:, c * 128:c * 128 + 128],
                                              so_bf[:, b, c, dh * 512:dh * 512 + 512],
                                              start=(c == 0), stop=(c == 1), skip_group_check=True)
                        sP.inc(ii)
                    T(_)
                    p_o = sP.n

                    def _(v, p_o=p_o):
                        v.wait_ge(sP.h, p_o)
                        sV.inc(v.bn_stats(bnF[:, 0:6], pB[:, 0:512]))
                        sV.inc(v.bn_stats(bnF[:, 6:12], pC[:, 0:512]))
                        v.wait_ge(sV.h, sV.n)
                        sV.inc(v.bn_aggr(mvF[:], bnF[:]))
                        v.wait_ge(sV.h, sV.n)
                        sV.inc(v.tensor_scalar(scF[:, 0:1], mvF[:, 1:2], EPS, None, op0=ALU.add))
                        v.wait_ge(sV.h, sV.n)
                        sV.inc(v.reciprocal(scF[:, 1:2], scF[:, 0:1]))
                    V(_)
                    v_u6 = sV.n

                    def _(s, v_u6=v_u6):
                        s.wait_ge(sV.h, v_u6)
                        sA.inc(s.activation(scF[:, 2:3], scF[:, 1:2], AF.Ln))
                        s.wait_ge(sA.h, sA.n)
                        sA.inc(s.activation(scF[:, 3:4], scF[:, 2:3], AF.Exp, scale=0.5))
                    A(_)
                    a_r6 = sA.n

                    ofree = ftrk.get(f"ostore{par}", 0)

                    def _(v, b=b, tt=tt, par=par, a_r6=a_r6, x_ld=x_ld, ofree=ofree):
                        v.wait_ge(sA.h, a_r6)
                        v.wait_ge(dXp[par].h, x_ld)
                        if ofree:
                            v.wait_ge(dOp[par].h, ofree)
                        sV.inc(v.tensor_scalar(scF[:, 4:5], mvF[:, 0:1], scF[:, 3:4], -1.0, op0=ALU.mult, op1=ALU.mult))
                        v.wait_ge(sV.h, sV.n)
                        sV.inc(v.tensor_scalar(ln5_t[:, 0:512], pB[:, 0:512], scF[:, 3:4], scF[:, 4:5], op0=ALU.mult, op1=ALU.add))
                        sV.inc(v.tensor_scalar(ln5_t[:, 512:1024], pC[:, 0:512], scF[:, 3:4], scF[:, 4:5], op0=ALU.mult, op1=ALU.add))
                        v.wait_ge(sV.h, sV.n)
                        sV.inc(v.tensor_tensor(out_t[par][:], x2f[par][:], ln5_t[:], op=ALU.add))
                    V(_)
                    v_out = sV.n
                    ftrk["pBC_F3"] = v_out
                    ftrk[f"x2r{par}"] = v_out

                    def _(q, b=b, tt=tt, par=par, v_out=v_out):
                        q.wait_ge(sV.h, v_out)
                        dOp[par].inc(q.dma_start(out=out_tile_v[b, tt], in_=out_t[par][:]), 16)
                    Q(_)
                    ftrk[f"ostore{par}"] = dOp[par].n

            emit_F1(0)
            emit_F1(1)
            esFd.close()
            cb_sm = esF.enter_context(nc.sbuf_tensor("cb_sm", [P, 256], BF16))
            ln5_t = esF.enter_context(nc.sbuf_tensor("ln5_t", [P, D], F32))
            out_t = [esF.enter_context(nc.sbuf_tensor(f"outt{i}", [P, D], F32)) for i in range(2)]
            emit_experts()
            emit_F3(0)
            emit_F3(1)
            esF.close()

        x2_done = (dSp[0].n, dSp[1].n)

        def _(q):
            if x2_done[0]:
                q.wait_ge(dSp[0].h, x2_done[0])
            if x2_done[1]:
                q.wait_ge(dSp[1].h, x2_done[1])
            if "B" in phases and "F" not in phases:
                for b in range(2):
                    dO.inc(q.dma_start(out=out_d[b], in_=x2scr[b]), 16)
            if dO.n:
                q.wait_ge(dO.h, dO.n)
            if dOp[0].n:
                q.wait_ge(dOp[0].h, dOp[0].n)
            if dOp[1].n:
                q.wait_ge(dOp[1].h, dOp[1].n)
            if dDbg.n:
                q.wait_ge(dDbg.h, dDbg.n)
        Q(_)

    return nc, dbg_outs


def _host_prep(inp):
    f = np.float32
    Wq, Wk, Wv, Wo = (np.asarray(inp[k], f) for k in ("L0_attn_Wq", "L0_attn_Wk", "L0_attn_Wv", "L0_attn_Wo"))

    def wlay(W):
        return np.ascontiguousarray(
            W.T.reshape(8, 128, 8, 128).transpose(1, 0, 2, 3).reshape(128, 8192))

    planes_T = np.asarray(inp["L0_attn_planes_T"], f)
    negA = np.zeros((64, 1024), f)
    for j in range(64):
        negA[j, j * 16:(j + 1) * 16] = -1.0
    negB = np.zeros((8, 1024), f)
    for j in range(8):
        negB[j, j * 128:(j + 1) * 128] = -1.0
    mu = np.asarray(inp["L0_ffn_mu"], f).reshape(1024, 256)
    w1 = np.asarray(inp["L0_ffn_w1"], f)
    w2 = np.asarray(inp["L0_ffn_w2"], f)
    b1 = np.asarray(inp["L0_ffn_b1"], f)
    com = {
        "wq": wlay(Wq), "wk": wlay(Wk), "wv": wlay(Wv), "wo": wlay(Wo),
        "bo": np.asarray(inp["L0_attn_bo"], f).reshape(1, 1024),
        "planes2": np.concatenate([planes_T, planes_T], 0),
        "protos": np.asarray(inp["L0_attn_protos_T"], f),
        "ltemp": np.asarray(inp["L0_attn_logit_temp"], f).reshape(1, 1),
        "negindA": negA, "negindB": negB,
        "ident": np.eye(128, dtype=f),
        "ones1": np.ones((1, 128), f),
        "onescol": np.ones((128, 1), f),
        "mul": np.ascontiguousarray(mu.reshape(8, 128, 256).transpose(1, 0, 2).reshape(128, 2048)),
        "gam": np.ascontiguousarray(np.broadcast_to(np.asarray(inp["L0_ffn_ln_gamma"], f), (128, 1024))),
        "bet": np.ascontiguousarray(np.broadcast_to(np.asarray(inp["L0_ffn_ln_beta"], f), (128, 1024))),
        "scl": np.ascontiguousarray(np.broadcast_to(np.asarray(inp["L0_ffn_scale"], f).reshape(1, 1), (128, 1))),
        "w1l": np.ascontiguousarray(w1.reshape(E, 8, 128, 2, 128).transpose(0, 2, 1, 3, 4).reshape(E, 128, 2048)),
        "w2l": np.ascontiguousarray(w2.reshape(E, 2, 128, 2, 512).transpose(0, 2, 1, 3, 4).reshape(E, 128, 2048)),
        "b1f": np.ascontiguousarray(b1.reshape(16, 2, 128).transpose(2, 0, 1).reshape(128, 32)),
        "b2r": np.ascontiguousarray(np.repeat(np.asarray(inp["L0_ffn_b2"], f), 16, axis=0)),
    }
    return com


def kernel(**inputs):
    x = np.ascontiguousarray(np.asarray(inputs["L0_attn_x"], np.float32))
    com = _host_prep(inputs)
    nc, _ = build_program(DBG_NAMES)
    in_maps = []
    for c in range(8):
        m = dict(com)
        m["x"] = np.ascontiguousarray(x[2 * c:2 * c + 2])
        in_maps.append(m)
    res = run_bass_kernel_spmd(nc, in_maps, core_ids=list(range(8)))
    return np.concatenate([r["out"] for r in res.results], axis=0)


# version 02
# speedup vs baseline: 1.0469x; 1.0469x over previous
import numpy as np
from contextlib import ExitStack

import concourse.bass as bass
import concourse.mybir as mybir
from concourse.bass_utils import run_bass_kernel_spmd

F32 = mybir.dt.float32
BF16 = mybir.dt.bfloat16
AF = mybir.ActivationFunctionType
ALU = mybir.AluOpType
AX = mybir.AxisListType

P = 128
D = 1024
H = 16
HD = 64
NT = 16
HALF = 8
SB = 128
LTAB = 8
RB = 16
E = 16
ES = 256
HFF = 256
EPS = 1e-5

DBG_NAMES = ()
A_CFG = dict(bs=(0, 1), halves=(0, 1), tiles=8, buckets=True, stages=99, sub9=99)


class Sm:
    def __init__(self, es, nc, name):
        self.h = es.enter_context(nc.semaphore(name))
        self.n = 0

    def inc(self, instr, k=1):
        instr.then_inc(self.h, k)
        self.n += k
        return self.n


def build_program(dbg_names=(), phases=("A", "B", "F")):
    nc = bass.Bass("TRN2", target_bir_lowering=False, detect_race_conditions=False)
    dbg_outs = {}

    x_d = nc.dram_tensor("x", [2, 2048, D], F32, kind="ExternalInput").ap()
    wq_d = nc.dram_tensor("wq", [128, 8192], F32, kind="ExternalInput").ap()
    wk_d = nc.dram_tensor("wk", [128, 8192], F32, kind="ExternalInput").ap()
    wv_d = nc.dram_tensor("wv", [128, 8192], F32, kind="ExternalInput").ap()
    wo_d = nc.dram_tensor("wo", [128, 8192], F32, kind="ExternalInput").ap()
    bo_d = nc.dram_tensor("bo", [1, D], F32, kind="ExternalInput").ap()
    planes_d = nc.dram_tensor("planes2", [128, 32], F32, kind="ExternalInput").ap()
    protos_d = nc.dram_tensor("protos", [4, 16], F32, kind="ExternalInput").ap()
    ltemp_d = nc.dram_tensor("ltemp", [1, 1], F32, kind="ExternalInput").ap()
    negA_d = nc.dram_tensor("negindA", [64, 1024], F32, kind="ExternalInput").ap()
    negB_d = nc.dram_tensor("negindB", [8, 1024], F32, kind="ExternalInput").ap()
    ident_d = nc.dram_tensor("ident", [128, 128], F32, kind="ExternalInput").ap()
    ones1_d = nc.dram_tensor("ones1", [1, 128], F32, kind="ExternalInput").ap()
    onescol_d = nc.dram_tensor("onescol", [128, 1], F32, kind="ExternalInput").ap()
    nh8_d = nc.dram_tensor("nh8", [128, 8], F32, kind="ExternalInput").ap()
    nh1_d = nc.dram_tensor("nh1", [128, 1], F32, kind="ExternalInput").ap()
    ph1_d = nc.dram_tensor("ph1", [128, 1], F32, kind="ExternalInput").ap()
    mu_d = nc.dram_tensor("mul", [128, 2048], F32, kind="ExternalInput").ap()
    gam_d = nc.dram_tensor("gam", [128, D], F32, kind="ExternalInput").ap()
    bet_d = nc.dram_tensor("bet", [128, D], F32, kind="ExternalInput").ap()
    scl_d = nc.dram_tensor("scl", [128, 1], F32, kind="ExternalInput").ap()
    w1_d = nc.dram_tensor("w1l", [E, 128, 2048], F32, kind="ExternalInput").ap()
    w2_d = nc.dram_tensor("w2l", [E, 128, 2048], F32, kind="ExternalInput").ap()
    b1_d = nc.dram_tensor("b1f", [128, 32], F32, kind="ExternalInput").ap()
    b2_d = nc.dram_tensor("b2r", [256, D], F32, kind="ExternalInput").ap()

    out_d = nc.dram_tensor("out", [2, 2048, D], F32, kind="ExternalOutput").ap()
    x2scr = nc.dram_tensor("x2scr", [2, 2048, D], F32, kind="Internal").ap()

    x_half_v = x_d.rearrange("b (hh t p) d -> b hh p t d", hh=2, t=HALF, p=P)
    x_tile_v = x_d.rearrange("b (t p) d -> b t p d", t=NT, p=P)
    x2_tile_v = x2scr.rearrange("b (t p) d -> b t p d", t=NT, p=P)
    out_tile_v = out_d.rearrange("b (t p) d -> b t p d", t=NT, p=P)

    es = ExitStack()
    with es:
        sP = Sm(es, nc, "sP")
        sV = Sm(es, nc, "sV")
        sA = Sm(es, nc, "sA")
        sG = Sm(es, nc, "sG")
        dXA = Sm(es, nc, "dXA")
        dXp = [Sm(es, nc, "dX0"), Sm(es, nc, "dX1")]
        dCon = Sm(es, nc, "dCon")
        dPro = Sm(es, nc, "dPro")
        dWkv = Sm(es, nc, "dWkv")
        dWqo = Sm(es, nc, "dWqo")
        dE = [Sm(es, nc, "dE0"), Sm(es, nc, "dE1")]
        dSp = [Sm(es, nc, "dS0"), Sm(es, nc, "dS1")]
        dSh = [Sm(es, nc, "dSh0"), Sm(es, nc, "dSh1")]
        dO = Sm(es, nc, "dO")
        dOp = [Sm(es, nc, "dO0"), Sm(es, nc, "dO1")]
        dDbg = Sm(es, nc, "dDbg")

        ptrb = es.enter_context(nc.psum_tensor("ptrb", [P, 1024], BF16))
        ptrf = es.enter_context(nc.psum_tensor("ptrf", [P, 256], F32))
        pA = es.enter_context(nc.psum_tensor("pA", [P, 512], F32))
        pB = es.enter_context(nc.psum_tensor("pB", [P, 512], F32))
        pC = es.enter_context(nc.psum_tensor("pC", [P, 512], F32))
        pD = es.enter_context(nc.psum_tensor("pD", [P, 512], F32))
        pE = es.enter_context(nc.psum_tensor("pE", [P, 1024], F32))

        identb = es.enter_context(nc.sbuf_tensor("identb", [128, 128], BF16))
        identf = es.enter_context(nc.sbuf_tensor("identf", [128, 128], F32))
        onesb = es.enter_context(nc.sbuf_tensor("onesb", [1, 128], BF16))
        ones1f = es.enter_context(nc.sbuf_tensor("ones1f", [1, 128], F32))
        onescolb = es.enter_context(nc.sbuf_tensor("onescolb", [128, 1], BF16))
        negA = es.enter_context(nc.sbuf_tensor("negA", [64, 1024], BF16))
        negB = es.enter_context(nc.sbuf_tensor("negB", [8, 1024], BF16))
        planes = es.enter_context(nc.sbuf_tensor("planes", [128, 32], BF16))
        PbigF = es.enter_context(nc.sbuf_tensor("PbigF", [32, 128], F32))
        PbigS = es.enter_context(nc.sbuf_tensor("PbigS", [128, 128], BF16))
        bo_s = es.enter_context(nc.sbuf_tensor("bo_s", [1, D], BF16))
        lt_s = es.enter_context(nc.sbuf_tensor("lt_s", [1, 8], F32))
        nh8_s = es.enter_context(nc.sbuf_tensor("nh8_s", [128, 8], F32))
        nh1_s = es.enter_context(nc.sbuf_tensor("nh1_s", [128, 1], F32))
        ph1_s = es.enter_context(nc.sbuf_tensor("ph1_s", [128, 1], F32))
        invs32 = es.enter_context(nc.sbuf_tensor("invs32", [32, 1], F32))

        block = es.enter_context(nc.Block())
        T = block.tensor
        V = block.vector
        A = block.scalar
        G = block.gpsimd
        Q = block.sync

        def dump(name, ap, sem, val):
            if name not in dbg_names:
                return
            dd = nc.dram_tensor("dbg_" + name, list(ap.shape), F32, kind="ExternalOutput").ap()
            dbg_outs[name] = dd

            def _(g):
                g.wait_ge(sem.h, val)
                dDbg.inc(g.dma_start(out=dd, in_=ap), 16)
            G(_)

        def _(g):
            g.dma_start(out=identb[:], in_=ident_d).then_inc(dCon.h, 16)
            g.dma_start(out=identf[:], in_=ident_d).then_inc(dCon.h, 16)
            g.dma_start(out=onesb[:], in_=ones1_d).then_inc(dCon.h, 16)
            g.dma_start(out=ones1f[:], in_=ones1_d).then_inc(dCon.h, 16)
            g.dma_start(out=onescolb[:], in_=onescol_d).then_inc(dCon.h, 16)
            g.dma_start(out=negA[:], in_=negA_d).then_inc(dCon.h, 16)
            g.dma_start(out=negB[:], in_=negB_d).then_inc(dCon.h, 16)
            g.dma_start(out=planes[:], in_=planes_d).then_inc(dCon.h, 16)
            g.dma_start(out=bo_s[:], in_=bo_d).then_inc(dCon.h, 16)
            g.dma_start(out=lt_s[0:1, 0:1], in_=ltemp_d).then_inc(dCon.h, 16)
            g.dma_start(out=nh8_s[:], in_=nh8_d).then_inc(dCon.h, 16)
            g.dma_start(out=nh1_s[:], in_=nh1_d).then_inc(dCon.h, 16)
            g.dma_start(out=ph1_s[:], in_=ph1_d).then_inc(dCon.h, 16)
            dCon.n += 208
        G(_)
        w_pre = dCon.n

        def _(v):
            sV.inc(v.memset(PbigF[:], 0.0))
        V(_)
        v_pz = sV.n

        def _(g):
            g.wait_ge(sV.h, v_pz)
            for l in range(LTAB):
                g.dma_start(out=PbigF[4 * l:4 * l + 4, 16 * l:16 * l + 16], in_=protos_d).then_inc(dPro.h, 16)
            dPro.n += 16 * LTAB
        G(_)
        w_pbig = dPro.n

        def _(s):
            s.wait_ge(dCon.h, w_pre)
            sA.inc(s.activation(lt_s[0:1, 1:2], lt_s[0:1, 0:1], AF.Exp))
        A(_)
        a_e = sA.n

        def _(v):
            v.wait_ge(sA.h, a_e)
            sV.inc(v.tensor_scalar(lt_s[0:1, 2:3], lt_s[0:1, 1:2], 20.0, 0.01, op0=ALU.min, op1=ALU.max))
            v.wait_ge(sV.h, sV.n)
            sV.inc(v.reciprocal(lt_s[0:1, 3:4], lt_s[0:1, 2:3]))
        V(_)
        v_invs = sV.n

        def _(t):
            t.wait_ge(sV.h, v_invs)
            sP.inc(t.matmul(pD[0:32, 0:1], ones1f[0:1, 0:32], lt_s[0:1, 3:4], start=True, stop=True))
        T(_)
        p_inv = sP.n

        def _(s):
            s.wait_ge(sP.h, p_inv)
            sA.inc(s.copy(invs32[:], pD[0:32, 0:1]))
        A(_)
        a_inv = sA.n

        def _(v):
            v.wait_ge(sA.h, a_inv)
            v.wait_ge(dPro.h, w_pbig)
            sV.inc(v.tensor_scalar(PbigS[0:32, :], PbigF[:], invs32[:], None, op0=ALU.mult))
            sV.inc(v.tensor_scalar(PbigS[64:96, :], PbigF[:], invs32[:], None, op0=ALU.mult))
        V(_)
        v_pbig = sV.n
        dump("PbigS", PbigS[0:32, :], sV, v_pbig)

        wq_s = es.enter_context(nc.sbuf_tensor("wq_s", [128, 8192], BF16))
        wo_s = es.enter_context(nc.sbuf_tensor("wo_s", [128, 8192], BF16))

        def _(g):
            g.dma_start(out=wq_s[:], in_=wq_d).then_inc(dWqo.h, 16)
            g.dma_start(out=wo_s[:], in_=wo_d).then_inc(dWqo.h, 16)
            dWqo.n += 32
        G(_)

        stats = es.enter_context(nc.sbuf_tensor("stats", [P, 64], F32))
        negm1 = es.enter_context(nc.sbuf_tensor("negm1", [P, 32], F32))
        r1s = es.enter_context(nc.sbuf_tensor("r1s", [P, 32], F32))
        xn1 = es.enter_context(nc.sbuf_tensor("xn1", [P, D], BF16))
        xn1f = es.enter_context(nc.sbuf_tensor("xn1f", [P, D], BF16))

        esAB = es.enter_context(ExitStack())
        bKTa = [esAB.enter_context(nc.sbuf_tensor(f"bKTa{b}", [64, 2048], F32)) for b in range(2)]
        bVa = [esAB.enter_context(nc.sbuf_tensor(f"bVa{b}", [P, 1024], F32)) for b in range(2)]

        esA = ExitStack()
        wk_s = esA.enter_context(nc.sbuf_tensor("wk_s", [128, 8192], BF16))
        wv_s = esA.enter_context(nc.sbuf_tensor("wv_s", [128, 8192], BF16))
        x_sb = esA.enter_context(nc.sbuf_tensor("x_sb", [P, HALF, D], F32))
        Ktm = esA.enter_context(nc.sbuf_tensor("Ktm", [P, HALF, D], BF16))
        Vtm = esA.enter_context(nc.sbuf_tensor("Vtm", [P, HALF, D], BF16))
        probs = esA.enter_context(nc.sbuf_tensor("probs", [P, HALF, 2048], BF16))
        bnbuf = esA.enter_context(nc.sbuf_tensor("bnbuf", [P, 96], F32))
        tsb = esA.enter_context(nc.sbuf_tensor("tsb", [P, 8], F32))
        tsb2 = esA.enter_context(nc.sbuf_tensor("tsb2", [P, 8], F32))
        Kfm = esA.enter_context(nc.sbuf_tensor("Kfm", [P, D], BF16))
        tanhp = esA.enter_context(nc.sbuf_tensor("tanhp", [P, D], BF16))
        nmaxA = esA.enter_context(nc.sbuf_tensor("nmaxA", [P, 64], F32))
        cTA = esA.enter_context(nc.sbuf_tensor("cTA", [64, 128], BF16))
        expA = esA.enter_context(nc.sbuf_tensor("expA", [P, 1024], F32))
        ZA = esA.enter_context(nc.sbuf_tensor("ZA", [P, 64], F32))
        RA = esA.enter_context(nc.sbuf_tensor("RA", [P, 64], F32))

        def _(g):
            g.dma_start(out=wk_s[:], in_=wk_d).then_inc(dWkv.h, 16)
            g.dma_start(out=wv_s[:], in_=wv_d).then_inc(dWkv.h, 16)
            dWkv.n += 32
        G(_)
        w_attn = dWkv.n

        trk = {
            "x_free": 0,
            "ptrb_free": 0,
            "ptrf_free": 0,
            "pAB_free_pe": 0,
            "pCD_free": 0,
            "pE_free": 0,
            "prod_free_pe": 0,
            "bk_ready": 0,
            "x2_store": 0,
        }

        def emit_A(b):
            for half in A_CFG["halves"]:
                g0 = half * HALF
                x_free = trk["x_free"]
                prod_free = trk["prod_free_pe"]

                def _(q):
                    if x_free:
                        q.wait_ge(sV.h, x_free)
                    dXA.inc(q.dma_start(out=x_sb[:], in_=x_half_v[b, half]), 16)
                Q(_)
                x_ld = dXA.n

                def _(v):
                    v.wait_ge(dXA.h, x_ld)
                    for tt in range(HALF):
                        sV.inc(v.bn_stats(bnbuf[:, tt * 12:tt * 12 + 6], x_sb[:, tt, 0:512]))
                        sV.inc(v.bn_stats(bnbuf[:, tt * 12 + 6:tt * 12 + 12], x_sb[:, tt, 512:1024]))
                    v.wait_ge(sV.h, sV.n)
                    for tt in range(HALF):
                        col = b * 32 + (g0 + tt) * 2
                        sV.inc(v.bn_aggr(stats[:, col:col + 2], bnbuf[:, tt * 12:tt * 12 + 12]))
                    v.wait_ge(sV.h, sV.n)
                    vsl = stats[:, b * 32 + g0 * 2 + 1: b * 32 + (g0 + HALF) * 2: 2]
                    msl = stats[:, b * 32 + g0 * 2: b * 32 + (g0 + HALF) * 2: 2]
                    sV.inc(v.tensor_scalar(tsb[:], vsl, EPS, None, op0=ALU.add))
                    sV.inc(v.tensor_scalar(negm1[:, b * 16 + g0: b * 16 + g0 + HALF], msl, -1.0, None, op0=ALU.mult))
                V(_)
                v_stats = sV.n
                dump(f"xsb_b{b}h{half}", x_sb[:, 0, :], dXA, x_ld)
                dump(f"stats_b{b}h{half}", stats[:, b * 32 + g0 * 2: b * 32 + (g0 + HALF) * 2], sV, v_stats)

                def _(g):
                    g.wait_ge(sV.h, v_stats)
                    g.wait_ge(dCon.h, 208)
                    sG.inc(g.tensor_tensor(r1s[:, b * 16 + g0: b * 16 + g0 + HALF], tsb[:], nh8_s[:], op=ALU.pow))
                G(_)
                g_r1 = sG.n
                dump(f"r1s_b{b}h{half}", r1s[:, b * 16 + g0:b * 16 + g0 + HALF], sG, g_r1)

                for tt in range(A_CFG["tiles"]):
                    gt = g0 + tt
                    ncol = b * 16 + gt

                    p_xn1T_prev = trk.get("p_xn1T", 0)

                    def _(v, tt=tt, ncol=ncol, p_xn1T_prev=p_xn1T_prev, g_r1=g_r1):
                        v.wait_ge(sG.h, g_r1)
                        if p_xn1T_prev:
                            v.wait_ge(sP.h, p_xn1T_prev)
                        sV.inc(v.tensor_scalar(xn1[:], x_sb[:, tt, :], negm1[:, ncol:ncol + 1],
                                               r1s[:, ncol:ncol + 1], op0=ALU.add, op1=ALU.mult))
                    V(_)
                    v_xn1 = sV.n

                    if A_CFG['stages'] < 2:
                        continue
                    ptrb_free = trk["ptrb_free"]

                    def _(t, v_xn1=v_xn1, ptrb_free=ptrb_free, prod_free=prod_free):
                        t.wait_ge(sV.h, v_xn1)
                        t.wait_ge(dCon.h, 208)
                        t.wait_ge(dWkv.h, 32)
                        if ptrb_free:
                            t.wait_ge(sA.h, ptrb_free)
                        for kc in range(8):
                            ii = t.transpose(ptrb[:, kc * 128:kc * 128 + 128], xn1[:, kc * 128:kc * 128 + 128], identb[:])
                        sP.inc(ii)
                    T(_)
                    trk["p_xn1T"] = sP.n
                    p_xn1T = sP.n

                    if A_CFG['stages'] < 3:
                        continue
                    def _(s, p_xn1T=p_xn1T):
                        s.wait_ge(sP.h, p_xn1T)
                        sA.inc(s.copy(xn1f[:], ptrb[:]))
                    A(_)
                    a_xn1f = sA.n

                    if A_CFG['stages'] < 4:
                        continue
                    pab_free = trk["pAB_free_pe"]

                    def _(t, a_xn1f=a_xn1f, pab_free=pab_free):
                        t.wait_ge(sA.h, a_xn1f)
                        if pab_free:
                            t.wait_ge(sA.h, pab_free)
                        for kc in range(8):
                            for oc in range(8):
                                bank = pA if oc < 4 else pB
                                ii = t.matmul(bank[:, (oc % 4) * 128:(oc % 4) * 128 + 128],
                                              wk_s[:, kc * 1024 + oc * 128:kc * 1024 + oc * 128 + 128],
                                              xn1f[:, kc * 128:kc * 128 + 128],
                                              start=(kc == 0 and oc % 4 == 0),
                                              stop=(kc == 7 and oc % 4 == 3),
                                              skip_group_check=True)
                        sP.inc(ii)
                    T(_)
                    p_kfm = sP.n

                    if A_CFG['stages'] < 5:
                        continue
                    def _(s, p_kfm=p_kfm):
                        s.wait_ge(sP.h, p_kfm)
                        sA.inc(s.copy(Kfm[:, 0:512], pA[:]))
                        sA.inc(s.copy(Kfm[:, 512:1024], pB[:]))
                    A(_)
                    a_kfm = sA.n

                    if A_CFG['stages'] < 6:
                        continue
                    def _(t, a_kfm=a_kfm):
                        t.wait_ge(sA.h, a_kfm)
                        for oc in range(8):
                            ii = t.transpose(ptrb[:, oc * 128:oc * 128 + 128], Kfm[:, oc * 128:oc * 128 + 128], identb[:])
                        sP.inc(ii)
                    T(_)
                    p_ktT = sP.n

                    def _(s, p_ktT=p_ktT, tt=tt):
                        s.wait_ge(sP.h, p_ktT)
                        sA.inc(s.copy(Ktm[:, tt, :], ptrb[:]))
                    A(_)
                    a_ktm = sA.n
                    trk["ptrb_free"] = a_ktm

                    if A_CFG['stages'] < 7:
                        continue
                    def _(t, a_kfm=a_kfm, v_pbig=v_pbig):
                        t.wait_ge(sV.h, v_pbig)
                        t.wait_ge(sA.h, a_kfm)
                        for h in range(H):
                            bank = pA if h < 8 else pB
                            base = 64 * (h % 2)
                            cs = (h % 8) // 2
                            ii = t.matmul(bank[base:base + 32, cs * 128:cs * 128 + 128],
                                          planes[base:base + 64, :],
                                          Kfm[base:base + 64, (h // 2) * 128:(h // 2) * 128 + 128],
                                          start=(h % 8 in (0, 1)), stop=(h % 8 in (6, 7)),
                                          skip_group_check=True)
                        sP.inc(ii)
                    T(_)
                    p_pj = sP.n

                    def _(s, p_pj=p_pj):
                        s.wait_ge(sP.h, p_pj)
                        sA.inc(s.activation(tanhp[:, 0:512], pA[:], AF.Tanh))
                        sA.inc(s.activation(tanhp[:, 512:1024], pB[:], AF.Tanh))
                    A(_)
                    a_tanh = sA.n
                    trk["pAB_free_pe"] = a_tanh

                    if A_CFG['stages'] < 8:
                        continue
                    pcd_free = trk["pCD_free"]

                    def _(t, a_xn1f=a_xn1f, pcd_free=pcd_free):
                        t.wait_ge(sA.h, a_xn1f)
                        if pcd_free:
                            t.wait_ge(sA.h, pcd_free)
                        for dh in range(2):
                            bank = pC if dh == 0 else pD
                            for kc in range(8):
                                ii = t.matmul(bank[:, 0:512],
                                              xn1f[:, kc * 128:kc * 128 + 128],
                                              wv_s[:, kc * 1024 + dh * 512:kc * 1024 + dh * 512 + 512],
                                              start=(kc == 0), stop=(kc == 7))
                        sP.inc(ii)
                    T(_)
                    p_v = sP.n

                    def _(s, p_v=p_v, tt=tt):
                        s.wait_ge(sP.h, p_v)
                        sA.inc(s.copy(Vtm[:, tt, 0:512], pC[:]))
                        sA.inc(s.copy(Vtm[:, tt, 512:1024], pD[:]))
                    A(_)
                    a_v = sA.n
                    trk["pCD_free"] = a_v

                    if A_CFG['stages'] < 9:
                        continue
                    for grp in range(2):
                        pE_free = trk["pE_free"]

                        def _(t, grp=grp, a_tanh=a_tanh, pE_free=pE_free):
                            t.wait_ge(sA.h, a_tanh)
                            if pE_free:
                                t.wait_ge(sV.h, pE_free)
                            for hl in range(8):
                                h = grp * 8 + hl
                                base = 64 * (h % 2)
                                col = (h // 8) * 512 + ((h % 8) // 2) * 128
                                oc = (hl // 2) * 128 + (hl % 2) * 512
                                ii = t.matmul(pE[:, oc:oc + 128],
                                              tanhp[base:base + 32, col:col + 128],
                                              PbigS[base:base + 32, :],
                                              start=(hl in (0, 1)), stop=(hl in (6, 7)),
                                              skip_group_check=True)
                            sP.inc(ii)
                        T(_)
                        p_lg = sP.n

                        if A_CFG['sub9'] < 2:
                            continue
                        def _(v, p_lg=p_lg):
                            v.wait_ge(sP.h, p_lg)
                            sV.inc(v.tensor_reduce(nmaxA[:], pE[:].rearrange("p (s r) -> p s r", r=RB),
                                                   axis=AX.X, op=ALU.max, negate=False))
                        V(_)
                        v_nm = sV.n

                        if A_CFG['sub9'] < 3:
                            continue
                        ptrf_free = trk["ptrf_free"]

                        def _(t, v_nm=v_nm, ptrf_free=ptrf_free):
                            t.wait_ge(sV.h, v_nm)
                            if ptrf_free:
                                t.wait_ge(sA.h, ptrf_free)
                            sP.inc(t.transpose(ptrf[0:64, 0:128], nmaxA[:], identf[:]))
                        T(_)
                        p_nt = sP.n

                        if A_CFG['sub9'] < 4:
                            continue
                        def _(s, p_nt=p_nt):
                            s.wait_ge(sP.h, p_nt)
                            sA.inc(s.copy(cTA[:], ptrf[0:64, 0:128]))
                        A(_)
                        a_ct = sA.n
                        trk["ptrf_free"] = a_ct

                        if A_CFG['sub9'] < 5:
                            continue
                        def _(t, a_ct=a_ct):
                            t.wait_ge(sA.h, a_ct)
                            for j in range(2):
                                ii = t.matmul(pE[:, j * 512:j * 512 + 512], cTA[:], negA[:, j * 512:j * 512 + 512],
                                              start=False, stop=True, skip_group_check=True)
                            sP.inc(ii)
                        T(_)
                        p_bm = sP.n

                        if A_CFG['sub9'] < 6:
                            continue
                        def _(s, p_bm=p_bm):
                            s.wait_ge(sP.h, p_bm)
                            sA.inc(s.activation(expA[:], pE[:], AF.Exp))
                        A(_)
                        a_exp = sA.n

                        if A_CFG['sub9'] < 7:
                            continue
                        def _(v, a_exp=a_exp):
                            v.wait_ge(sA.h, a_exp)
                            sV.inc(v.tensor_reduce(ZA[:], expA[:].rearrange("p (s r) -> p s r", r=RB),
                                                   axis=AX.X, op=ALU.add))
                            v.wait_ge(sV.h, sV.n)
                            sV.inc(v.reciprocal(RA[:], ZA[:]))
                        V(_)
                        v_r = sV.n

                        if A_CFG['sub9'] < 8:
                            continue
                        def _(v, v_r=v_r, tt=tt, grp=grp):
                            v.wait_ge(sV.h, v_r)
                            dst = probs[:, tt, grp * 1024:grp * 1024 + 1024].rearrange("p (s r) -> p s r", r=RB)
                            src = expA[:].rearrange("p (s r) -> p s r", r=RB)
                            bc = RA[:].unsqueeze(2).broadcast_to([P, 64, RB])
                            sV.inc(v.tensor_tensor(dst, src, bc, op=ALU.mult))
                        V(_)
                        trk["pE_free"] = sV.n
                        v_pm = sV.n

                    if b == 0 and half == 0 and tt == 0 and dbg_names:
                        dump("xn1_t0", xn1[:], sV, v_xn1)
                        dump("xn1f_t0", xn1f[:], sA, a_xn1f)
                        dump("Kfm_t0", Kfm[:], sA, a_kfm)
                        dump("tanhp_t0", tanhp[:], sA, a_tanh)
                        dump("expA_t0", expA[:], sA, a_exp)
                        dump("probs_t0", probs[:, 0, :], sV, v_pm)
                        dump("Ktm_t0", Ktm[:, 0, :], sA, a_ktm)
                        dump("Vtm_t0", Vtm[:, 0, :], sA, a_v)

                trk["x_free"] = sV.n

                a_tanh_last = trk["pAB_free_pe"]
                a_v_last = trk["pCD_free"]
                v_prod = sV.n
                a_prod = sA.n
                prev_bk = 0
                for h in (range(H) if A_CFG["buckets"] else ()):
                    kb = pA if h % 2 == 0 else pC
                    vb = pB if h % 2 == 0 else pD

                    def _(t, h=h, kb=kb, vb=vb, v_prod=v_prod, a_prod=a_prod, prev_bk=prev_bk):
                        if h == 0:
                            t.wait_ge(sV.h, v_prod)
                            t.wait_ge(sA.h, a_prod)
                        if prev_bk:
                            t.wait_ge(sV.h, prev_bk)
                        pcol = (h // 8) * 1024 + ((h % 8) // 2) * 128 + (h % 2) * 512
                        for tt in range(HALF):
                            t.matmul(kb[0:64, 0:128], Ktm[:, tt, h * 64:h * 64 + 64],
                                     probs[:, tt, pcol:pcol + 128],
                                     start=(tt == 0), stop=(tt == HALF - 1), skip_group_check=True)
                        for tt in range(HALF):
                            ii = t.matmul(vb[0:128, 0:64], probs[:, tt, pcol:pcol + 128],
                                          Vtm[:, tt, h * 64:h * 64 + 64],
                                          start=(tt == 0), stop=(tt == HALF - 1), skip_group_check=True)
                        sP.inc(ii)
                    T(_)
                    p_bk = sP.n

                    def _(v, h=h, kb=kb, vb=vb, half=half, p_bk=p_bk, b=b):
                        v.wait_ge(sP.h, p_bk)
                        if half == 0:
                            sV.inc(v.tensor_copy(bKTa[b][:, h * 128:h * 128 + 128], kb[0:64, 0:128]))
                            sV.inc(v.tensor_copy(bVa[b][:, h * 64:h * 64 + 64], vb[0:128, 0:64]))
                        else:
                            sV.inc(v.tensor_tensor(bKTa[b][:, h * 128:h * 128 + 128], bKTa[b][:, h * 128:h * 128 + 128],
                                                   kb[0:64, 0:128], op=ALU.add))
                            sV.inc(v.tensor_tensor(bVa[b][:, h * 64:h * 64 + 64], bVa[b][:, h * 64:h * 64 + 64],
                                                   vb[0:128, 0:64], op=ALU.add))
                    V(_)
                    prev_bk = sV.n
                trk["pAB_free_pe"] = 0
                trk["pCD_free"] = 0
                trk["bk_evac"] = prev_bk
                trk["prod_free_pe"] = sP.n

                def _(t, prev_bk=prev_bk):
                    t.wait_ge(sV.h, prev_bk)
                    sP.inc(t.nop())
                T(_)

            trk[f"bk_evac{b}"] = trk["bk_evac"]
            dump(f"bKT_b{b}", bKTa[b][:], sV, trk["bk_evac"])
            dump(f"bV_b{b}", bVa[b][:], sV, trk["bk_evac"])

        def emit_B(b):
            def _(v, b=b):
                v.wait_ge(sV.h, trk[f"bk_evac{b}"])
                sV.inc(v.tensor_copy(bKTb[b][0:64, :], bKTa[b][:]))
                sV.inc(v.tensor_copy(bKTb[b][64:128, :], bKTa[b][:]))
                sV.inc(v.tensor_copy(bVb[b][:], bVa[b][:]))
            V(_)
            bk_ready = sV.n
            for tt in range(NT):
                par = tt % 2
                x2_free = trk.get(f"x2st_{par}", 0)
                xin_free = trk.get(f"xinfree_{par}", 0)

                def _(q, b=b, tt=tt, par=par, xin_free=xin_free):
                    if xin_free:
                        q.wait_ge(sV.h, xin_free)
                    dXp[par].inc(q.dma_start(out=xin[par][:], in_=x_tile_v[b, tt]), 16)
                Q(_)
                x_ld = dXp[par].n

                p_xn1T_prev = trk.get("p_xn1T_B", 0)

                def _(v, b=b, tt=tt, par=par, x_ld=x_ld, p_xn1T_prev=p_xn1T_prev):
                    v.wait_ge(dXp[par].h, x_ld)
                    if p_xn1T_prev:
                        v.wait_ge(sP.h, p_xn1T_prev)
                    ncol = b * 16 + tt
                    sV.inc(v.tensor_scalar(xn1[:], xin[par][:], negm1[:, ncol:ncol + 1],
                                           r1s[:, ncol:ncol + 1], op0=ALU.add, op1=ALU.mult))
                V(_)
                v_xn1 = sV.n

                ptrb_free = trk["ptrb_free"]

                def _(t, v_xn1=v_xn1, ptrb_free=ptrb_free):
                    t.wait_ge(sV.h, v_xn1)
                    if ptrb_free:
                        t.wait_ge(sA.h, ptrb_free)
                    for kc in range(8):
                        ii = t.transpose(ptrb[:, kc * 128:kc * 128 + 128], xn1[:, kc * 128:kc * 128 + 128], identb[:])
                    sP.inc(ii)
                T(_)
                trk["p_xn1T_B"] = sP.n
                p_xn1T = sP.n

                def _(s, p_xn1T=p_xn1T):
                    s.wait_ge(sP.h, p_xn1T)
                    sA.inc(s.copy(xn1f[:], ptrb[:]))
                A(_)
                a_xn1f = sA.n

                pab_free = trk.get("pAB_free_B", 0)

                def _(t, a_xn1f=a_xn1f, pab_free=pab_free):
                    t.wait_ge(sA.h, a_xn1f)
                    t.wait_ge(dWqo.h, 32)
                    if pab_free:
                        t.wait_ge(sV.h, pab_free)
                    for kc in range(8):
                        for oc in range(8):
                            bank = pA if oc < 4 else pB
                            ii = t.matmul(bank[:, (oc % 4) * 128:(oc % 4) * 128 + 128],
                                          wq_s[:, kc * 1024 + oc * 128:kc * 1024 + oc * 128 + 128],
                                          xn1f[:, kc * 128:kc * 128 + 128],
                                          start=(kc == 0 and oc % 4 == 0),
                                          stop=(kc == 7 and oc % 4 == 3),
                                          skip_group_check=True)
                    sP.inc(ii)
                T(_)
                p_qm = sP.n

                def _(s, p_qm=p_qm):
                    s.wait_ge(sP.h, p_qm)
                    sA.inc(s.copy(Qt[:, 0:512], pA[:]))
                    sA.inc(s.copy(Qt[:, 512:1024], pB[:]))
                A(_)
                a_q = sA.n

                for grp in range(2):
                    pE_free = trk["pE_free"]

                    def _(t, grp=grp, a_q=a_q, bk_ready=bk_ready, pE_free=pE_free, b=b):
                        t.wait_ge(sA.h, a_q)
                        t.wait_ge(sV.h, bk_ready)
                        if pE_free:
                            t.wait_ge(sV.h, pE_free)
                        for hl in range(8):
                            h = grp * 8 + hl
                            base = 64 * (h % 2)
                            oc = (hl // 2) * 128 + (hl % 2) * 512
                            ii = t.matmul(pE[:, oc:oc + 128],
                                          Qt[base:base + 64, (h // 2) * 128:(h // 2) * 128 + 128],
                                          bKTb[b][base:base + 64, h * 128:h * 128 + 128],
                                          start=(hl in (0, 1)), stop=(hl in (6, 7)),
                                          skip_group_check=True)
                        sP.inc(ii)
                    T(_)
                    p_sc = sP.n

                    def _(v, p_sc=p_sc):
                        v.wait_ge(sP.h, p_sc)
                        sV.inc(v.tensor_reduce(nmaxB[:], pE[:].rearrange("p (h s) -> p h s", s=128),
                                               axis=AX.X, op=ALU.max, negate=False))
                    V(_)
                    v_nm = sV.n

                    ptrf_free = trk["ptrf_free"]

                    def _(t, v_nm=v_nm, ptrf_free=ptrf_free):
                        t.wait_ge(sV.h, v_nm)
                        if ptrf_free:
                            t.wait_ge(sA.h, ptrf_free)
                        sP.inc(t.transpose(ptrf[0:8, 0:128], nmaxB[:], identf[:]))
                    T(_)
                    p_nt = sP.n

                    def _(s, p_nt=p_nt):
                        s.wait_ge(sP.h, p_nt)
                        sA.inc(s.copy(cTB[:], ptrf[0:8, 0:128]))
                    A(_)
                    a_ct = sA.n
                    trk["ptrf_free"] = a_ct

                    def _(t, a_ct=a_ct):
                        t.wait_ge(sA.h, a_ct)
                        for j in range(2):
                            ii = t.matmul(pE[:, j * 512:j * 512 + 512], cTB[:], negB[:, j * 512:j * 512 + 512],
                                          start=False, stop=True, skip_group_check=True)
                        sP.inc(ii)
                    T(_)
                    p_bm = sP.n

                    def _(s, p_bm=p_bm):
                        s.wait_ge(sP.h, p_bm)
                        sA.inc(s.activation(expB[:], pE[:], AF.Exp, scale=0.125))
                    A(_)
                    a_exp = sA.n

                    def _(v, a_exp=a_exp):
                        v.wait_ge(sA.h, a_exp)
                        sV.inc(v.tensor_reduce(ZB[:], expB[:].rearrange("p (h s) -> p h s", s=128),
                                               axis=AX.X, op=ALU.add))
                        v.wait_ge(sV.h, sV.n)
                        sV.inc(v.reciprocal(RBs[:], ZB[:]))
                        v.wait_ge(sV.h, sV.n)
                        sV.inc(v.tensor_tensor(attn_t[:].rearrange("p (h s) -> p h s", s=128),
                                               expB[:].rearrange("p (h s) -> p h s", s=128),
                                               RBs[:].unsqueeze(2).broadcast_to([P, 8, 128]), op=ALU.mult))
                    V(_)
                    v_attn = sV.n
                    trk["pE_free"] = v_attn

                    atb_free = trk.get("ptrb_free_B", a_xn1f)

                    def _(t, v_attn=v_attn, atb_free=atb_free):
                        t.wait_ge(sV.h, v_attn)
                        t.wait_ge(sA.h, atb_free)
                        for hl in range(8):
                            ii = t.transpose(ptrb[:, hl * 128:hl * 128 + 128], attn_t[:, hl * 128:hl * 128 + 128], identb[:])
                        sP.inc(ii)
                    T(_)
                    p_at = sP.n

                    def _(s, p_at=p_at):
                        s.wait_ge(sP.h, p_at)
                        sA.inc(s.copy(expT[:], ptrb[:]))
                    A(_)
                    a_expT = sA.n
                    trk["ptrb_free_B"] = a_expT
                    trk["ptrb_free"] = a_expT

                    pc_free = trk.get("pC_free_B", 0)

                    def _(t, grp=grp, a_expT=a_expT, pc_free=pc_free, b=b):
                        t.wait_ge(sA.h, a_expT)
                        if pc_free:
                            t.wait_ge(sA.h, pc_free)
                        for hl in range(8):
                            h = grp * 8 + hl
                            base = 64 * (h % 2)
                            cs = (h // 2) % 4
                            ec = (hl // 2) * 128 + (hl % 2) * 512
                            ii = t.matmul(pC[base:base + 64, cs * 128:cs * 128 + 128],
                                          bVb[b][:, h * 64:h * 64 + 64],
                                          expT[:, ec:ec + 128],
                                          start=(hl in (0, 1)), stop=(hl in (6, 7)), skip_group_check=True)
                        sP.inc(ii)
                    T(_)
                    p_om = sP.n

                    def _(s, grp=grp, p_om=p_om):
                        s.wait_ge(sP.h, p_om)
                        sA.inc(s.copy(aout[:, grp * 512:grp * 512 + 512], pC[:]))
                    A(_)
                    trk["pC_free_B"] = sA.n
                a_out2 = sA.n

                def _(t, a_out2=a_out2, a_q=a_q):
                    t.wait_ge(sA.h, a_out2)
                    for dh in range(2):
                        bank = pA if dh == 0 else pB
                        for kc in range(8):
                            t.matmul(bank[:, 0:512], aout[:, kc * 128:kc * 128 + 128],
                                     wo_s[:, kc * 1024 + dh * 512:kc * 1024 + dh * 512 + 512],
                                     start=(kc == 0), stop=False, skip_group_check=True)
                        ii = t.matmul(bank[:, 0:512], onesb[:], bo_s[0:1, dh * 512:dh * 512 + 512],
                                      start=False, stop=True, skip_group_check=True)
                    sP.inc(ii)
                T(_)
                p_wo = sP.n

                def _(v, p_wo=p_wo):
                    v.wait_ge(sP.h, p_wo)
                    sV.inc(v.bn_stats(bnb2[:, 0:6], pA[:]))
                    sV.inc(v.bn_stats(bnb2[:, 6:12], pB[:]))
                    v.wait_ge(sV.h, sV.n)
                    sV.inc(v.bn_aggr(mv2[:], bnb2[:]))
                    v.wait_ge(sV.h, sV.n)
                    sV.inc(v.tensor_scalar(sc2[:, 0:1], mv2[:, 1:2], EPS, None, op0=ALU.add))
                V(_)
                v_u2 = sV.n

                def _(g, v_u2=v_u2):
                    g.wait_ge(sV.h, v_u2)
                    sG.inc(g.tensor_tensor(sc2[:, 3:4], sc2[:, 0:1], nh1_s[:], op=ALU.pow))
                G(_)
                g_r2 = sG.n
                a_r2 = None

                x2_free = trk.get(f"x2free_{par}", 0)

                def _(v, par=par, g_r2=g_r2, x2_free=x2_free):
                    v.wait_ge(sG.h, g_r2)
                    if x2_free:
                        v.wait_ge(dSp[par].h, x2_free)
                    sV.inc(v.tensor_scalar(sc2[:, 0:1], mv2[:, 0:1], sc2[:, 3:4], -1.0, op0=ALU.mult, op1=ALU.mult))
                    v.wait_ge(sV.h, sV.n)
                    sV.inc(v.tensor_scalar(ln2t[:, 0:512], pA[:], sc2[:, 3:4], sc2[:, 0:1], op0=ALU.mult, op1=ALU.add))
                    sV.inc(v.tensor_scalar(ln2t[:, 512:1024], pB[:], sc2[:, 3:4], sc2[:, 0:1], op0=ALU.mult, op1=ALU.add))
                    v.wait_ge(sV.h, sV.n)
                    sV.inc(v.tensor_tensor(x2t[par][:], xin[par][:], ln2t[:], op=ALU.add))
                V(_)
                v_x2 = sV.n
                trk["pAB_free_B"] = v_x2
                trk[f"xinfree_{par}"] = v_x2

                def _(q, b=b, tt=tt, par=par, v_x2=v_x2):
                    q.wait_ge(sV.h, v_x2)
                    dSp[par].inc(q.dma_start(out=x2_tile_v[b, tt], in_=x2t[par][:]), 16)
                Q(_)
                trk[f"x2free_{par}"] = dSp[par].n

                if b == 0 and tt == 0:
                    dump("Qt_t0", Qt[:], sA, a_q)
                    dump("expT_t0", expT[:], sA, trk["ptrb_free_B"])
                    dump("aout_t0", aout[:], sA, a_out2)
                    dump("x2_t0", x2t[0][:], sV, v_x2)

        def barrier():
            vals = [(sP, sP.n), (sV, sV.n), (sA, sA.n), (sG, sG.n),
                    (dXA, dXA.n), (dXp[0], dXp[0].n), (dXp[1], dXp[1].n),
                    (dCon, dCon.n), (dPro, dPro.n), (dWkv, dWkv.n), (dWqo, dWqo.n),
                    (dSp[0], dSp[0].n), (dSp[1], dSp[1].n)]

            def _(e):
                for s, vv in vals:
                    if vv:
                        e.wait_ge(s.h, vv)
            for body in (T, V, A, G, Q):
                body(_)

        if "A" in phases:
            for _b in A_CFG["bs"]:
                emit_A(_b)
        barrier()
        esA.close()

        esB = ExitStack()
        Qt = esB.enter_context(nc.sbuf_tensor("Qt", [P, D], BF16))
        nmaxB = esB.enter_context(nc.sbuf_tensor("nmaxB", [P, 8], F32))
        cTB = esB.enter_context(nc.sbuf_tensor("cTB", [8, 128], BF16))
        expB = esB.enter_context(nc.sbuf_tensor("expB", [P, 1024], F32))
        ZB = esB.enter_context(nc.sbuf_tensor("ZB", [P, 8], F32))
        RBs = esB.enter_context(nc.sbuf_tensor("RBs", [P, 8], F32))
        attn_t = esB.enter_context(nc.sbuf_tensor("attn_t", [P, 1024], BF16))
        expT = esB.enter_context(nc.sbuf_tensor("expT", [P, 1024], BF16))
        aout = esB.enter_context(nc.sbuf_tensor("aout", [P, D], BF16))
        ln2t = esB.enter_context(nc.sbuf_tensor("ln2t", [P, D], F32))
        bnb2 = esB.enter_context(nc.sbuf_tensor("bnb2", [P, 12], F32))
        mv2 = esB.enter_context(nc.sbuf_tensor("mv2", [P, 2], F32))
        sc2 = esB.enter_context(nc.sbuf_tensor("sc2", [P, 4], F32))
        bKTb = [esB.enter_context(nc.sbuf_tensor(f"bKTb{b}", [P, 2048], BF16)) for b in range(2)]
        bVb = [esB.enter_context(nc.sbuf_tensor(f"bVb{b}", [P, 1024], BF16)) for b in range(2)]
        xin = [esB.enter_context(nc.sbuf_tensor(f"xin{i}", [P, D], F32)) for i in range(2)]
        x2t = [esB.enter_context(nc.sbuf_tensor(f"x2t{i}", [P, D], F32)) for i in range(2)]

        if "B" in phases:
            emit_B(0)
            emit_B(1)
        barrier()
        esB.close()
        esAB.close()

        if "F" in phases:
            dFc = Sm(es, nc, "dFc")
            esF = ExitStack()
            muu = esF.enter_context(nc.sbuf_tensor("muu", [P, 2048], BF16))
            gam_s = esF.enter_context(nc.sbuf_tensor("gam_s", [P, D], F32))
            bet_s = esF.enter_context(nc.sbuf_tensor("bet_s", [P, D], F32))
            scl_s = esF.enter_context(nc.sbuf_tensor("scl_s", [P, 1], F32))
            b1f_s = esF.enter_context(nc.sbuf_tensor("b1f_s", [P, 32], F32))
            b2r_s = esF.enter_context(nc.sbuf_tensor("b2r_s", [P, 2, D], BF16))
            esFpre = ExitStack()
            mu_sb = esFpre.enter_context(nc.sbuf_tensor("mu_sb", [P, 2048], F32))
            musq = esFpre.enter_context(nc.sbuf_tensor("musq", [P, 2048], BF16))
            ssmu = esFpre.enter_context(nc.sbuf_tensor("ssmu", [1, 1024], F32))
            rmub = esFpre.enter_context(nc.sbuf_tensor("rmub", [P, 256], F32))
            def _(g):
                g.dma_start(out=mu_sb[:], in_=mu_d).then_inc(dFc.h, 16)
                g.dma_start(out=gam_s[:], in_=gam_d).then_inc(dFc.h, 16)
                g.dma_start(out=bet_s[:], in_=bet_d).then_inc(dFc.h, 16)
                g.dma_start(out=scl_s[:], in_=scl_d).then_inc(dFc.h, 16)
                g.dma_start(out=b1f_s[:], in_=b1_d).then_inc(dFc.h, 16)
                g.dma_start(out=b2r_s[:], in_=b2_d.rearrange("(c p) d -> p c d", p=P)).then_inc(dFc.h, 16)
                dFc.n += 96
            G(_)
            w_fc = dFc.n

            def _(s):
                s.wait_ge(dFc.h, w_fc)
                sA.inc(s.activation(musq[:], mu_sb[:], AF.Square))
            A(_)
            a_musq = sA.n

            def _(t):
                t.wait_ge(sA.h, a_musq)
                for kc in range(8):
                    ii = t.matmul(pD[0:1, 0:256], onescolb[:], musq[:, kc * 256:kc * 256 + 256],
                                  start=(kc == 0), stop=(kc == 7))
                sP.inc(ii)
            T(_)
            p_mu = sP.n

            def _(s):
                s.wait_ge(sP.h, p_mu)
                sA.inc(s.copy(ssmu[0:1, 0:256], pD[0:1, 0:256]))
                s.wait_ge(sA.h, sA.n)
                sA.inc(s.activation(ssmu[0:1, 256:512], ssmu[0:1, 0:256], AF.Ln))
                s.wait_ge(sA.h, sA.n)
                sA.inc(s.activation(ssmu[0:1, 512:768], ssmu[0:1, 256:512], AF.Exp, scale=0.5))
            A(_)
            a_nmu = sA.n

            def _(v):
                v.wait_ge(sA.h, a_nmu)
                sV.inc(v.tensor_scalar(ssmu[0:1, 768:1024], ssmu[0:1, 512:768], 1e-12, None, op0=ALU.max))
                v.wait_ge(sV.h, sV.n)
                sV.inc(v.reciprocal(ssmu[0:1, 0:256], ssmu[0:1, 768:1024]))
            V(_)
            v_rmu = sV.n

            def _(t):
                t.wait_ge(sV.h, v_rmu)
                sP.inc(t.matmul(pD[0:128, 0:256], ones1f[:], ssmu[0:1, 0:256], start=True, stop=True))
            T(_)
            p_rmub = sP.n

            def _(s):
                s.wait_ge(sP.h, p_rmub)
                sA.inc(s.copy(rmub[:], pD[0:128, 0:256]))
            A(_)
            a_rmub = sA.n

            def _(v):
                v.wait_ge(sA.h, a_rmub)
                for kc in range(8):
                    sV.inc(v.tensor_tensor(muu[:, kc * 256:kc * 256 + 256], mu_sb[:, kc * 256:kc * 256 + 256],
                                           rmub[:], op=ALU.mult))
            V(_)
            v_muu = sV.n

            esFpre.close()
            xnn_sb = esF.enter_context(nc.sbuf_tensor("xnn_sb", [P, NT, D], BF16))
            lgsm = esF.enter_context(nc.sbuf_tensor("lgsm", [P, 2, 2048], BF16))
            comb = esF.enter_context(nc.sbuf_tensor("comb", [P, 2, NT, 256], BF16))
            si_sb = esF.enter_context(nc.sbuf_tensor("si_sb", [P, 2, 8, 256], BF16))
            so_stg = esF.enter_context(nc.sbuf_tensor("so_stg", [P, 2, 2, D], BF16))
            so_bf = esF.enter_context(nc.sbuf_tensor("so_bf", [P, 2, 2, D], BF16))
            w1buf = [esF.enter_context(nc.sbuf_tensor(f"w1b{i}", [P, 2048], BF16)) for i in range(2)]
            w2buf = [esF.enter_context(nc.sbuf_tensor(f"w2b{i}", [P, 2048], BF16)) for i in range(2)]
            so_st = [esF.enter_context(nc.sbuf_tensor(f"sost{i}", [32, 1024], F32)) for i in range(2)]
            h_bf = esF.enter_context(nc.sbuf_tensor("h_bf", [P, 64], BF16))
            x2f = [esF.enter_context(nc.sbuf_tensor(f"x2f{i}", [P, D], F32)) for i in range(2)]
            t1f = esF.enter_context(nc.sbuf_tensor("t1f", [P, D], F32))
            xu_t = esF.enter_context(nc.sbuf_tensor("xu_t", [P, D], BF16))
            xuf = esF.enter_context(nc.sbuf_tensor("xuf", [P, D], BF16))
            lg_t = esF.enter_context(nc.sbuf_tensor("lg_t", [P, 256], F32))
            expc = esF.enter_context(nc.sbuf_tensor("expc", [P, 256], F32))
            bnF = esF.enter_context(nc.sbuf_tensor("bnF", [P, 12], F32))
            mvF = esF.enter_context(nc.sbuf_tensor("mvF", [P, 2], F32))
            scF = esF.enter_context(nc.sbuf_tensor("scF", [P, 16], F32))
            dspb = esF.enter_context(nc.sbuf_tensor("dspb", [P, 2048], BF16))
            esFd = ExitStack()
            dtm = esFd.enter_context(nc.sbuf_tensor("dtm", [P, NT, 256], BF16))

            ftrk = {"ptrb": 0, "ptrf": 0, "pA": 0, "si": {}, "comb": {}}

            def emit_F1(b):
                for tt in range(NT):
                    par = tt % 2
                    xfree = ftrk.get(f"x2f{par}", 0)

                    def _(q, b=b, tt=tt, par=par, xfree=xfree):
                        if xfree:
                            q.wait_ge(sV.h, xfree)
                        if b == 0 and tt < 2:
                            q.wait_ge(sV.h, v_muu)
                        dXp[par].inc(q.dma_start(out=x2f[par][:], in_=x2_tile_v[b, tt]), 16)
                    Q(_)
                    x_ld = dXp[par].n

                    def _(v, par=par, x_ld=x_ld):
                        v.wait_ge(dXp[par].h, x_ld)
                        sV.inc(v.bn_stats(bnF[:, 0:6], x2f[par][:, 0:512]))
                        sV.inc(v.bn_stats(bnF[:, 6:12], x2f[par][:, 512:1024]))
                        v.wait_ge(sV.h, sV.n)
                        sV.inc(v.bn_aggr(mvF[:], bnF[:]))
                        v.wait_ge(sV.h, sV.n)
                        sV.inc(v.tensor_scalar(scF[:, 0:1], mvF[:, 1:2], EPS, None, op0=ALU.add))
                    V(_)
                    v_u3 = sV.n

                    def _(g, v_u3=v_u3):
                        g.wait_ge(sV.h, v_u3)
                        sG.inc(g.tensor_tensor(scF[:, 2:3], scF[:, 0:1], nh1_s[:], op=ALU.pow))
                    G(_)
                    g_r3 = sG.n

                    def _(v, g_r3=g_r3):
                        v.wait_ge(sG.h, g_r3)
                        sV.inc(v.tensor_scalar(scF[:, 3:4], mvF[:, 1:2], scF[:, 2:3], scF[:, 2:3], op0=ALU.mult, op1=ALU.mult))
                        v.wait_ge(sV.h, sV.n)
                        sV.inc(v.tensor_scalar(scF[:, 4:5], scF[:, 3:4], EPS, None, op0=ALU.add))
                    V(_)
                    v_v4 = sV.n

                    def _(g, v_v4=v_v4):
                        g.wait_ge(sV.h, v_v4)
                        sG.inc(g.tensor_tensor(scF[:, 6:7], scF[:, 4:5], nh1_s[:], op=ALU.pow))
                    G(_)
                    a_r4 = None
                    g_r4 = sG.n

                    p_xuT_prev = ftrk.get("p_xuT", 0)

                    si_mm = ftrk.get("si_mm_done", 0)

                    def _(v, par=par, tt=tt, g_r4=g_r4, p_xuT_prev=p_xuT_prev, si_mm=si_mm):
                        v.wait_ge(sG.h, g_r4)
                        if p_xuT_prev:
                            v.wait_ge(sP.h, p_xuT_prev)
                        sV.inc(v.tensor_tensor(scF[:, 7:8], scF[:, 2:3], scF[:, 6:7], op=ALU.mult))
                        v.wait_ge(sV.h, sV.n)
                        sV.inc(v.tensor_scalar(scF[:, 8:9], mvF[:, 0:1], scF[:, 7:8], -1.0, op0=ALU.mult, op1=ALU.mult))
                        v.wait_ge(sV.h, sV.n)
                        sV.inc(v.tensor_scalar(t1f[:], x2f[par][:], scF[:, 7:8], scF[:, 8:9], op0=ALU.mult, op1=ALU.add))
                        v.wait_ge(sV.h, sV.n)
                        sV.inc(v.tensor_tensor(t1f[:], t1f[:], gam_s[:], op=ALU.mult))
                        v.wait_ge(sV.h, sV.n)
                        if si_mm:
                            v.wait_ge(sP.h, si_mm)
                        sV.inc(v.tensor_tensor(xnn_sb[:, tt, :], t1f[:], bet_s[:], op=ALU.add))
                        v.wait_ge(sV.h, sV.n)
                        sV.inc(v.bn_stats(bnF[:, 0:6], xnn_sb[:, tt, 0:512]))
                        sV.inc(v.bn_stats(bnF[:, 6:12], xnn_sb[:, tt, 512:1024]))
                        v.wait_ge(sV.h, sV.n)
                        sV.inc(v.bn_aggr(mvF[:], bnF[:]))
                        v.wait_ge(sV.h, sV.n)
                        sV.inc(v.tensor_tensor(scF[:, 9:10], mvF[:, 0:1], mvF[:, 0:1], op=ALU.mult))
                        v.wait_ge(sV.h, sV.n)
                        sV.inc(v.tensor_tensor(scF[:, 10:11], scF[:, 9:10], mvF[:, 1:2], op=ALU.add))
                        v.wait_ge(sV.h, sV.n)
                        sV.inc(v.tensor_scalar(scF[:, 11:12], scF[:, 10:11], 1024.0, None, op0=ALU.mult))
                    V(_)
                    v_ss = sV.n

                    def _(g, v_ss=v_ss):
                        g.wait_ge(sV.h, v_ss)
                        sG.inc(g.tensor_tensor(scF[:, 12:13], scF[:, 11:12], ph1_s[:], op=ALU.pow))
                    G(_)
                    a_n5 = None
                    g_n5 = sG.n

                    def _(v, tt=tt, g_n5=g_n5):
                        v.wait_ge(sG.h, g_n5)
                        sV.inc(v.tensor_scalar(scF[:, 13:14], scF[:, 12:13], 1e-12, None, op0=ALU.max))
                        v.wait_ge(sV.h, sV.n)
                        sV.inc(v.reciprocal(scF[:, 14:15], scF[:, 13:14]))
                        v.wait_ge(sV.h, sV.n)
                        sV.inc(v.tensor_tensor(scF[:, 15:16], scF[:, 14:15], scl_s[:], op=ALU.mult))
                        v.wait_ge(sV.h, sV.n)
                        sV.inc(v.tensor_scalar(xu_t[:], xnn_sb[:, tt, :], scF[:, 15:16], None, op0=ALU.mult))
                    V(_)
                    v_xu = sV.n

                    g_xnn = sG.n
                    ftrk["g_xnn"] = g_xnn


                    ptrb_free = ftrk["ptrb"]

                    def _(t, v_xu=v_xu, ptrb_free=ptrb_free):
                        t.wait_ge(sV.h, v_xu)
                        if ptrb_free:
                            t.wait_ge(sA.h, ptrb_free)
                        for kc in range(8):
                            ii = t.transpose(ptrb[:, kc * 128:kc * 128 + 128], xu_t[:, kc * 128:kc * 128 + 128], identb[:])
                        sP.inc(ii)
                    T(_)
                    ftrk["p_xuT"] = sP.n
                    p_xuT = sP.n

                    def _(s, p_xuT=p_xuT):
                        s.wait_ge(sP.h, p_xuT)
                        sA.inc(s.copy(xuf[:], ptrb[:]))
                    A(_)
                    a_xuf = sA.n
                    ftrk["ptrb"] = a_xuf

                    pa_free = ftrk["pA"]

                    def _(t, a_xuf=a_xuf, pa_free=pa_free, v_muu=v_muu):
                        t.wait_ge(sA.h, a_xuf)
                        t.wait_ge(sV.h, v_muu)
                        if pa_free:
                            t.wait_ge(sA.h, pa_free)
                        for kc in range(8):
                            ii = t.matmul(pA[:, 0:256], xuf[:, kc * 128:kc * 128 + 128],
                                          muu[:, kc * 256:kc * 256 + 256],
                                          start=(kc == 0), stop=(kc == 7))
                        sP.inc(ii)
                    T(_)
                    p_lg2 = sP.n

                    def _(s, p_lg2=p_lg2):
                        s.wait_ge(sP.h, p_lg2)
                        sA.inc(s.copy(lg_t[:], pA[:, 0:256]))
                    A(_)
                    a_lg = sA.n
                    ftrk["pA"] = a_lg

                    def _(v, a_lg=a_lg):
                        v.wait_ge(sA.h, a_lg)
                        sV.inc(v.tensor_reduce(scF[:, 0:1], lg_t[:], axis=AX.X, op=ALU.max, negate=True))
                    V(_)
                    v_nmc = sV.n

                    def _(s, v_nmc=v_nmc):
                        s.wait_ge(sV.h, v_nmc)
                        sA.inc(s.activation(expc[:], lg_t[:], AF.Exp, bias=scF[:, 0:1], accum_out=scF[:, 1:2]))
                    A(_)
                    a_expc = sA.n

                    def _(v, b=b, tt=tt, a_expc=a_expc):
                        v.wait_ge(sA.h, a_expc)
                        sV.inc(v.reciprocal(scF[:, 2:3], scF[:, 1:2]))
                        v.wait_ge(sV.h, sV.n)
                        sV.inc(v.tensor_scalar(comb[:, b, tt, :], expc[:], scF[:, 2:3], None, op0=ALU.mult))
                    V(_)
                    ftrk["comb"][b] = sV.n

                    ptrf_free = ftrk["ptrf"]

                    def _(t, a_lg=a_lg, ptrf_free=ptrf_free):
                        t.wait_ge(sA.h, a_lg)
                        if ptrf_free:
                            t.wait_ge(sA.h, ptrf_free)
                        t.transpose(ptrf[:, 0:128], lg_t[:, 0:128], identf[:])
                        sP.inc(t.transpose(ptrf[:, 128:256], lg_t[:, 128:256], identf[:]))
                    T(_)
                    p_lgT = sP.n

                    disp_done = ftrk.get("disp_done_v", 0)

                    def _(s, b=b, tt=tt, p_lgT=p_lgT, disp_done=disp_done):
                        s.wait_ge(sP.h, p_lgT)
                        if disp_done:
                            s.wait_ge(sV.h, disp_done)
                        sA.inc(s.copy(lgsm[:, :, tt * 128:tt * 128 + 128],
                                      ptrf[:, 0:256].rearrange("p (c t) -> p c t", t=128)))
                    A(_)
                    a_lgsm = sA.n
                    ftrk["ptrf"] = a_lgsm
                    ftrk[f"x2f{par}"] = v_ss
                    if tt == 0 and dbg_names:
                        dump(f"xnnf_b{b}", xnn_f[:], sV, v_ss)
                        dump(f"lg_b{b}", lg_t[:], sA, a_lg)
                        dump(f"comb0_b{b}", comb[:, b, 0, :], sV, ftrk["comb"][b])

                ftrk[f"lgsm_done"] = sA.n
                ftrk[f"xnn_done"] = sV.n
                if dbg_names:
                    dump(f"xnn14_a_b{b}", xnn_sb[:, 14, :], sG, sG.n)

                for c in range(2):
                    def _(v, c=c):
                        v.wait_ge(sA.h, ftrk["lgsm_done"])
                        sV.inc(v.tensor_reduce(scF[:, 0:1], lgsm[:, c, :], axis=AX.X, op=ALU.max, negate=True))
                    V(_)
                    v_nmd = sV.n

                    dspb_free = ftrk.get("dspb_free", 0)

                    def _(s, c=c, v_nmd=v_nmd, dspb_free=dspb_free):
                        s.wait_ge(sV.h, v_nmd)
                        if dspb_free:
                            s.wait_ge(sP.h, dspb_free)
                        sA.inc(s.activation(dspb[:], lgsm[:, c, :], AF.Exp, bias=scF[:, 0:1], accum_out=scF[:, 1:2]))
                        s.wait_ge(sA.h, sA.n)
                        sA.inc(s.activation(scF[:, 2:3], scF[:, 1:2], AF.Ln))
                    A(_)
                    a_lnz = sA.n

                    def _(v, a_lnz=a_lnz):
                        v.wait_ge(sA.h, a_lnz)
                        sV.inc(v.tensor_scalar(scF[:, 3:4], scF[:, 2:3], -1.0, scF[:, 0:1], op0=ALU.mult, op1=ALU.add))
                    V(_)
                    v_b2x = sV.n

                    def _(s, c=c, v_b2x=v_b2x):
                        s.wait_ge(sV.h, v_b2x)
                        sA.inc(s.activation(dspb[:], lgsm[:, c, :], AF.Exp, bias=scF[:, 3:4]))
                    A(_)
                    v_dsp = sV.n
                    a_dsp = sA.n

                    for rnd in range(2):
                        ptrb_free = ftrk["ptrb"]

                        def _(t, c=c, rnd=rnd, a_dsp=a_dsp, ptrb_free=ptrb_free):
                            t.wait_ge(sA.h, a_dsp)
                            if ptrb_free:
                                t.wait_ge(sA.h, ptrb_free)
                            for j in range(8):
                                tt = rnd * 8 + j
                                ii = t.transpose(ptrb[:, j * 128:j * 128 + 128],
                                                 dspb[:, tt * 128:tt * 128 + 128], identb[:])
                            sP.inc(ii)
                        T(_)
                        p_dT = sP.n
                        ftrk["dspb_free"] = p_dT
                        si_mm_done = ftrk.get("si_mm_done", 0)

                        def _(s, c=c, rnd=rnd, p_dT=p_dT, si_mm_done=si_mm_done):
                            s.wait_ge(sP.h, p_dT)
                            if si_mm_done:
                                s.wait_ge(sP.h, si_mm_done)
                            sA.inc(s.copy(dtm[:, rnd * 8:rnd * 8 + 8, c * 128:c * 128 + 128],
                                          ptrb[:].rearrange("p (j t) -> p j t", t=128)))
                        A(_)
                        ftrk["ptrb"] = sA.n
                v_dtm = sA.n
                ftrk["disp_done_v"] = sV.n

                prev_si = 0
                for dc in range(8):
                    bank = pB if dc % 2 == 0 else pC

                    def _(t, b=b, dc=dc, bank=bank, v_dtm=v_dtm, prev_si=prev_si):
                        t.wait_ge(sA.h, v_dtm)
                        t.wait_ge(sV.h, ftrk["xnn_done"])
                        if prev_si:
                            t.wait_ge(sA.h, prev_si)
                        for tt in range(NT):
                            ii = t.matmul(bank[:, 0:256], xnn_sb[:, tt, dc * 128:dc * 128 + 128],
                                          dtm[:, tt, :], start=(tt == 0), stop=(tt == NT - 1))
                        sP.inc(ii)
                    T(_)
                    p_si = sP.n

                    def _(s, b=b, dc=dc, bank=bank, p_si=p_si):
                        s.wait_ge(sP.h, p_si)
                        sA.inc(s.copy(si_sb[:, b, dc, :], bank[:, 0:256]))
                    A(_)
                    prev_si = sA.n if dc >= 1 else 0
                    if dc >= 1:
                        prev_si = ftrk["si"].get(dc - 1, 0)
                    ftrk["si"][dc] = sA.n
                ftrk[f"si_done{b}"] = sA.n
                ftrk["si_mm_done"] = sP.n
                if dbg_names:
                    dump(f"xnn14_b_b{b}", xnn_sb[:, 14, :], sP, sP.n)
                if dbg_names:
                    dump(f"si_b{b}", si_sb[:, b], sA, sA.n)
                    dump(f"dtm_b{b}", dtm[:, 0, :], sA, sA.n)
                    dump(f"dtm7_b{b}", dtm[:, 7, :], sA, sA.n)
                    dump(f"dtm8_b{b}", dtm[:, 8, :], sA, sA.n)
                    dump(f"dtm15_b{b}", dtm[:, 15, :], sA, sA.n)

            def emit_experts():
                si_done = max(ftrk["si_done0"], ftrk["si_done1"])
                for e in range(E):
                    par = e % 2
                    wfree = ftrk.get(f"wbuf{par}", 0)

                    def _(g, e=e, par=par, wfree=wfree):
                        if wfree:
                            g.wait_ge(sP.h, wfree)
                        if e < 2:
                            g.wait_ge(sV.h, v_muu)
                        g.dma_start(out=w1buf[par][:], in_=w1_d[e]).then_inc(dE[par].h, 16)
                        g.dma_start(out=w2buf[par][:], in_=w2_d[e]).then_inc(dE[par].h, 16)
                        dE[par].n += 32
                    G(_)
                    w_e = dE[par].n

                    pd_free = ftrk.get("pD_exp", 0)

                    def _(t, e=e, par=par, w_e=w_e, si_done=si_done, pd_free=pd_free):
                        t.wait_ge(dE[par].h, w_e)
                        t.wait_ge(sA.h, si_done)
                        if pd_free:
                            t.wait_ge(sA.h, pd_free)
                        for dc in range(8):
                            for hc in range(2):
                                ii = t.matmul(pD[:, hc * 32:hc * 32 + 32],
                                              w1buf[par][:, dc * 256 + hc * 128:dc * 256 + hc * 128 + 128],
                                              si_sb[:, :, dc, e * 16:e * 16 + 16],
                                              start=(dc == 0 and hc == 0), stop=(dc == 7 and hc == 1),
                                              skip_group_check=True)
                        sP.inc(ii)
                    T(_)
                    p_h = sP.n

                    def _(s, e=e, p_h=p_h):
                        s.wait_ge(sP.h, p_h)
                        sA.inc(s.activation(h_bf[:, 0:32], pD[:, 0:32], AF.Gelu, bias=b1f_s[:, 2 * e:2 * e + 1]))
                        sA.inc(s.activation(h_bf[:, 32:64], pD[:, 32:64], AF.Gelu, bias=b1f_s[:, 2 * e + 1:2 * e + 2]))
                    A(_)
                    a_h = sA.n
                    ftrk["pD_exp"] = a_h

                    pe_free = ftrk.get("pE_exp", 0)

                    def _(t, e=e, par=par, a_h=a_h, pe_free=pe_free):
                        t.wait_ge(sA.h, a_h)
                        if pe_free:
                            t.wait_ge(sA.h, pe_free)
                        for dh in range(2):
                            for hc in range(2):
                                ii = t.matmul(pE[0:32, dh * 512:dh * 512 + 512],
                                              h_bf[:, hc * 32:hc * 32 + 32],
                                              w2buf[par][:, hc * 1024 + dh * 512:hc * 1024 + dh * 512 + 512],
                                              start=(hc == 0), stop=(hc == 1), skip_group_check=True)
                        sP.inc(ii)
                    T(_)
                    p_so = sP.n
                    ftrk[f"wbuf{par}"] = p_so

                    spar = e % 2
                    stfree = ftrk.get(f"sost{spar}", 0)

                    def _(s, e=e, spar=spar, p_so=p_so, stfree=stfree):
                        s.wait_ge(sP.h, p_so)
                        if stfree:
                            s.wait_ge(dSh[spar].h, stfree)
                        sA.inc(s.copy(so_st[spar][0:32, 0:512], pE[0:32, 0:512]))
                        sA.inc(s.copy(so_st[spar][0:32, 512:1024], pE[0:32, 512:1024]))
                    A(_)
                    a_so = sA.n
                    ftrk["pE_exp"] = a_so

                    def _(g, e=e, spar=spar, a_so=a_so):
                        g.wait_ge(sA.h, a_so)
                        r0 = (e % 8) * 16
                        g.dma_start(out=so_stg[r0:r0 + 16, 0, e // 8, :], in_=so_st[spar][0:16, :]).then_inc(dSh[spar].h, 16)
                        g.dma_start(out=so_stg[r0:r0 + 16, 1, e // 8, :], in_=so_st[spar][16:32, :]).then_inc(dSh[spar].h, 16)
                        dSh[spar].n += 32
                    G(_)
                    ftrk[f"sost{spar}"] = dSh[spar].n

                def _(v):
                    v.wait_ge(dSh[0].h, dSh[0].n)
                    v.wait_ge(dSh[1].h, dSh[1].n)
                    for bb in range(2):
                        for c in range(2):
                            sV.inc(v.tensor_tensor(so_bf[:, bb, c, :], so_stg[:, bb, c, :],
                                                   b2r_s[:, c, :], op=ALU.add))
                V(_)
                ftrk["so_ready"] = sV.n
                if dbg_names:
                    dump("sobf_b0", so_bf[:, 0], sV, sV.n)
                    dump("sobf_b1", so_bf[:, 1], sV, sV.n)

            def emit_F3(b):
                for tt in range(NT):
                    par = tt % 2
                    xfree = ftrk.get(f"x2r{par}", 0) or ftrk.get(f"x2f{par}", 0)

                    def _(q, b=b, tt=tt, par=par, xfree=xfree):
                        if xfree:
                            q.wait_ge(sV.h, xfree)
                        dXp[par].inc(q.dma_start(out=x2f[par][:], in_=x2_tile_v[b, tt]), 16)
                    Q(_)
                    x_ld = dXp[par].n

                    ptrb_free = ftrk["ptrb"]

                    def _(t, b=b, tt=tt, ptrb_free=ptrb_free):
                        t.wait_ge(sV.h, ftrk["comb"][b])
                        if ptrb_free:
                            t.wait_ge(sA.h, ptrb_free)
                        t.transpose(ptrb[:, 0:128], comb[:, b, tt, 0:128], identb[:])
                        sP.inc(t.transpose(ptrb[:, 128:256], comb[:, b, tt, 128:256], identb[:]))
                    T(_)
                    p_cT = sP.n

                    def _(s, p_cT=p_cT):
                        s.wait_ge(sP.h, p_cT)
                        sA.inc(s.copy(cb_sm[:], ptrb[:, 0:256]))
                    A(_)
                    a_cb = sA.n
                    ftrk["ptrb"] = a_cb

                    pbc_free = ftrk.get("pBC_F3", 0)

                    def _(t, b=b, a_cb=a_cb, pbc_free=pbc_free):
                        t.wait_ge(sA.h, a_cb)
                        t.wait_ge(sV.h, ftrk["so_ready"])
                        if pbc_free:
                            t.wait_ge(sV.h, pbc_free)
                        for dh in range(2):
                            bank = pB if dh == 0 else pC
                            for c in range(2):
                                ii = t.matmul(bank[:, 0:512], cb_sm[:, c * 128:c * 128 + 128],
                                              so_bf[:, b, c, dh * 512:dh * 512 + 512],
                                              start=(c == 0), stop=(c == 1), skip_group_check=True)
                        sP.inc(ii)
                    T(_)
                    p_o = sP.n

                    def _(v, p_o=p_o):
                        v.wait_ge(sP.h, p_o)
                        sV.inc(v.bn_stats(bnF[:, 0:6], pB[:, 0:512]))
                        sV.inc(v.bn_stats(bnF[:, 6:12], pC[:, 0:512]))
                        v.wait_ge(sV.h, sV.n)
                        sV.inc(v.bn_aggr(mvF[:], bnF[:]))
                        v.wait_ge(sV.h, sV.n)
                        sV.inc(v.tensor_scalar(scF[:, 0:1], mvF[:, 1:2], EPS, None, op0=ALU.add))
                    V(_)
                    v_u6 = sV.n

                    def _(g, v_u6=v_u6):
                        g.wait_ge(sV.h, v_u6)
                        sG.inc(g.tensor_tensor(scF[:, 3:4], scF[:, 0:1], nh1_s[:], op=ALU.pow))
                    G(_)
                    a_r6 = None
                    g_r6 = sG.n

                    ofree = ftrk.get(f"ostore{par}", 0)

                    def _(v, b=b, tt=tt, par=par, g_r6=g_r6, x_ld=x_ld, ofree=ofree):
                        v.wait_ge(sG.h, g_r6)
                        v.wait_ge(dXp[par].h, x_ld)
                        if ofree:
                            v.wait_ge(dOp[par].h, ofree)
                        sV.inc(v.tensor_scalar(scF[:, 4:5], mvF[:, 0:1], scF[:, 3:4], -1.0, op0=ALU.mult, op1=ALU.mult))
                        v.wait_ge(sV.h, sV.n)
                        sV.inc(v.tensor_scalar(ln5_t[:, 0:512], pB[:, 0:512], scF[:, 3:4], scF[:, 4:5], op0=ALU.mult, op1=ALU.add))
                        sV.inc(v.tensor_scalar(ln5_t[:, 512:1024], pC[:, 0:512], scF[:, 3:4], scF[:, 4:5], op0=ALU.mult, op1=ALU.add))
                        v.wait_ge(sV.h, sV.n)
                        sV.inc(v.tensor_tensor(out_t[par][:], x2f[par][:], ln5_t[:], op=ALU.add))
                    V(_)
                    v_out = sV.n
                    ftrk["pBC_F3"] = v_out
                    ftrk[f"x2r{par}"] = v_out

                    def _(q, b=b, tt=tt, par=par, v_out=v_out):
                        q.wait_ge(sV.h, v_out)
                        dOp[par].inc(q.dma_start(out=out_tile_v[b, tt], in_=out_t[par][:]), 16)
                    Q(_)
                    ftrk[f"ostore{par}"] = dOp[par].n

            emit_F1(0)
            emit_F1(1)
            esFd.close()
            cb_sm = esF.enter_context(nc.sbuf_tensor("cb_sm", [P, 256], BF16))
            ln5_t = esF.enter_context(nc.sbuf_tensor("ln5_t", [P, D], F32))
            out_t = [esF.enter_context(nc.sbuf_tensor(f"outt{i}", [P, D], F32)) for i in range(2)]
            emit_experts()
            emit_F3(0)
            emit_F3(1)
            esF.close()

        x2_done = (dSp[0].n, dSp[1].n)

        def _(q):
            if x2_done[0]:
                q.wait_ge(dSp[0].h, x2_done[0])
            if x2_done[1]:
                q.wait_ge(dSp[1].h, x2_done[1])
            if "B" in phases and "F" not in phases:
                for b in range(2):
                    dO.inc(q.dma_start(out=out_d[b], in_=x2scr[b]), 16)
            if dO.n:
                q.wait_ge(dO.h, dO.n)
            if dOp[0].n:
                q.wait_ge(dOp[0].h, dOp[0].n)
            if dOp[1].n:
                q.wait_ge(dOp[1].h, dOp[1].n)
            if dDbg.n:
                q.wait_ge(dDbg.h, dDbg.n)
        Q(_)

    return nc, dbg_outs


def _host_prep(inp):
    f = np.float32
    Wq, Wk, Wv, Wo = (np.asarray(inp[k], f) for k in ("L0_attn_Wq", "L0_attn_Wk", "L0_attn_Wv", "L0_attn_Wo"))

    def wlay(W):
        return np.ascontiguousarray(
            W.T.reshape(8, 128, 8, 128).transpose(1, 0, 2, 3).reshape(128, 8192))

    planes_T = np.asarray(inp["L0_attn_planes_T"], f)
    negA = np.zeros((64, 1024), f)
    for j in range(64):
        negA[j, j * 16:(j + 1) * 16] = -1.0
    negB = np.zeros((8, 1024), f)
    for j in range(8):
        negB[j, j * 128:(j + 1) * 128] = -1.0
    mu = np.asarray(inp["L0_ffn_mu"], f).reshape(1024, 256)
    w1 = np.asarray(inp["L0_ffn_w1"], f)
    w2 = np.asarray(inp["L0_ffn_w2"], f)
    b1 = np.asarray(inp["L0_ffn_b1"], f)
    com = {
        "wq": wlay(Wq), "wk": wlay(Wk), "wv": wlay(Wv), "wo": wlay(Wo),
        "bo": np.asarray(inp["L0_attn_bo"], f).reshape(1, 1024),
        "planes2": np.concatenate([planes_T, planes_T], 0),
        "protos": np.asarray(inp["L0_attn_protos_T"], f),
        "ltemp": np.asarray(inp["L0_attn_logit_temp"], f).reshape(1, 1),
        "negindA": negA, "negindB": negB,
        "ident": np.eye(128, dtype=f),
        "ones1": np.ones((1, 128), f),
        "onescol": np.ones((128, 1), f),
        "nh8": np.full((128, 8), -0.5, f),
        "nh1": np.full((128, 1), -0.5, f),
        "ph1": np.full((128, 1), 0.5, f),
        "mul": np.ascontiguousarray(mu.reshape(8, 128, 256).transpose(1, 0, 2).reshape(128, 2048)),
        "gam": np.ascontiguousarray(np.broadcast_to(np.asarray(inp["L0_ffn_ln_gamma"], f), (128, 1024))),
        "bet": np.ascontiguousarray(np.broadcast_to(np.asarray(inp["L0_ffn_ln_beta"], f), (128, 1024))),
        "scl": np.ascontiguousarray(np.broadcast_to(np.asarray(inp["L0_ffn_scale"], f).reshape(1, 1), (128, 1))),
        "w1l": np.ascontiguousarray(w1.reshape(E, 8, 128, 2, 128).transpose(0, 2, 1, 3, 4).reshape(E, 128, 2048)),
        "w2l": np.ascontiguousarray(w2.reshape(E, 2, 128, 2, 512).transpose(0, 2, 1, 3, 4).reshape(E, 128, 2048)),
        "b1f": np.ascontiguousarray(b1.reshape(16, 2, 128).transpose(2, 0, 1).reshape(128, 32)),
        "b2r": np.ascontiguousarray(np.repeat(np.asarray(inp["L0_ffn_b2"], f), 16, axis=0)),
    }
    return com


def kernel(**inputs):
    x = np.ascontiguousarray(np.asarray(inputs["L0_attn_x"], np.float32))
    com = _host_prep(inputs)
    nc, _ = build_program(DBG_NAMES)
    in_maps = []
    for c in range(8):
        m = dict(com)
        m["x"] = np.ascontiguousarray(x[2 * c:2 * c + 2])
        in_maps.append(m)
    res = run_bass_kernel_spmd(nc, in_maps, core_ids=list(range(8)))
    return np.concatenate([r["out"] for r in res.results], axis=0)


# version 03
# speedup vs baseline: 1.2463x; 1.1904x over previous
import numpy as np
from contextlib import ExitStack

import concourse.bass as bass
import concourse.mybir as mybir
from concourse.bass_utils import run_bass_kernel_spmd

F32 = mybir.dt.float32
BF16 = mybir.dt.bfloat16
AF = mybir.ActivationFunctionType
ALU = mybir.AluOpType
AX = mybir.AxisListType

P = 128
D = 1024
H = 16
HD = 64
NT = 16
HALF = 8
SB = 128
LTAB = 8
RB = 16
E = 16
ES = 256
HFF = 256
EPS = 1e-5

DBG_NAMES = ()
A_CFG = dict(bs=(0, 1), halves=(0, 1), tiles=8, buckets=True, stages=99, sub9=99)


class Sm:
    def __init__(self, es, nc, name):
        self.h = es.enter_context(nc.semaphore(name))
        self.n = 0

    def inc(self, instr, k=1):
        instr.then_inc(self.h, k)
        self.n += k
        return self.n


def build_program(dbg_names=(), phases=("A", "B", "F")):
    nc = bass.Bass("TRN2", target_bir_lowering=False, detect_race_conditions=False)
    dbg_outs = {}

    x_d = nc.dram_tensor("x", [2, 2048, D], F32, kind="ExternalInput").ap()
    wq_d = nc.dram_tensor("wq", [128, 8192], F32, kind="ExternalInput").ap()
    wk_d = nc.dram_tensor("wk", [128, 8192], F32, kind="ExternalInput").ap()
    wv_d = nc.dram_tensor("wv", [128, 8192], F32, kind="ExternalInput").ap()
    wo_d = nc.dram_tensor("wo", [128, 8192], F32, kind="ExternalInput").ap()
    bo_d = nc.dram_tensor("bo", [1, D], F32, kind="ExternalInput").ap()
    planes_d = nc.dram_tensor("planes2", [128, 32], F32, kind="ExternalInput").ap()
    protos_d = nc.dram_tensor("protos", [4, 16], F32, kind="ExternalInput").ap()
    ltemp_d = nc.dram_tensor("ltemp", [1, 1], F32, kind="ExternalInput").ap()
    negA_d = nc.dram_tensor("negindA", [64, 1024], F32, kind="ExternalInput").ap()
    negB_d = nc.dram_tensor("negindB", [8, 1024], F32, kind="ExternalInput").ap()
    ident_d = nc.dram_tensor("ident", [128, 128], F32, kind="ExternalInput").ap()
    ones1_d = nc.dram_tensor("ones1", [1, 128], F32, kind="ExternalInput").ap()
    onescol_d = nc.dram_tensor("onescol", [128, 1], F32, kind="ExternalInput").ap()
    nh8_d = nc.dram_tensor("nh8", [128, 8], F32, kind="ExternalInput").ap()
    nh1_d = nc.dram_tensor("nh1", [128, 1], F32, kind="ExternalInput").ap()
    ph1_d = nc.dram_tensor("ph1", [128, 1], F32, kind="ExternalInput").ap()
    mu_d = nc.dram_tensor("mul", [128, 2048], F32, kind="ExternalInput").ap()
    gam_d = nc.dram_tensor("gam", [128, D], F32, kind="ExternalInput").ap()
    bet_d = nc.dram_tensor("bet", [128, D], F32, kind="ExternalInput").ap()
    scl_d = nc.dram_tensor("scl", [128, 1], F32, kind="ExternalInput").ap()
    w1_d = nc.dram_tensor("w1l", [E, 128, 2048], F32, kind="ExternalInput").ap()
    w2_d = nc.dram_tensor("w2l", [E, 128, 2048], F32, kind="ExternalInput").ap()
    b1_d = nc.dram_tensor("b1f", [128, 32], F32, kind="ExternalInput").ap()
    b2_d = nc.dram_tensor("b2r", [256, D], F32, kind="ExternalInput").ap()

    out_d = nc.dram_tensor("out", [2, 2048, D], F32, kind="ExternalOutput").ap()
    x2scr = nc.dram_tensor("x2scr", [2, 2048, D], F32, kind="Internal").ap()

    x_half_v = x_d.rearrange("b (hh t p) d -> b hh p t d", hh=2, t=HALF, p=P)
    x_tile_v = x_d.rearrange("b (t p) d -> b t p d", t=NT, p=P)
    x2_tile_v = x2scr.rearrange("b (t p) d -> b t p d", t=NT, p=P)
    out_tile_v = out_d.rearrange("b (t p) d -> b t p d", t=NT, p=P)

    es = ExitStack()
    with es:
        sP = Sm(es, nc, "sP")
        sV = Sm(es, nc, "sV")
        sA = Sm(es, nc, "sA")
        sG = Sm(es, nc, "sG")
        dXA = Sm(es, nc, "dXA")
        dXp = [Sm(es, nc, "dX0"), Sm(es, nc, "dX1")]
        dCon = Sm(es, nc, "dCon")
        dPro = Sm(es, nc, "dPro")
        dWkv = Sm(es, nc, "dWkv")
        dWqo = Sm(es, nc, "dWqo")
        dE = [Sm(es, nc, "dE0"), Sm(es, nc, "dE1")]
        dSp = [Sm(es, nc, "dS0"), Sm(es, nc, "dS1")]
        dSh = [Sm(es, nc, "dSh0"), Sm(es, nc, "dSh1")]
        dO = Sm(es, nc, "dO")
        dOp = [Sm(es, nc, "dO0"), Sm(es, nc, "dO1")]
        dDbg = Sm(es, nc, "dDbg")

        ptrb = es.enter_context(nc.psum_tensor("ptrb", [P, 1024], BF16))
        ptrf = es.enter_context(nc.psum_tensor("ptrf", [P, 256], F32))
        pA = es.enter_context(nc.psum_tensor("pA", [P, 512], F32))
        pB = es.enter_context(nc.psum_tensor("pB", [P, 512], F32))
        pC = es.enter_context(nc.psum_tensor("pC", [P, 512], F32))
        pD = es.enter_context(nc.psum_tensor("pD", [P, 512], F32))
        pE = es.enter_context(nc.psum_tensor("pE", [P, 1024], F32))

        identb = es.enter_context(nc.sbuf_tensor("identb", [128, 128], BF16))
        identf = es.enter_context(nc.sbuf_tensor("identf", [128, 128], F32))
        onesb = es.enter_context(nc.sbuf_tensor("onesb", [1, 128], BF16))
        ones1f = es.enter_context(nc.sbuf_tensor("ones1f", [1, 128], F32))
        onescolb = es.enter_context(nc.sbuf_tensor("onescolb", [128, 1], BF16))
        negA = es.enter_context(nc.sbuf_tensor("negA", [64, 1024], BF16))
        negB = es.enter_context(nc.sbuf_tensor("negB", [8, 1024], BF16))
        planes = es.enter_context(nc.sbuf_tensor("planes", [128, 32], BF16))
        PbigF = es.enter_context(nc.sbuf_tensor("PbigF", [32, 128], F32))
        PbigS = es.enter_context(nc.sbuf_tensor("PbigS", [128, 128], BF16))
        bo_s = es.enter_context(nc.sbuf_tensor("bo_s", [1, D], BF16))
        lt_s = es.enter_context(nc.sbuf_tensor("lt_s", [1, 8], F32))
        nh8_s = es.enter_context(nc.sbuf_tensor("nh8_s", [128, 8], F32))
        nh1_s = es.enter_context(nc.sbuf_tensor("nh1_s", [128, 1], F32))
        ph1_s = es.enter_context(nc.sbuf_tensor("ph1_s", [128, 1], F32))
        invs32 = es.enter_context(nc.sbuf_tensor("invs32", [32, 1], F32))

        block = es.enter_context(nc.Block())
        T = block.tensor
        V = block.vector
        A = block.scalar
        G = block.gpsimd
        Q = block.sync

        def dump(name, ap, sem, val):
            if name not in dbg_names:
                return
            dd = nc.dram_tensor("dbg_" + name, list(ap.shape), F32, kind="ExternalOutput").ap()
            dbg_outs[name] = dd

            def _(g):
                g.wait_ge(sem.h, val)
                dDbg.inc(g.dma_start(out=dd, in_=ap), 16)
            G(_)

        def _(g):
            g.dma_start(out=identb[:], in_=ident_d).then_inc(dCon.h, 16)
            g.dma_start(out=identf[:], in_=ident_d).then_inc(dCon.h, 16)
            g.dma_start(out=onesb[:], in_=ones1_d).then_inc(dCon.h, 16)
            g.dma_start(out=ones1f[:], in_=ones1_d).then_inc(dCon.h, 16)
            g.dma_start(out=onescolb[:], in_=onescol_d).then_inc(dCon.h, 16)
            g.dma_start(out=negA[:], in_=negA_d).then_inc(dCon.h, 16)
            g.dma_start(out=negB[:], in_=negB_d).then_inc(dCon.h, 16)
            g.dma_start(out=planes[:], in_=planes_d).then_inc(dCon.h, 16)
            g.dma_start(out=bo_s[:], in_=bo_d).then_inc(dCon.h, 16)
            g.dma_start(out=lt_s[0:1, 0:1], in_=ltemp_d).then_inc(dCon.h, 16)
            g.dma_start(out=nh8_s[:], in_=nh8_d).then_inc(dCon.h, 16)
            g.dma_start(out=nh1_s[:], in_=nh1_d).then_inc(dCon.h, 16)
            g.dma_start(out=ph1_s[:], in_=ph1_d).then_inc(dCon.h, 16)
            dCon.n += 208
        G(_)
        w_pre = dCon.n

        def _(v):
            sV.inc(v.memset(PbigF[:], 0.0))
        V(_)
        v_pz = sV.n

        def _(g):
            g.wait_ge(sV.h, v_pz)
            for l in range(LTAB):
                g.dma_start(out=PbigF[4 * l:4 * l + 4, 16 * l:16 * l + 16], in_=protos_d).then_inc(dPro.h, 16)
            dPro.n += 16 * LTAB
        G(_)
        w_pbig = dPro.n

        def _(s):
            s.wait_ge(dCon.h, w_pre)
            sA.inc(s.activation(lt_s[0:1, 1:2], lt_s[0:1, 0:1], AF.Exp))
        A(_)
        a_e = sA.n

        def _(v):
            v.wait_ge(sA.h, a_e)
            sV.inc(v.tensor_scalar(lt_s[0:1, 2:3], lt_s[0:1, 1:2], 20.0, 0.01, op0=ALU.min, op1=ALU.max))
            v.wait_ge(sV.h, sV.n)
            sV.inc(v.reciprocal(lt_s[0:1, 3:4], lt_s[0:1, 2:3]))
        V(_)
        v_invs = sV.n

        def _(t):
            t.wait_ge(sV.h, v_invs)
            sP.inc(t.matmul(pD[0:32, 0:1], ones1f[0:1, 0:32], lt_s[0:1, 3:4], start=True, stop=True))
        T(_)
        p_inv = sP.n

        def _(s):
            s.wait_ge(sP.h, p_inv)
            sA.inc(s.copy(invs32[:], pD[0:32, 0:1]))
        A(_)
        a_inv = sA.n

        def _(v):
            v.wait_ge(sA.h, a_inv)
            v.wait_ge(dPro.h, w_pbig)
            sV.inc(v.tensor_scalar(PbigS[0:32, :], PbigF[:], invs32[:], None, op0=ALU.mult))
            sV.inc(v.tensor_scalar(PbigS[64:96, :], PbigF[:], invs32[:], None, op0=ALU.mult))
        V(_)
        v_pbig = sV.n
        dump("PbigS", PbigS[0:32, :], sV, v_pbig)

        wq_s = es.enter_context(nc.sbuf_tensor("wq_s", [128, 8192], BF16))
        wo_s = es.enter_context(nc.sbuf_tensor("wo_s", [128, 8192], BF16))

        def _(g):
            g.dma_start(out=wq_s[:], in_=wq_d).then_inc(dWqo.h, 16)
            g.dma_start(out=wo_s[:], in_=wo_d).then_inc(dWqo.h, 16)
            dWqo.n += 32
        G(_)

        stats = es.enter_context(nc.sbuf_tensor("stats", [P, 64], F32))
        negm1 = es.enter_context(nc.sbuf_tensor("negm1", [P, 32], F32))
        r1s = es.enter_context(nc.sbuf_tensor("r1s", [P, 32], F32))
        xn1 = es.enter_context(nc.sbuf_tensor("xn1", [P, D], BF16))
        xn1f = es.enter_context(nc.sbuf_tensor("xn1f", [P, D], BF16))

        esAB = es.enter_context(ExitStack())
        bKTa = [esAB.enter_context(nc.sbuf_tensor(f"bKTa{b}", [64, 2048], F32)) for b in range(2)]
        bVa = [esAB.enter_context(nc.sbuf_tensor(f"bVa{b}", [P, 1024], F32)) for b in range(2)]

        esA = ExitStack()
        wk_s = esA.enter_context(nc.sbuf_tensor("wk_s", [128, 8192], BF16))
        wv_s = esA.enter_context(nc.sbuf_tensor("wv_s", [128, 8192], BF16))
        x_sb = esA.enter_context(nc.sbuf_tensor("x_sb", [P, HALF, D], F32))
        Ktm = esA.enter_context(nc.sbuf_tensor("Ktm", [P, HALF, D], BF16))
        Vtm = esA.enter_context(nc.sbuf_tensor("Vtm", [P, HALF, D], BF16))
        probs = esA.enter_context(nc.sbuf_tensor("probs", [P, HALF, 2048], BF16))
        bnbuf = esA.enter_context(nc.sbuf_tensor("bnbuf", [P, 96], F32))
        tsb = esA.enter_context(nc.sbuf_tensor("tsb", [P, 8], F32))
        tsb2 = esA.enter_context(nc.sbuf_tensor("tsb2", [P, 8], F32))
        Kfm = esA.enter_context(nc.sbuf_tensor("Kfm", [P, D], BF16))
        tanhp = esA.enter_context(nc.sbuf_tensor("tanhp", [P, D], BF16))
        nmaxA = esA.enter_context(nc.sbuf_tensor("nmaxA", [P, 64], F32))
        cTA = esA.enter_context(nc.sbuf_tensor("cTA", [64, 128], BF16))
        expA = esA.enter_context(nc.sbuf_tensor("expA", [P, 1024], F32))
        ZA = esA.enter_context(nc.sbuf_tensor("ZA", [P, 64], F32))
        RA = esA.enter_context(nc.sbuf_tensor("RA", [P, 64], F32))

        def _(g):
            g.dma_start(out=wk_s[:], in_=wk_d).then_inc(dWkv.h, 16)
            g.dma_start(out=wv_s[:], in_=wv_d).then_inc(dWkv.h, 16)
            dWkv.n += 32
        G(_)
        w_attn = dWkv.n

        trk = {
            "x_free": 0,
            "ptrb_free": 0,
            "ptrf_free": 0,
            "pAB_free_pe": 0,
            "pCD_free": 0,
            "pE_free": 0,
            "prod_free_pe": 0,
            "bk_ready": 0,
            "x2_store": 0,
        }

        def emit_A(b):
            for half in A_CFG["halves"]:
                g0 = half * HALF
                x_free = trk["x_free"]
                prod_free = trk["prod_free_pe"]

                def _(q):
                    if x_free:
                        q.wait_ge(sV.h, x_free)
                    dXA.inc(q.dma_start(out=x_sb[:], in_=x_half_v[b, half]), 16)
                Q(_)
                x_ld = dXA.n

                def _(v):
                    v.wait_ge(dXA.h, x_ld)
                    for tt in range(HALF):
                        sV.inc(v.bn_stats(bnbuf[:, tt * 12:tt * 12 + 6], x_sb[:, tt, 0:512]))
                        sV.inc(v.bn_stats(bnbuf[:, tt * 12 + 6:tt * 12 + 12], x_sb[:, tt, 512:1024]))
                    v.wait_ge(sV.h, sV.n)
                    for tt in range(HALF):
                        col = b * 32 + (g0 + tt) * 2
                        sV.inc(v.bn_aggr(stats[:, col:col + 2], bnbuf[:, tt * 12:tt * 12 + 12]))
                    v.wait_ge(sV.h, sV.n)
                    vsl = stats[:, b * 32 + g0 * 2 + 1: b * 32 + (g0 + HALF) * 2: 2]
                    msl = stats[:, b * 32 + g0 * 2: b * 32 + (g0 + HALF) * 2: 2]
                    sV.inc(v.tensor_scalar(tsb[:], vsl, EPS, None, op0=ALU.add))
                    sV.inc(v.tensor_scalar(negm1[:, b * 16 + g0: b * 16 + g0 + HALF], msl, -1.0, None, op0=ALU.mult))
                V(_)
                v_stats = sV.n
                dump(f"xsb_b{b}h{half}", x_sb[:, 0, :], dXA, x_ld)
                dump(f"stats_b{b}h{half}", stats[:, b * 32 + g0 * 2: b * 32 + (g0 + HALF) * 2], sV, v_stats)

                def _(g):
                    g.wait_ge(sV.h, v_stats)
                    g.wait_ge(dCon.h, 208)
                    sG.inc(g.tensor_tensor(r1s[:, b * 16 + g0: b * 16 + g0 + HALF], tsb[:], nh8_s[:], op=ALU.pow))
                G(_)
                g_r1 = sG.n
                dump(f"r1s_b{b}h{half}", r1s[:, b * 16 + g0:b * 16 + g0 + HALF], sG, g_r1)

                for tt in range(A_CFG["tiles"]):
                    gt = g0 + tt
                    ncol = b * 16 + gt

                    p_xn1T_prev = trk.get("p_xn1T", 0)

                    def _(v, tt=tt, ncol=ncol, p_xn1T_prev=p_xn1T_prev, g_r1=g_r1):
                        v.wait_ge(sG.h, g_r1)
                        if p_xn1T_prev:
                            v.wait_ge(sP.h, p_xn1T_prev)
                        sV.inc(v.tensor_scalar(xn1[:], x_sb[:, tt, :], negm1[:, ncol:ncol + 1],
                                               r1s[:, ncol:ncol + 1], op0=ALU.add, op1=ALU.mult))
                    V(_)
                    v_xn1 = sV.n

                    if A_CFG['stages'] < 2:
                        continue
                    ptrb_free = trk["ptrb_free"]

                    def _(t, v_xn1=v_xn1, ptrb_free=ptrb_free, prod_free=prod_free):
                        t.wait_ge(sV.h, v_xn1)
                        t.wait_ge(dCon.h, 208)
                        t.wait_ge(dWkv.h, 32)
                        if ptrb_free:
                            t.wait_ge(sA.h, ptrb_free)
                        for kc in range(8):
                            ii = t.transpose(ptrb[:, kc * 128:kc * 128 + 128], xn1[:, kc * 128:kc * 128 + 128], identb[:])
                        sP.inc(ii)
                    T(_)
                    trk["p_xn1T"] = sP.n
                    p_xn1T = sP.n

                    if A_CFG['stages'] < 3:
                        continue
                    def _(s, p_xn1T=p_xn1T):
                        s.wait_ge(sP.h, p_xn1T)
                        sA.inc(s.copy(xn1f[:], ptrb[:]))
                    A(_)
                    a_xn1f = sA.n

                    if A_CFG['stages'] < 4:
                        continue
                    pab_free = trk["pAB_free_pe"]

                    def _(t, a_xn1f=a_xn1f, pab_free=pab_free):
                        t.wait_ge(sA.h, a_xn1f)
                        if pab_free:
                            t.wait_ge(sA.h, pab_free)
                        for kc in range(8):
                            for oc in range(8):
                                bank = pA if oc < 4 else pB
                                ii = t.matmul(bank[:, (oc % 4) * 128:(oc % 4) * 128 + 128],
                                              wk_s[:, kc * 1024 + oc * 128:kc * 1024 + oc * 128 + 128],
                                              xn1f[:, kc * 128:kc * 128 + 128],
                                              start=(kc == 0 and oc % 4 == 0),
                                              stop=(kc == 7 and oc % 4 == 3),
                                              skip_group_check=True)
                        sP.inc(ii)
                    T(_)
                    p_kfm = sP.n

                    if A_CFG['stages'] < 5:
                        continue
                    def _(s, p_kfm=p_kfm):
                        s.wait_ge(sP.h, p_kfm)
                        sA.inc(s.copy(Kfm[:, 0:512], pA[:]))
                        sA.inc(s.copy(Kfm[:, 512:1024], pB[:]))
                    A(_)
                    a_kfm = sA.n

                    if A_CFG['stages'] < 6:
                        continue
                    def _(t, a_kfm=a_kfm):
                        t.wait_ge(sA.h, a_kfm)
                        for oc in range(8):
                            ii = t.transpose(ptrb[:, oc * 128:oc * 128 + 128], Kfm[:, oc * 128:oc * 128 + 128], identb[:])
                        sP.inc(ii)
                    T(_)
                    p_ktT = sP.n

                    def _(s, p_ktT=p_ktT, tt=tt):
                        s.wait_ge(sP.h, p_ktT)
                        sA.inc(s.copy(Ktm[:, tt, :], ptrb[:]))
                    A(_)
                    a_ktm = sA.n
                    trk["ptrb_free"] = a_ktm

                    if A_CFG['stages'] < 7:
                        continue
                    def _(t, a_kfm=a_kfm, v_pbig=v_pbig):
                        t.wait_ge(sV.h, v_pbig)
                        t.wait_ge(sA.h, a_kfm)
                        for h in range(H):
                            bank = pA if h < 8 else pB
                            base = 64 * (h % 2)
                            cs = (h % 8) // 2
                            ii = t.matmul(bank[base:base + 32, cs * 128:cs * 128 + 128],
                                          planes[base:base + 64, :],
                                          Kfm[base:base + 64, (h // 2) * 128:(h // 2) * 128 + 128],
                                          start=(h % 8 in (0, 1)), stop=(h % 8 in (6, 7)),
                                          skip_group_check=True)
                        sP.inc(ii)
                    T(_)
                    p_pj = sP.n

                    def _(s, p_pj=p_pj):
                        s.wait_ge(sP.h, p_pj)
                        sA.inc(s.activation(tanhp[:, 0:512], pA[:], AF.Tanh))
                        sA.inc(s.activation(tanhp[:, 512:1024], pB[:], AF.Tanh))
                    A(_)
                    a_tanh = sA.n
                    trk["pAB_free_pe"] = a_tanh

                    if A_CFG['stages'] < 8:
                        continue
                    pcd_free = trk["pCD_free"]

                    def _(t, a_xn1f=a_xn1f, pcd_free=pcd_free):
                        t.wait_ge(sA.h, a_xn1f)
                        if pcd_free:
                            t.wait_ge(sA.h, pcd_free)
                        for dh in range(2):
                            bank = pC if dh == 0 else pD
                            for kc in range(8):
                                ii = t.matmul(bank[:, 0:512],
                                              xn1f[:, kc * 128:kc * 128 + 128],
                                              wv_s[:, kc * 1024 + dh * 512:kc * 1024 + dh * 512 + 512],
                                              start=(kc == 0), stop=(kc == 7))
                        sP.inc(ii)
                    T(_)
                    p_v = sP.n

                    def _(s, p_v=p_v, tt=tt):
                        s.wait_ge(sP.h, p_v)
                        sA.inc(s.copy(Vtm[:, tt, 0:512], pC[:]))
                        sA.inc(s.copy(Vtm[:, tt, 512:1024], pD[:]))
                    A(_)
                    a_v = sA.n
                    trk["pCD_free"] = a_v

                    if A_CFG['stages'] < 9:
                        continue
                    for grp in range(2):
                        pE_free = trk["pE_free"]

                        def _(t, grp=grp, a_tanh=a_tanh, pE_free=pE_free):
                            t.wait_ge(sA.h, a_tanh)
                            if pE_free:
                                t.wait_ge(sV.h, pE_free)
                            for hl in range(8):
                                h = grp * 8 + hl
                                base = 64 * (h % 2)
                                col = (h // 8) * 512 + ((h % 8) // 2) * 128
                                oc = (hl // 2) * 128 + (hl % 2) * 512
                                ii = t.matmul(pE[:, oc:oc + 128],
                                              tanhp[base:base + 32, col:col + 128],
                                              PbigS[base:base + 32, :],
                                              start=(hl in (0, 1)), stop=(hl in (6, 7)),
                                              skip_group_check=True)
                            sP.inc(ii)
                        T(_)
                        p_lg = sP.n

                        if A_CFG['sub9'] < 2:
                            continue
                        def _(v, p_lg=p_lg):
                            v.wait_ge(sP.h, p_lg)
                            sV.inc(v.tensor_reduce(nmaxA[:], pE[:].rearrange("p (s r) -> p s r", r=RB),
                                                   axis=AX.X, op=ALU.max, negate=False))
                        V(_)
                        v_nm = sV.n

                        if A_CFG['sub9'] < 3:
                            continue
                        ptrf_free = trk["ptrf_free"]

                        def _(t, v_nm=v_nm, ptrf_free=ptrf_free):
                            t.wait_ge(sV.h, v_nm)
                            if ptrf_free:
                                t.wait_ge(sA.h, ptrf_free)
                            sP.inc(t.transpose(ptrf[0:64, 0:128], nmaxA[:], identf[:]))
                        T(_)
                        p_nt = sP.n

                        if A_CFG['sub9'] < 4:
                            continue
                        def _(s, p_nt=p_nt):
                            s.wait_ge(sP.h, p_nt)
                            sA.inc(s.copy(cTA[:], ptrf[0:64, 0:128]))
                        A(_)
                        a_ct = sA.n
                        trk["ptrf_free"] = a_ct

                        if A_CFG['sub9'] < 5:
                            continue
                        def _(t, a_ct=a_ct):
                            t.wait_ge(sA.h, a_ct)
                            for j in range(2):
                                ii = t.matmul(pE[:, j * 512:j * 512 + 512], cTA[:], negA[:, j * 512:j * 512 + 512],
                                              start=False, stop=True, skip_group_check=True)
                            sP.inc(ii)
                        T(_)
                        p_bm = sP.n

                        if A_CFG['sub9'] < 6:
                            continue
                        def _(s, p_bm=p_bm):
                            s.wait_ge(sP.h, p_bm)
                            sA.inc(s.activation(expA[:], pE[:], AF.Exp))
                        A(_)
                        a_exp = sA.n

                        if A_CFG['sub9'] < 7:
                            continue
                        def _(v, a_exp=a_exp):
                            v.wait_ge(sA.h, a_exp)
                            sV.inc(v.tensor_reduce(ZA[:], expA[:].rearrange("p (s r) -> p s r", r=RB),
                                                   axis=AX.X, op=ALU.add))
                            v.wait_ge(sV.h, sV.n)
                            sV.inc(v.reciprocal(RA[:], ZA[:]))
                        V(_)
                        v_r = sV.n

                        if A_CFG['sub9'] < 8:
                            continue
                        def _(v, v_r=v_r, tt=tt, grp=grp):
                            v.wait_ge(sV.h, v_r)
                            dst = probs[:, tt, grp * 1024:grp * 1024 + 1024].rearrange("p (s r) -> p s r", r=RB)
                            src = expA[:].rearrange("p (s r) -> p s r", r=RB)
                            bc = RA[:].unsqueeze(2).broadcast_to([P, 64, RB])
                            sV.inc(v.tensor_tensor(dst, src, bc, op=ALU.mult))
                        V(_)
                        trk["pE_free"] = sV.n
                        v_pm = sV.n

                    if b == 0 and half == 0 and tt == 0 and dbg_names:
                        dump("xn1_t0", xn1[:], sV, v_xn1)
                        dump("xn1f_t0", xn1f[:], sA, a_xn1f)
                        dump("Kfm_t0", Kfm[:], sA, a_kfm)
                        dump("tanhp_t0", tanhp[:], sA, a_tanh)
                        dump("expA_t0", expA[:], sA, a_exp)
                        dump("probs_t0", probs[:, 0, :], sV, v_pm)
                        dump("Ktm_t0", Ktm[:, 0, :], sA, a_ktm)
                        dump("Vtm_t0", Vtm[:, 0, :], sA, a_v)

                trk["x_free"] = sV.n

                a_tanh_last = trk["pAB_free_pe"]
                a_v_last = trk["pCD_free"]
                v_prod = sV.n
                a_prod = sA.n
                prev_bk = 0
                for h in (range(H) if A_CFG["buckets"] else ()):
                    kb = pA if h % 2 == 0 else pC
                    vb = pB if h % 2 == 0 else pD

                    def _(t, h=h, kb=kb, vb=vb, v_prod=v_prod, a_prod=a_prod, prev_bk=prev_bk):
                        if h == 0:
                            t.wait_ge(sV.h, v_prod)
                            t.wait_ge(sA.h, a_prod)
                        if prev_bk:
                            t.wait_ge(sV.h, prev_bk)
                        pcol = (h // 8) * 1024 + ((h % 8) // 2) * 128 + (h % 2) * 512
                        for tt in range(HALF):
                            t.matmul(kb[0:64, 0:128], Ktm[:, tt, h * 64:h * 64 + 64],
                                     probs[:, tt, pcol:pcol + 128],
                                     start=(tt == 0), stop=(tt == HALF - 1), skip_group_check=True)
                        for tt in range(HALF):
                            ii = t.matmul(vb[0:128, 0:64], probs[:, tt, pcol:pcol + 128],
                                          Vtm[:, tt, h * 64:h * 64 + 64],
                                          start=(tt == 0), stop=(tt == HALF - 1), skip_group_check=True)
                        sP.inc(ii)
                    T(_)
                    p_bk = sP.n

                    def _(v, h=h, kb=kb, vb=vb, half=half, p_bk=p_bk, b=b):
                        v.wait_ge(sP.h, p_bk)
                        if half == 0:
                            sV.inc(v.tensor_copy(bKTa[b][:, h * 128:h * 128 + 128], kb[0:64, 0:128]))
                            sV.inc(v.tensor_copy(bVa[b][:, h * 64:h * 64 + 64], vb[0:128, 0:64]))
                        else:
                            sV.inc(v.tensor_tensor(bKTa[b][:, h * 128:h * 128 + 128], bKTa[b][:, h * 128:h * 128 + 128],
                                                   kb[0:64, 0:128], op=ALU.add))
                            sV.inc(v.tensor_tensor(bVa[b][:, h * 64:h * 64 + 64], bVa[b][:, h * 64:h * 64 + 64],
                                                   vb[0:128, 0:64], op=ALU.add))
                    V(_)
                    prev_bk = sV.n
                trk["pAB_free_pe"] = 0
                trk["pCD_free"] = 0
                trk["bk_evac"] = prev_bk
                trk["prod_free_pe"] = sP.n

                def _(t, prev_bk=prev_bk):
                    t.wait_ge(sV.h, prev_bk)
                    sP.inc(t.nop())
                T(_)

            trk[f"bk_evac{b}"] = trk["bk_evac"]
            dump(f"bKT_b{b}", bKTa[b][:], sV, trk["bk_evac"])
            dump(f"bV_b{b}", bVa[b][:], sV, trk["bk_evac"])

        def emit_B(b):
            def _(v, b=b):
                v.wait_ge(sV.h, trk[f"bk_evac{b}"])
                sV.inc(v.tensor_copy(bKTb[b][0:64, :], bKTa[b][:]))
                sV.inc(v.tensor_copy(bKTb[b][64:128, :], bKTa[b][:]))
                sV.inc(v.tensor_copy(bVb[b][:], bVa[b][:]))
            V(_)
            bk_ready = sV.n
            for tt in range(NT):
                par = tt % 2
                x2_free = trk.get(f"x2st_{par}", 0)
                xin_free = trk.get(f"xinfree_{par}", 0)

                def _(q, b=b, tt=tt, par=par, xin_free=xin_free):
                    if xin_free:
                        q.wait_ge(sV.h, xin_free)
                    dXp[par].inc(q.dma_start(out=xin[par][:], in_=x_tile_v[b, tt]), 16)
                Q(_)
                x_ld = dXp[par].n

                p_xn1T_prev = trk.get("p_xn1T_B", 0)

                def _(v, b=b, tt=tt, par=par, x_ld=x_ld, p_xn1T_prev=p_xn1T_prev):
                    v.wait_ge(dXp[par].h, x_ld)
                    if p_xn1T_prev:
                        v.wait_ge(sP.h, p_xn1T_prev)
                    ncol = b * 16 + tt
                    sV.inc(v.tensor_scalar(xn1[:], xin[par][:], negm1[:, ncol:ncol + 1],
                                           r1s[:, ncol:ncol + 1], op0=ALU.add, op1=ALU.mult))
                V(_)
                v_xn1 = sV.n

                ptrb_free = trk["ptrb_free"]

                def _(t, v_xn1=v_xn1, ptrb_free=ptrb_free):
                    t.wait_ge(sV.h, v_xn1)
                    if ptrb_free:
                        t.wait_ge(sA.h, ptrb_free)
                    for kc in range(8):
                        ii = t.transpose(ptrb[:, kc * 128:kc * 128 + 128], xn1[:, kc * 128:kc * 128 + 128], identb[:])
                    sP.inc(ii)
                T(_)
                trk["p_xn1T_B"] = sP.n
                p_xn1T = sP.n

                def _(s, p_xn1T=p_xn1T):
                    s.wait_ge(sP.h, p_xn1T)
                    sA.inc(s.copy(xn1f[:], ptrb[:]))
                A(_)
                a_xn1f = sA.n

                pab_free = trk.get("pAB_free_B", 0)

                def _(t, a_xn1f=a_xn1f, pab_free=pab_free):
                    t.wait_ge(sA.h, a_xn1f)
                    t.wait_ge(dWqo.h, 32)
                    if pab_free:
                        t.wait_ge(sV.h, pab_free)
                    for kc in range(8):
                        for oc in range(8):
                            bank = pA if oc < 4 else pB
                            ii = t.matmul(bank[:, (oc % 4) * 128:(oc % 4) * 128 + 128],
                                          wq_s[:, kc * 1024 + oc * 128:kc * 1024 + oc * 128 + 128],
                                          xn1f[:, kc * 128:kc * 128 + 128],
                                          start=(kc == 0 and oc % 4 == 0),
                                          stop=(kc == 7 and oc % 4 == 3),
                                          skip_group_check=True)
                    sP.inc(ii)
                T(_)
                p_qm = sP.n

                def _(s, p_qm=p_qm):
                    s.wait_ge(sP.h, p_qm)
                    sA.inc(s.copy(Qt[:, 0:512], pA[:]))
                    sA.inc(s.copy(Qt[:, 512:1024], pB[:]))
                A(_)
                a_q = sA.n

                for grp in range(2):
                    pE_free = trk["pE_free"]

                    def _(t, grp=grp, a_q=a_q, bk_ready=bk_ready, pE_free=pE_free, b=b):
                        t.wait_ge(sA.h, a_q)
                        t.wait_ge(sV.h, bk_ready)
                        if pE_free:
                            t.wait_ge(sV.h, pE_free)
                        for hl in range(8):
                            h = grp * 8 + hl
                            base = 64 * (h % 2)
                            oc = (hl // 2) * 128 + (hl % 2) * 512
                            ii = t.matmul(pE[:, oc:oc + 128],
                                          Qt[base:base + 64, (h // 2) * 128:(h // 2) * 128 + 128],
                                          bKTb[b][base:base + 64, h * 128:h * 128 + 128],
                                          start=(hl in (0, 1)), stop=(hl in (6, 7)),
                                          skip_group_check=True)
                        sP.inc(ii)
                    T(_)
                    p_sc = sP.n

                    def _(v, p_sc=p_sc):
                        v.wait_ge(sP.h, p_sc)
                        sV.inc(v.tensor_reduce(nmaxB[:], pE[:].rearrange("p (h s) -> p h s", s=128),
                                               axis=AX.X, op=ALU.max, negate=False))
                    V(_)
                    v_nm = sV.n

                    ptrf_free = trk["ptrf_free"]

                    def _(t, v_nm=v_nm, ptrf_free=ptrf_free):
                        t.wait_ge(sV.h, v_nm)
                        if ptrf_free:
                            t.wait_ge(sA.h, ptrf_free)
                        sP.inc(t.transpose(ptrf[0:8, 0:128], nmaxB[:], identf[:]))
                    T(_)
                    p_nt = sP.n

                    def _(s, p_nt=p_nt):
                        s.wait_ge(sP.h, p_nt)
                        sA.inc(s.copy(cTB[:], ptrf[0:8, 0:128]))
                    A(_)
                    a_ct = sA.n
                    trk["ptrf_free"] = a_ct

                    def _(t, a_ct=a_ct):
                        t.wait_ge(sA.h, a_ct)
                        for j in range(2):
                            ii = t.matmul(pE[:, j * 512:j * 512 + 512], cTB[:], negB[:, j * 512:j * 512 + 512],
                                          start=False, stop=True, skip_group_check=True)
                        sP.inc(ii)
                    T(_)
                    p_bm = sP.n

                    def _(s, p_bm=p_bm):
                        s.wait_ge(sP.h, p_bm)
                        sA.inc(s.activation(expB[:], pE[:], AF.Exp, scale=0.125))
                    A(_)
                    a_exp = sA.n

                    def _(v, a_exp=a_exp):
                        v.wait_ge(sA.h, a_exp)
                        sV.inc(v.tensor_reduce(ZB[:], expB[:].rearrange("p (h s) -> p h s", s=128),
                                               axis=AX.X, op=ALU.add))
                        v.wait_ge(sV.h, sV.n)
                        sV.inc(v.reciprocal(RBs[:], ZB[:]))
                        v.wait_ge(sV.h, sV.n)
                        sV.inc(v.tensor_tensor(attn_t[:].rearrange("p (h s) -> p h s", s=128),
                                               expB[:].rearrange("p (h s) -> p h s", s=128),
                                               RBs[:].unsqueeze(2).broadcast_to([P, 8, 128]), op=ALU.mult))
                    V(_)
                    v_attn = sV.n
                    trk["pE_free"] = v_attn

                    atb_free = trk.get("ptrb_free_B", a_xn1f)

                    def _(t, v_attn=v_attn, atb_free=atb_free):
                        t.wait_ge(sV.h, v_attn)
                        t.wait_ge(sA.h, atb_free)
                        for hl in range(8):
                            ii = t.transpose(ptrb[:, hl * 128:hl * 128 + 128], attn_t[:, hl * 128:hl * 128 + 128], identb[:])
                        sP.inc(ii)
                    T(_)
                    p_at = sP.n

                    def _(s, p_at=p_at):
                        s.wait_ge(sP.h, p_at)
                        sA.inc(s.copy(expT[:], ptrb[:]))
                    A(_)
                    a_expT = sA.n
                    trk["ptrb_free_B"] = a_expT
                    trk["ptrb_free"] = a_expT

                    pc_free = trk.get("pC_free_B", 0)

                    def _(t, grp=grp, a_expT=a_expT, pc_free=pc_free, b=b):
                        t.wait_ge(sA.h, a_expT)
                        if pc_free:
                            t.wait_ge(sA.h, pc_free)
                        for hl in range(8):
                            h = grp * 8 + hl
                            base = 64 * (h % 2)
                            cs = (h // 2) % 4
                            ec = (hl // 2) * 128 + (hl % 2) * 512
                            ii = t.matmul(pC[base:base + 64, cs * 128:cs * 128 + 128],
                                          bVb[b][:, h * 64:h * 64 + 64],
                                          expT[:, ec:ec + 128],
                                          start=(hl in (0, 1)), stop=(hl in (6, 7)), skip_group_check=True)
                        sP.inc(ii)
                    T(_)
                    p_om = sP.n

                    def _(s, grp=grp, p_om=p_om):
                        s.wait_ge(sP.h, p_om)
                        sA.inc(s.copy(aout[:, grp * 512:grp * 512 + 512], pC[:]))
                    A(_)
                    trk["pC_free_B"] = sA.n
                a_out2 = sA.n

                def _(t, a_out2=a_out2, a_q=a_q):
                    t.wait_ge(sA.h, a_out2)
                    for dh in range(2):
                        bank = pA if dh == 0 else pB
                        for kc in range(8):
                            t.matmul(bank[:, 0:512], aout[:, kc * 128:kc * 128 + 128],
                                     wo_s[:, kc * 1024 + dh * 512:kc * 1024 + dh * 512 + 512],
                                     start=(kc == 0), stop=False, skip_group_check=True)
                        ii = t.matmul(bank[:, 0:512], onesb[:], bo_s[0:1, dh * 512:dh * 512 + 512],
                                      start=False, stop=True, skip_group_check=True)
                    sP.inc(ii)
                T(_)
                p_wo = sP.n

                def _(v, p_wo=p_wo):
                    v.wait_ge(sP.h, p_wo)
                    sV.inc(v.bn_stats(bnb2[:, 0:6], pA[:]))
                    sV.inc(v.bn_stats(bnb2[:, 6:12], pB[:]))
                    v.wait_ge(sV.h, sV.n)
                    sV.inc(v.bn_aggr(mv2[:], bnb2[:]))
                    v.wait_ge(sV.h, sV.n)
                    sV.inc(v.tensor_scalar(sc2[:, 0:1], mv2[:, 1:2], EPS, None, op0=ALU.add))
                V(_)
                v_u2 = sV.n

                def _(g, v_u2=v_u2):
                    g.wait_ge(sV.h, v_u2)
                    sG.inc(g.tensor_tensor(sc2[:, 3:4], sc2[:, 0:1], nh1_s[:], op=ALU.pow))
                G(_)
                g_r2 = sG.n
                a_r2 = None

                x2_free = trk.get(f"x2free_{par}", 0)

                def _(v, par=par, g_r2=g_r2, x2_free=x2_free):
                    v.wait_ge(sG.h, g_r2)
                    if x2_free:
                        v.wait_ge(dSp[par].h, x2_free)
                    sV.inc(v.tensor_scalar(sc2[:, 0:1], mv2[:, 0:1], sc2[:, 3:4], -1.0, op0=ALU.mult, op1=ALU.mult))
                    v.wait_ge(sV.h, sV.n)
                    sV.inc(v.tensor_scalar(ln2t[:, 0:512], pA[:], sc2[:, 3:4], sc2[:, 0:1], op0=ALU.mult, op1=ALU.add))
                    sV.inc(v.tensor_scalar(ln2t[:, 512:1024], pB[:], sc2[:, 3:4], sc2[:, 0:1], op0=ALU.mult, op1=ALU.add))
                    v.wait_ge(sV.h, sV.n)
                    sV.inc(v.tensor_tensor(x2t[par][:], xin[par][:], ln2t[:], op=ALU.add))
                V(_)
                v_x2 = sV.n
                trk["pAB_free_B"] = v_x2
                trk[f"xinfree_{par}"] = v_x2

                def _(q, b=b, tt=tt, par=par, v_x2=v_x2):
                    q.wait_ge(sV.h, v_x2)
                    dSp[par].inc(q.dma_start(out=x2_tile_v[b, tt], in_=x2t[par][:]), 16)
                Q(_)
                trk[f"x2free_{par}"] = dSp[par].n

                if b == 0 and tt == 0:
                    dump("Qt_t0", Qt[:], sA, a_q)
                    dump("expT_t0", expT[:], sA, trk["ptrb_free_B"])
                    dump("aout_t0", aout[:], sA, a_out2)
                    dump("x2_t0", x2t[0][:], sV, v_x2)

        def barrier():
            vals = [(sP, sP.n), (sV, sV.n), (sA, sA.n), (sG, sG.n),
                    (dXA, dXA.n), (dXp[0], dXp[0].n), (dXp[1], dXp[1].n),
                    (dCon, dCon.n), (dPro, dPro.n), (dWkv, dWkv.n), (dWqo, dWqo.n),
                    (dSp[0], dSp[0].n), (dSp[1], dSp[1].n)]

            def _(e):
                for s, vv in vals:
                    if vv:
                        e.wait_ge(s.h, vv)
            for body in (T, V, A, G, Q):
                body(_)

        if "A" in phases:
            for _b in A_CFG["bs"]:
                emit_A(_b)
        barrier()
        esA.close()

        esB = ExitStack()
        Qt = esB.enter_context(nc.sbuf_tensor("Qt", [P, D], BF16))
        nmaxB = esB.enter_context(nc.sbuf_tensor("nmaxB", [P, 8], F32))
        cTB = esB.enter_context(nc.sbuf_tensor("cTB", [8, 128], BF16))
        expB = esB.enter_context(nc.sbuf_tensor("expB", [P, 1024], F32))
        ZB = esB.enter_context(nc.sbuf_tensor("ZB", [P, 8], F32))
        RBs = esB.enter_context(nc.sbuf_tensor("RBs", [P, 8], F32))
        attn_t = esB.enter_context(nc.sbuf_tensor("attn_t", [P, 1024], BF16))
        expT = esB.enter_context(nc.sbuf_tensor("expT", [P, 1024], BF16))
        aout = esB.enter_context(nc.sbuf_tensor("aout", [P, D], BF16))
        ln2t = esB.enter_context(nc.sbuf_tensor("ln2t", [P, D], F32))
        bnb2 = esB.enter_context(nc.sbuf_tensor("bnb2", [P, 12], F32))
        mv2 = esB.enter_context(nc.sbuf_tensor("mv2", [P, 2], F32))
        sc2 = esB.enter_context(nc.sbuf_tensor("sc2", [P, 4], F32))
        bKTb = [esB.enter_context(nc.sbuf_tensor(f"bKTb{b}", [P, 2048], BF16)) for b in range(2)]
        bVb = [esB.enter_context(nc.sbuf_tensor(f"bVb{b}", [P, 1024], BF16)) for b in range(2)]
        xin = [esB.enter_context(nc.sbuf_tensor(f"xin{i}", [P, D], F32)) for i in range(2)]
        x2t = [esB.enter_context(nc.sbuf_tensor(f"x2t{i}", [P, D], F32)) for i in range(2)]

        if "B" in phases:
            emit_B(0)
            emit_B(1)
        barrier()
        esB.close()
        esAB.close()

        if "F" in phases:
            dFc = Sm(es, nc, "dFc")
            esF = ExitStack()
            muu = esF.enter_context(nc.sbuf_tensor("muu", [P, 2048], BF16))
            gam_s = esF.enter_context(nc.sbuf_tensor("gam_s", [P, D], F32))
            bet_s = esF.enter_context(nc.sbuf_tensor("bet_s", [P, D], F32))
            scl_s = esF.enter_context(nc.sbuf_tensor("scl_s", [P, 1], F32))
            b1f_s = esF.enter_context(nc.sbuf_tensor("b1f_s", [P, 32], F32))
            b2r_s = esF.enter_context(nc.sbuf_tensor("b2r_s", [P, 2, D], BF16))
            esFpre = ExitStack()
            mu_sb = esFpre.enter_context(nc.sbuf_tensor("mu_sb", [P, 2048], F32))
            musq = esFpre.enter_context(nc.sbuf_tensor("musq", [P, 2048], BF16))
            ssmu = esFpre.enter_context(nc.sbuf_tensor("ssmu", [1, 1024], F32))
            rmub = esFpre.enter_context(nc.sbuf_tensor("rmub", [P, 256], F32))
            def _(g):
                g.dma_start(out=mu_sb[:], in_=mu_d).then_inc(dFc.h, 16)
                g.dma_start(out=gam_s[:], in_=gam_d).then_inc(dFc.h, 16)
                g.dma_start(out=bet_s[:], in_=bet_d).then_inc(dFc.h, 16)
                g.dma_start(out=scl_s[:], in_=scl_d).then_inc(dFc.h, 16)
                g.dma_start(out=b1f_s[:], in_=b1_d).then_inc(dFc.h, 16)
                g.dma_start(out=b2r_s[:], in_=b2_d.rearrange("(c p) d -> p c d", p=P)).then_inc(dFc.h, 16)
                dFc.n += 96
            G(_)
            w_fc = dFc.n

            def _(s):
                s.wait_ge(dFc.h, w_fc)
                sA.inc(s.activation(musq[:], mu_sb[:], AF.Square))
            A(_)
            a_musq = sA.n

            def _(t):
                t.wait_ge(sA.h, a_musq)
                for kc in range(8):
                    ii = t.matmul(pD[0:1, 0:256], onescolb[:], musq[:, kc * 256:kc * 256 + 256],
                                  start=(kc == 0), stop=(kc == 7))
                sP.inc(ii)
            T(_)
            p_mu = sP.n

            def _(s):
                s.wait_ge(sP.h, p_mu)
                sA.inc(s.copy(ssmu[0:1, 0:256], pD[0:1, 0:256]))
                s.wait_ge(sA.h, sA.n)
                sA.inc(s.activation(ssmu[0:1, 256:512], ssmu[0:1, 0:256], AF.Ln))
                s.wait_ge(sA.h, sA.n)
                sA.inc(s.activation(ssmu[0:1, 512:768], ssmu[0:1, 256:512], AF.Exp, scale=0.5))
            A(_)
            a_nmu = sA.n

            def _(v):
                v.wait_ge(sA.h, a_nmu)
                sV.inc(v.tensor_scalar(ssmu[0:1, 768:1024], ssmu[0:1, 512:768], 1e-12, None, op0=ALU.max))
                v.wait_ge(sV.h, sV.n)
                sV.inc(v.reciprocal(ssmu[0:1, 0:256], ssmu[0:1, 768:1024]))
            V(_)
            v_rmu = sV.n

            def _(t):
                t.wait_ge(sV.h, v_rmu)
                sP.inc(t.matmul(pD[0:128, 0:256], ones1f[:], ssmu[0:1, 0:256], start=True, stop=True))
            T(_)
            p_rmub = sP.n

            def _(s):
                s.wait_ge(sP.h, p_rmub)
                sA.inc(s.copy(rmub[:], pD[0:128, 0:256]))
            A(_)
            a_rmub = sA.n

            def _(v):
                v.wait_ge(sA.h, a_rmub)
                for kc in range(8):
                    sV.inc(v.tensor_tensor(muu[:, kc * 256:kc * 256 + 256], mu_sb[:, kc * 256:kc * 256 + 256],
                                           rmub[:], op=ALU.mult))
            V(_)
            v_muu = sV.n

            esFpre.close()
            xnn_sb = esF.enter_context(nc.sbuf_tensor("xnn_sb", [P, NT, D], BF16))
            lgsm = esF.enter_context(nc.sbuf_tensor("lgsm", [P, 2, 2048], BF16))
            comb = esF.enter_context(nc.sbuf_tensor("comb", [P, 2, NT, 256], BF16))
            si_sb = esF.enter_context(nc.sbuf_tensor("si_sb", [P, 2, 8, 256], BF16))
            so_stg = esF.enter_context(nc.sbuf_tensor("so_stg", [P, 2, 2, D], BF16))
            so_bf = esF.enter_context(nc.sbuf_tensor("so_bf", [P, 2, 2, D], BF16))
            w1buf = [esF.enter_context(nc.sbuf_tensor(f"w1b{i}", [P, 2048], BF16)) for i in range(2)]
            w2buf = [esF.enter_context(nc.sbuf_tensor(f"w2b{i}", [P, 2048], BF16)) for i in range(2)]
            so_st = [esF.enter_context(nc.sbuf_tensor(f"sost{i}", [32, 1024], F32)) for i in range(2)]
            h_bf = esF.enter_context(nc.sbuf_tensor("h_bf", [P, 64], BF16))
            x2f = [esF.enter_context(nc.sbuf_tensor(f"x2f{i}", [P, D], F32)) for i in range(2)]
            t1f = esF.enter_context(nc.sbuf_tensor("t1f", [P, D], F32))
            xu_t = esF.enter_context(nc.sbuf_tensor("xu_t", [P, D], BF16))
            xuf = esF.enter_context(nc.sbuf_tensor("xuf", [P, D], BF16))
            lg_t = esF.enter_context(nc.sbuf_tensor("lg_t", [P, 256], F32))
            expc = esF.enter_context(nc.sbuf_tensor("expc", [P, 256], F32))
            bnF = esF.enter_context(nc.sbuf_tensor("bnF", [P, 12], F32))
            mvF = esF.enter_context(nc.sbuf_tensor("mvF", [P, 2], F32))
            scF = esF.enter_context(nc.sbuf_tensor("scF", [P, 16], F32))
            dspb = esF.enter_context(nc.sbuf_tensor("dspb", [P, 2048], BF16))
            esFd = ExitStack()
            dtm = esFd.enter_context(nc.sbuf_tensor("dtm", [P, NT, 256], BF16))

            ftrk = {"ptrb": 0, "ptrf": 0, "pA": 0, "si": {}, "comb": {}}

            def emit_F1(b):
                for tt in range(NT):
                    par = tt % 2
                    xfree = ftrk.get(f"x2f{par}", 0)

                    def _(q, b=b, tt=tt, par=par, xfree=xfree):
                        if xfree:
                            q.wait_ge(sV.h, xfree)
                        if b == 0 and tt < 2:
                            q.wait_ge(sV.h, v_muu)
                        dXp[par].inc(q.dma_start(out=x2f[par][:], in_=x2_tile_v[b, tt]), 16)
                    Q(_)
                    x_ld = dXp[par].n

                    def _(v, par=par, x_ld=x_ld):
                        v.wait_ge(dXp[par].h, x_ld)
                        sV.inc(v.bn_stats(bnF[:, 0:6], x2f[par][:, 0:512]))
                        sV.inc(v.bn_stats(bnF[:, 6:12], x2f[par][:, 512:1024]))
                        v.wait_ge(sV.h, sV.n)
                        sV.inc(v.bn_aggr(mvF[:], bnF[:]))
                        v.wait_ge(sV.h, sV.n)
                        sV.inc(v.tensor_scalar(scF[:, 0:1], mvF[:, 1:2], EPS, None, op0=ALU.add))
                    V(_)
                    v_u3 = sV.n

                    def _(g, v_u3=v_u3):
                        g.wait_ge(sV.h, v_u3)
                        sG.inc(g.tensor_tensor(scF[:, 2:3], scF[:, 0:1], nh1_s[:], op=ALU.pow))
                    G(_)
                    g_r3 = sG.n

                    def _(v, g_r3=g_r3):
                        v.wait_ge(sG.h, g_r3)
                        sV.inc(v.tensor_scalar(scF[:, 3:4], mvF[:, 1:2], scF[:, 2:3], scF[:, 2:3], op0=ALU.mult, op1=ALU.mult))
                        v.wait_ge(sV.h, sV.n)
                        sV.inc(v.tensor_scalar(scF[:, 4:5], scF[:, 3:4], EPS, None, op0=ALU.add))
                    V(_)
                    v_v4 = sV.n

                    def _(g, v_v4=v_v4):
                        g.wait_ge(sV.h, v_v4)
                        sG.inc(g.tensor_tensor(scF[:, 6:7], scF[:, 4:5], nh1_s[:], op=ALU.pow))
                    G(_)
                    a_r4 = None
                    g_r4 = sG.n

                    p_xuT_prev = ftrk.get("p_xuT", 0)

                    si_mm = ftrk.get("si_mm_done", 0)

                    def _(v, par=par, tt=tt, g_r4=g_r4, p_xuT_prev=p_xuT_prev, si_mm=si_mm):
                        v.wait_ge(sG.h, g_r4)
                        if p_xuT_prev:
                            v.wait_ge(sP.h, p_xuT_prev)
                        sV.inc(v.tensor_tensor(scF[:, 7:8], scF[:, 2:3], scF[:, 6:7], op=ALU.mult))
                        v.wait_ge(sV.h, sV.n)
                        sV.inc(v.tensor_scalar(scF[:, 8:9], mvF[:, 0:1], scF[:, 7:8], -1.0, op0=ALU.mult, op1=ALU.mult))
                        v.wait_ge(sV.h, sV.n)
                        sV.inc(v.tensor_scalar(t1f[:], x2f[par][:], scF[:, 7:8], scF[:, 8:9], op0=ALU.mult, op1=ALU.add))
                        v.wait_ge(sV.h, sV.n)
                        sV.inc(v.tensor_tensor(t1f[:], t1f[:], gam_s[:], op=ALU.mult))
                        v.wait_ge(sV.h, sV.n)
                        if si_mm:
                            v.wait_ge(sP.h, si_mm)
                        sV.inc(v.tensor_tensor(xnn_sb[:, tt, :], t1f[:], bet_s[:], op=ALU.add))
                        v.wait_ge(sV.h, sV.n)
                        sV.inc(v.bn_stats(bnF[:, 0:6], xnn_sb[:, tt, 0:512]))
                        sV.inc(v.bn_stats(bnF[:, 6:12], xnn_sb[:, tt, 512:1024]))
                        v.wait_ge(sV.h, sV.n)
                        sV.inc(v.bn_aggr(mvF[:], bnF[:]))
                        v.wait_ge(sV.h, sV.n)
                        sV.inc(v.tensor_tensor(scF[:, 9:10], mvF[:, 0:1], mvF[:, 0:1], op=ALU.mult))
                        v.wait_ge(sV.h, sV.n)
                        sV.inc(v.tensor_tensor(scF[:, 10:11], scF[:, 9:10], mvF[:, 1:2], op=ALU.add))
                        v.wait_ge(sV.h, sV.n)
                        sV.inc(v.tensor_scalar(scF[:, 11:12], scF[:, 10:11], 1024.0, None, op0=ALU.mult))
                    V(_)
                    v_ss = sV.n

                    def _(g, v_ss=v_ss):
                        g.wait_ge(sV.h, v_ss)
                        sG.inc(g.tensor_tensor(scF[:, 12:13], scF[:, 11:12], ph1_s[:], op=ALU.pow))
                    G(_)
                    a_n5 = None
                    g_n5 = sG.n

                    def _(v, tt=tt, g_n5=g_n5):
                        v.wait_ge(sG.h, g_n5)
                        sV.inc(v.tensor_scalar(scF[:, 13:14], scF[:, 12:13], 1e-12, None, op0=ALU.max))
                        v.wait_ge(sV.h, sV.n)
                        sV.inc(v.reciprocal(scF[:, 14:15], scF[:, 13:14]))
                        v.wait_ge(sV.h, sV.n)
                        sV.inc(v.tensor_tensor(scF[:, 15:16], scF[:, 14:15], scl_s[:], op=ALU.mult))
                        v.wait_ge(sV.h, sV.n)
                        sV.inc(v.tensor_scalar(xu_t[:], xnn_sb[:, tt, :], scF[:, 15:16], None, op0=ALU.mult))
                    V(_)
                    v_xu = sV.n

                    g_xnn = sG.n
                    ftrk["g_xnn"] = g_xnn


                    ptrb_free = ftrk["ptrb"]

                    def _(t, v_xu=v_xu, ptrb_free=ptrb_free):
                        t.wait_ge(sV.h, v_xu)
                        if ptrb_free:
                            t.wait_ge(sA.h, ptrb_free)
                        for kc in range(8):
                            ii = t.transpose(ptrb[:, kc * 128:kc * 128 + 128], xu_t[:, kc * 128:kc * 128 + 128], identb[:])
                        sP.inc(ii)
                    T(_)
                    ftrk["p_xuT"] = sP.n
                    p_xuT = sP.n

                    def _(s, p_xuT=p_xuT):
                        s.wait_ge(sP.h, p_xuT)
                        sA.inc(s.copy(xuf[:], ptrb[:]))
                    A(_)
                    a_xuf = sA.n
                    ftrk["ptrb"] = a_xuf

                    pa_free = ftrk["pA"]

                    def _(t, a_xuf=a_xuf, pa_free=pa_free, v_muu=v_muu):
                        t.wait_ge(sA.h, a_xuf)
                        t.wait_ge(sV.h, v_muu)
                        if pa_free:
                            t.wait_ge(sA.h, pa_free)
                        for kc in range(8):
                            ii = t.matmul(pA[:, 0:256], xuf[:, kc * 128:kc * 128 + 128],
                                          muu[:, kc * 256:kc * 256 + 256],
                                          start=(kc == 0), stop=(kc == 7))
                        sP.inc(ii)
                    T(_)
                    p_lg2 = sP.n

                    def _(s, p_lg2=p_lg2):
                        s.wait_ge(sP.h, p_lg2)
                        sA.inc(s.copy(lg_t[:], pA[:, 0:256]))
                    A(_)
                    a_lg = sA.n
                    ftrk["pA"] = a_lg

                    def _(v, a_lg=a_lg):
                        v.wait_ge(sA.h, a_lg)
                        sV.inc(v.tensor_reduce(scF[:, 0:1], lg_t[:], axis=AX.X, op=ALU.max, negate=True))
                    V(_)
                    v_nmc = sV.n

                    def _(s, v_nmc=v_nmc):
                        s.wait_ge(sV.h, v_nmc)
                        sA.inc(s.activation(expc[:], lg_t[:], AF.Exp, bias=scF[:, 0:1], accum_out=scF[:, 1:2]))
                    A(_)
                    a_expc = sA.n

                    def _(v, b=b, tt=tt, a_expc=a_expc):
                        v.wait_ge(sA.h, a_expc)
                        sV.inc(v.reciprocal(scF[:, 2:3], scF[:, 1:2]))
                        v.wait_ge(sV.h, sV.n)
                        sV.inc(v.tensor_scalar(comb[:, b, tt, :], expc[:], scF[:, 2:3], None, op0=ALU.mult))
                    V(_)
                    ftrk["comb"][b] = sV.n

                    ptrf_free = ftrk["ptrf"]

                    def _(t, a_lg=a_lg, ptrf_free=ptrf_free):
                        t.wait_ge(sA.h, a_lg)
                        if ptrf_free:
                            t.wait_ge(sA.h, ptrf_free)
                        t.transpose(ptrf[:, 0:128], lg_t[:, 0:128], identf[:])
                        sP.inc(t.transpose(ptrf[:, 128:256], lg_t[:, 128:256], identf[:]))
                    T(_)
                    p_lgT = sP.n

                    disp_done = ftrk.get("disp_done_v", 0)

                    def _(s, b=b, tt=tt, p_lgT=p_lgT, disp_done=disp_done):
                        s.wait_ge(sP.h, p_lgT)
                        if disp_done:
                            s.wait_ge(sV.h, disp_done)
                        sA.inc(s.copy(lgsm[:, :, tt * 128:tt * 128 + 128],
                                      ptrf[:, 0:256].rearrange("p (c t) -> p c t", t=128)))
                    A(_)
                    a_lgsm = sA.n
                    ftrk["ptrf"] = a_lgsm
                    ftrk[f"x2f{par}"] = v_ss
                    if tt == 0 and dbg_names:
                        dump(f"xnnf_b{b}", xnn_f[:], sV, v_ss)
                        dump(f"lg_b{b}", lg_t[:], sA, a_lg)
                        dump(f"comb0_b{b}", comb[:, b, 0, :], sV, ftrk["comb"][b])

                ftrk[f"lgsm_done"] = sA.n
                ftrk[f"xnn_done"] = sV.n
                if dbg_names:
                    dump(f"xnn14_a_b{b}", xnn_sb[:, 14, :], sG, sG.n)

                for c in range(2):
                    def _(v, c=c):
                        v.wait_ge(sA.h, ftrk["lgsm_done"])
                        sV.inc(v.tensor_reduce(scF[:, 0:1], lgsm[:, c, :], axis=AX.X, op=ALU.max, negate=True))
                    V(_)
                    v_nmd = sV.n

                    dspb_free = ftrk.get("dspb_free", 0)

                    def _(s, c=c, v_nmd=v_nmd, dspb_free=dspb_free):
                        s.wait_ge(sV.h, v_nmd)
                        if dspb_free:
                            s.wait_ge(sP.h, dspb_free)
                        sA.inc(s.activation(dspb[:], lgsm[:, c, :], AF.Exp, bias=scF[:, 0:1], accum_out=scF[:, 1:2]))
                        s.wait_ge(sA.h, sA.n)
                        sA.inc(s.activation(scF[:, 2:3], scF[:, 1:2], AF.Ln))
                    A(_)
                    a_lnz = sA.n

                    def _(v, a_lnz=a_lnz):
                        v.wait_ge(sA.h, a_lnz)
                        sV.inc(v.tensor_scalar(scF[:, 3:4], scF[:, 2:3], -1.0, scF[:, 0:1], op0=ALU.mult, op1=ALU.add))
                    V(_)
                    v_b2x = sV.n

                    def _(s, c=c, v_b2x=v_b2x):
                        s.wait_ge(sV.h, v_b2x)
                        sA.inc(s.activation(dspb[:], lgsm[:, c, :], AF.Exp, bias=scF[:, 3:4]))
                    A(_)
                    v_dsp = sV.n
                    a_dsp = sA.n

                    for rnd in range(2):
                        ptrb_free = ftrk["ptrb"]

                        def _(t, c=c, rnd=rnd, a_dsp=a_dsp, ptrb_free=ptrb_free):
                            t.wait_ge(sA.h, a_dsp)
                            if ptrb_free:
                                t.wait_ge(sA.h, ptrb_free)
                            for j in range(8):
                                tt = rnd * 8 + j
                                ii = t.transpose(ptrb[:, j * 128:j * 128 + 128],
                                                 dspb[:, tt * 128:tt * 128 + 128], identb[:])
                            sP.inc(ii)
                        T(_)
                        p_dT = sP.n
                        ftrk["dspb_free"] = p_dT
                        si_mm_done = ftrk.get("si_mm_done", 0)

                        def _(s, c=c, rnd=rnd, p_dT=p_dT, si_mm_done=si_mm_done):
                            s.wait_ge(sP.h, p_dT)
                            if si_mm_done:
                                s.wait_ge(sP.h, si_mm_done)
                            sA.inc(s.copy(dtm[:, rnd * 8:rnd * 8 + 8, c * 128:c * 128 + 128],
                                          ptrb[:].rearrange("p (j t) -> p j t", t=128)))
                        A(_)
                        ftrk["ptrb"] = sA.n
                v_dtm = sA.n
                ftrk["disp_done_v"] = sV.n

                prev_si = 0
                for dc in range(8):
                    bank = pB if dc % 2 == 0 else pC

                    def _(t, b=b, dc=dc, bank=bank, v_dtm=v_dtm, prev_si=prev_si):
                        t.wait_ge(sA.h, v_dtm)
                        t.wait_ge(sV.h, ftrk["xnn_done"])
                        if prev_si:
                            t.wait_ge(sA.h, prev_si)
                        for tt in range(NT):
                            ii = t.matmul(bank[:, 0:256], xnn_sb[:, tt, dc * 128:dc * 128 + 128],
                                          dtm[:, tt, :], start=(tt == 0), stop=(tt == NT - 1))
                        sP.inc(ii)
                    T(_)
                    p_si = sP.n

                    def _(s, b=b, dc=dc, bank=bank, p_si=p_si):
                        s.wait_ge(sP.h, p_si)
                        sA.inc(s.copy(si_sb[:, b, dc, :], bank[:, 0:256]))
                    A(_)
                    prev_si = sA.n if dc >= 1 else 0
                    if dc >= 1:
                        prev_si = ftrk["si"].get(dc - 1, 0)
                    ftrk["si"][dc] = sA.n
                ftrk[f"si_done{b}"] = sA.n
                ftrk["si_mm_done"] = sP.n
                if dbg_names:
                    dump(f"xnn14_b_b{b}", xnn_sb[:, 14, :], sP, sP.n)
                if dbg_names:
                    dump(f"si_b{b}", si_sb[:, b], sA, sA.n)
                    dump(f"dtm_b{b}", dtm[:, 0, :], sA, sA.n)
                    dump(f"dtm7_b{b}", dtm[:, 7, :], sA, sA.n)
                    dump(f"dtm8_b{b}", dtm[:, 8, :], sA, sA.n)
                    dump(f"dtm15_b{b}", dtm[:, 15, :], sA, sA.n)

            def emit_experts():
                si_done = max(ftrk["si_done0"], ftrk["si_done1"])
                for e in range(E):
                    par = e % 2
                    wfree = ftrk.get(f"wbuf{par}", 0)

                    def _(g, e=e, par=par, wfree=wfree):
                        if wfree:
                            g.wait_ge(sP.h, wfree)
                        if e < 2:
                            g.wait_ge(sV.h, v_muu)
                        g.dma_start(out=w1buf[par][:], in_=w1_d[e]).then_inc(dE[par].h, 16)
                        g.dma_start(out=w2buf[par][:], in_=w2_d[e]).then_inc(dE[par].h, 16)
                        dE[par].n += 32
                    G(_)
                    w_e = dE[par].n

                    pd_free = ftrk.get("pD_exp", 0)

                    def _(t, e=e, par=par, w_e=w_e, si_done=si_done, pd_free=pd_free):
                        t.wait_ge(dE[par].h, w_e)
                        t.wait_ge(sA.h, si_done)
                        if pd_free:
                            t.wait_ge(sA.h, pd_free)
                        for dc in range(8):
                            for hc in range(2):
                                ii = t.matmul(pD[:, hc * 32:hc * 32 + 32],
                                              w1buf[par][:, dc * 256 + hc * 128:dc * 256 + hc * 128 + 128],
                                              si_sb[:, :, dc, e * 16:e * 16 + 16],
                                              start=(dc == 0 and hc == 0), stop=(dc == 7 and hc == 1),
                                              skip_group_check=True)
                        sP.inc(ii)
                    T(_)
                    p_h = sP.n

                    def _(s, e=e, p_h=p_h):
                        s.wait_ge(sP.h, p_h)
                        sA.inc(s.activation(h_bf[:, 0:32], pD[:, 0:32], AF.Gelu, bias=b1f_s[:, 2 * e:2 * e + 1]))
                        sA.inc(s.activation(h_bf[:, 32:64], pD[:, 32:64], AF.Gelu, bias=b1f_s[:, 2 * e + 1:2 * e + 2]))
                    A(_)
                    a_h = sA.n
                    ftrk["pD_exp"] = a_h

                    pe_free = ftrk.get("pE_exp", 0)

                    def _(t, e=e, par=par, a_h=a_h, pe_free=pe_free):
                        t.wait_ge(sA.h, a_h)
                        if pe_free:
                            t.wait_ge(sA.h, pe_free)
                        for dh in range(2):
                            for hc in range(2):
                                ii = t.matmul(pE[0:32, dh * 512:dh * 512 + 512],
                                              h_bf[:, hc * 32:hc * 32 + 32],
                                              w2buf[par][:, hc * 1024 + dh * 512:hc * 1024 + dh * 512 + 512],
                                              start=(hc == 0), stop=(hc == 1), skip_group_check=True)
                        sP.inc(ii)
                    T(_)
                    p_so = sP.n
                    ftrk[f"wbuf{par}"] = p_so

                    spar = e % 2
                    stfree = ftrk.get(f"sost{spar}", 0)

                    def _(s, e=e, spar=spar, p_so=p_so, stfree=stfree):
                        s.wait_ge(sP.h, p_so)
                        if stfree:
                            s.wait_ge(dSh[spar].h, stfree)
                        sA.inc(s.copy(so_st[spar][0:32, 0:512], pE[0:32, 0:512]))
                        sA.inc(s.copy(so_st[spar][0:32, 512:1024], pE[0:32, 512:1024]))
                    A(_)
                    a_so = sA.n
                    ftrk["pE_exp"] = a_so

                    def _(g, e=e, spar=spar, a_so=a_so):
                        g.wait_ge(sA.h, a_so)
                        r0 = (e % 8) * 16
                        g.dma_start(out=so_stg[r0:r0 + 16, 0, e // 8, :], in_=so_st[spar][0:16, :]).then_inc(dSh[spar].h, 16)
                        g.dma_start(out=so_stg[r0:r0 + 16, 1, e // 8, :], in_=so_st[spar][16:32, :]).then_inc(dSh[spar].h, 16)
                        dSh[spar].n += 32
                    G(_)
                    ftrk[f"sost{spar}"] = dSh[spar].n

                def _(v):
                    v.wait_ge(dSh[0].h, dSh[0].n)
                    v.wait_ge(dSh[1].h, dSh[1].n)
                    for bb in range(2):
                        for c in range(2):
                            sV.inc(v.tensor_tensor(so_bf[:, bb, c, :], so_stg[:, bb, c, :],
                                                   b2r_s[:, c, :], op=ALU.add))
                V(_)
                ftrk["so_ready"] = sV.n
                if dbg_names:
                    dump("sobf_b0", so_bf[:, 0], sV, sV.n)
                    dump("sobf_b1", so_bf[:, 1], sV, sV.n)

            def emit_F3(b):
                for tt in range(NT):
                    par = tt % 2
                    xfree = ftrk.get(f"x2r{par}", 0) or ftrk.get(f"x2f{par}", 0)

                    def _(q, b=b, tt=tt, par=par, xfree=xfree):
                        if xfree:
                            q.wait_ge(sV.h, xfree)
                        dXp[par].inc(q.dma_start(out=x2f[par][:], in_=x2_tile_v[b, tt]), 16)
                    Q(_)
                    x_ld = dXp[par].n

                    ptrb_free = ftrk["ptrb"]

                    def _(t, b=b, tt=tt, ptrb_free=ptrb_free):
                        t.wait_ge(sV.h, ftrk["comb"][b])
                        if ptrb_free:
                            t.wait_ge(sA.h, ptrb_free)
                        t.transpose(ptrb[:, 0:128], comb[:, b, tt, 0:128], identb[:])
                        sP.inc(t.transpose(ptrb[:, 128:256], comb[:, b, tt, 128:256], identb[:]))
                    T(_)
                    p_cT = sP.n

                    def _(s, p_cT=p_cT):
                        s.wait_ge(sP.h, p_cT)
                        sA.inc(s.copy(cb_sm[:], ptrb[:, 0:256]))
                    A(_)
                    a_cb = sA.n
                    ftrk["ptrb"] = a_cb

                    pbc_free = ftrk.get("pBC_F3", 0)

                    def _(t, b=b, a_cb=a_cb, pbc_free=pbc_free):
                        t.wait_ge(sA.h, a_cb)
                        t.wait_ge(sV.h, ftrk["so_ready"])
                        if pbc_free:
                            t.wait_ge(sV.h, pbc_free)
                        for dh in range(2):
                            bank = pB if dh == 0 else pC
                            for c in range(2):
                                ii = t.matmul(bank[:, 0:512], cb_sm[:, c * 128:c * 128 + 128],
                                              so_bf[:, b, c, dh * 512:dh * 512 + 512],
                                              start=(c == 0), stop=(c == 1), skip_group_check=True)
                        sP.inc(ii)
                    T(_)
                    p_o = sP.n

                    def _(v, p_o=p_o):
                        v.wait_ge(sP.h, p_o)
                        sV.inc(v.bn_stats(bnF[:, 0:6], pB[:, 0:512]))
                        sV.inc(v.bn_stats(bnF[:, 6:12], pC[:, 0:512]))
                        v.wait_ge(sV.h, sV.n)
                        sV.inc(v.bn_aggr(mvF[:], bnF[:]))
                        v.wait_ge(sV.h, sV.n)
                        sV.inc(v.tensor_scalar(scF[:, 0:1], mvF[:, 1:2], EPS, None, op0=ALU.add))
                    V(_)
                    v_u6 = sV.n

                    def _(g, v_u6=v_u6):
                        g.wait_ge(sV.h, v_u6)
                        sG.inc(g.tensor_tensor(scF[:, 3:4], scF[:, 0:1], nh1_s[:], op=ALU.pow))
                    G(_)
                    a_r6 = None
                    g_r6 = sG.n

                    ofree = ftrk.get(f"ostore{par}", 0)

                    def _(v, b=b, tt=tt, par=par, g_r6=g_r6, x_ld=x_ld, ofree=ofree):
                        v.wait_ge(sG.h, g_r6)
                        v.wait_ge(dXp[par].h, x_ld)
                        if ofree:
                            v.wait_ge(dOp[par].h, ofree)
                        sV.inc(v.tensor_scalar(scF[:, 4:5], mvF[:, 0:1], scF[:, 3:4], -1.0, op0=ALU.mult, op1=ALU.mult))
                        v.wait_ge(sV.h, sV.n)
                        sV.inc(v.tensor_scalar(ln5_t[:, 0:512], pB[:, 0:512], scF[:, 3:4], scF[:, 4:5], op0=ALU.mult, op1=ALU.add))
                        sV.inc(v.tensor_scalar(ln5_t[:, 512:1024], pC[:, 0:512], scF[:, 3:4], scF[:, 4:5], op0=ALU.mult, op1=ALU.add))
                        v.wait_ge(sV.h, sV.n)
                        sV.inc(v.tensor_tensor(out_t[par][:], x2f[par][:], ln5_t[:], op=ALU.add))
                    V(_)
                    v_out = sV.n
                    ftrk["pBC_F3"] = v_out
                    ftrk[f"x2r{par}"] = v_out

                    def _(q, b=b, tt=tt, par=par, v_out=v_out):
                        q.wait_ge(sV.h, v_out)
                        dOp[par].inc(q.dma_start(out=out_tile_v[b, tt], in_=out_t[par][:]), 16)
                    Q(_)
                    ftrk[f"ostore{par}"] = dOp[par].n

            emit_F1(0)
            emit_F1(1)
            esFd.close()
            cb_sm = esF.enter_context(nc.sbuf_tensor("cb_sm", [P, 256], BF16))
            ln5_t = esF.enter_context(nc.sbuf_tensor("ln5_t", [P, D], F32))
            out_t = [esF.enter_context(nc.sbuf_tensor(f"outt{i}", [P, D], F32)) for i in range(2)]
            emit_experts()
            emit_F3(0)
            emit_F3(1)
            esF.close()

        x2_done = (dSp[0].n, dSp[1].n)

        def _(q):
            if x2_done[0]:
                q.wait_ge(dSp[0].h, x2_done[0])
            if x2_done[1]:
                q.wait_ge(dSp[1].h, x2_done[1])
            if "B" in phases and "F" not in phases:
                for b in range(2):
                    dO.inc(q.dma_start(out=out_d[b], in_=x2scr[b]), 16)
            if dO.n:
                q.wait_ge(dO.h, dO.n)
            if dOp[0].n:
                q.wait_ge(dOp[0].h, dOp[0].n)
            if dOp[1].n:
                q.wait_ge(dOp[1].h, dOp[1].n)
            if dDbg.n:
                q.wait_ge(dDbg.h, dDbg.n)
        Q(_)

    f = nc.m.functions[0]
    blocks = f.blocks
    if len(blocks) > 2:
        def ekey(bb):
            engs = {str(getattr(ins, "engine", None)) for ins in bb.instructions}
            return sorted(engs)[0] if engs else "zz"
        f.blocks = [blocks[0]] + sorted(blocks[1:-1], key=ekey) + [blocks[-1]]
    return nc, dbg_outs


def _host_prep(inp):
    f = np.float32
    Wq, Wk, Wv, Wo = (np.asarray(inp[k], f) for k in ("L0_attn_Wq", "L0_attn_Wk", "L0_attn_Wv", "L0_attn_Wo"))

    def wlay(W):
        return np.ascontiguousarray(
            W.T.reshape(8, 128, 8, 128).transpose(1, 0, 2, 3).reshape(128, 8192))

    planes_T = np.asarray(inp["L0_attn_planes_T"], f)
    negA = np.zeros((64, 1024), f)
    for j in range(64):
        negA[j, j * 16:(j + 1) * 16] = -1.0
    negB = np.zeros((8, 1024), f)
    for j in range(8):
        negB[j, j * 128:(j + 1) * 128] = -1.0
    mu = np.asarray(inp["L0_ffn_mu"], f).reshape(1024, 256)
    w1 = np.asarray(inp["L0_ffn_w1"], f)
    w2 = np.asarray(inp["L0_ffn_w2"], f)
    b1 = np.asarray(inp["L0_ffn_b1"], f)
    com = {
        "wq": wlay(Wq), "wk": wlay(Wk), "wv": wlay(Wv), "wo": wlay(Wo),
        "bo": np.asarray(inp["L0_attn_bo"], f).reshape(1, 1024),
        "planes2": np.concatenate([planes_T, planes_T], 0),
        "protos": np.asarray(inp["L0_attn_protos_T"], f),
        "ltemp": np.asarray(inp["L0_attn_logit_temp"], f).reshape(1, 1),
        "negindA": negA, "negindB": negB,
        "ident": np.eye(128, dtype=f),
        "ones1": np.ones((1, 128), f),
        "onescol": np.ones((128, 1), f),
        "nh8": np.full((128, 8), -0.5, f),
        "nh1": np.full((128, 1), -0.5, f),
        "ph1": np.full((128, 1), 0.5, f),
        "mul": np.ascontiguousarray(mu.reshape(8, 128, 256).transpose(1, 0, 2).reshape(128, 2048)),
        "gam": np.ascontiguousarray(np.broadcast_to(np.asarray(inp["L0_ffn_ln_gamma"], f), (128, 1024))),
        "bet": np.ascontiguousarray(np.broadcast_to(np.asarray(inp["L0_ffn_ln_beta"], f), (128, 1024))),
        "scl": np.ascontiguousarray(np.broadcast_to(np.asarray(inp["L0_ffn_scale"], f).reshape(1, 1), (128, 1))),
        "w1l": np.ascontiguousarray(w1.reshape(E, 8, 128, 2, 128).transpose(0, 2, 1, 3, 4).reshape(E, 128, 2048)),
        "w2l": np.ascontiguousarray(w2.reshape(E, 2, 128, 2, 512).transpose(0, 2, 1, 3, 4).reshape(E, 128, 2048)),
        "b1f": np.ascontiguousarray(b1.reshape(16, 2, 128).transpose(2, 0, 1).reshape(128, 32)),
        "b2r": np.ascontiguousarray(np.repeat(np.asarray(inp["L0_ffn_b2"], f), 16, axis=0)),
    }
    return com


def kernel(**inputs):
    x = np.ascontiguousarray(np.asarray(inputs["L0_attn_x"], np.float32))
    com = _host_prep(inputs)
    nc, _ = build_program(DBG_NAMES)
    in_maps = []
    for c in range(8):
        m = dict(com)
        m["x"] = np.ascontiguousarray(x[2 * c:2 * c + 2])
        in_maps.append(m)
    res = run_bass_kernel_spmd(nc, in_maps, core_ids=list(range(8)))
    return np.concatenate([r["out"] for r in res.results], axis=0)


# version 04
# speedup vs baseline: 1.4604x; 1.1718x over previous
import numpy as np
from contextlib import ExitStack

import concourse.bass as bass
import concourse.mybir as mybir
from concourse.bass_utils import run_bass_kernel_spmd

F32 = mybir.dt.float32
BF16 = mybir.dt.bfloat16
AF = mybir.ActivationFunctionType
ALU = mybir.AluOpType
AX = mybir.AxisListType

P = 128
D = 1024
H = 16
HD = 64
NT = 16
HALF = 8
SB = 128
LTAB = 8
RB = 16
E = 16
ES = 256
HFF = 256
EPS = 1e-5

DBG_NAMES = ()
A_CFG = dict(bs=(0, 1), halves=(0, 1), tiles=8, buckets=True, stages=99, sub9=99)


class Sm:
    def __init__(self, es, nc, name):
        self.h = es.enter_context(nc.semaphore(name))
        self.n = 0

    def inc(self, instr, k=1):
        instr.then_inc(self.h, k)
        self.n += k
        return self.n


def build_program(dbg_names=(), phases=("A", "B", "F")):
    nc = bass.Bass("TRN2", target_bir_lowering=False, detect_race_conditions=False)
    dbg_outs = {}

    x_d = nc.dram_tensor("x", [2, 2048, D], F32, kind="ExternalInput").ap()
    wq_d = nc.dram_tensor("wq", [128, 8192], F32, kind="ExternalInput").ap()
    wk_d = nc.dram_tensor("wk", [128, 8192], F32, kind="ExternalInput").ap()
    wv_d = nc.dram_tensor("wv", [128, 8192], F32, kind="ExternalInput").ap()
    wo_d = nc.dram_tensor("wo", [128, 8192], F32, kind="ExternalInput").ap()
    bo_d = nc.dram_tensor("bo", [1, D], F32, kind="ExternalInput").ap()
    planes_d = nc.dram_tensor("planes2", [128, 32], F32, kind="ExternalInput").ap()
    protos_d = nc.dram_tensor("protos", [4, 16], F32, kind="ExternalInput").ap()
    ltemp_d = nc.dram_tensor("ltemp", [1, 1], F32, kind="ExternalInput").ap()
    negA_d = nc.dram_tensor("negindA", [64, 1024], F32, kind="ExternalInput").ap()
    negB_d = nc.dram_tensor("negindB", [8, 1024], F32, kind="ExternalInput").ap()
    ident_d = nc.dram_tensor("ident", [128, 128], F32, kind="ExternalInput").ap()
    ones1_d = nc.dram_tensor("ones1", [1, 128], F32, kind="ExternalInput").ap()
    onescol_d = nc.dram_tensor("onescol", [128, 1], F32, kind="ExternalInput").ap()
    nh8_d = nc.dram_tensor("nh8", [128, 8], F32, kind="ExternalInput").ap()
    nh1_d = nc.dram_tensor("nh1", [128, 1], F32, kind="ExternalInput").ap()
    ph1_d = nc.dram_tensor("ph1", [128, 1], F32, kind="ExternalInput").ap()
    mu_d = nc.dram_tensor("mul", [128, 2048], F32, kind="ExternalInput").ap()
    gam_d = nc.dram_tensor("gam", [128, D], F32, kind="ExternalInput").ap()
    bet_d = nc.dram_tensor("bet", [128, D], F32, kind="ExternalInput").ap()
    scl_d = nc.dram_tensor("scl", [128, 1], F32, kind="ExternalInput").ap()
    w1_d = nc.dram_tensor("w1l", [E, 128, 2048], F32, kind="ExternalInput").ap()
    w2_d = nc.dram_tensor("w2l", [E, 128, 2048], F32, kind="ExternalInput").ap()
    b1_d = nc.dram_tensor("b1f", [128, 32], F32, kind="ExternalInput").ap()
    b2_d = nc.dram_tensor("b2r", [256, D], F32, kind="ExternalInput").ap()

    out_d = nc.dram_tensor("out", [2, 2048, D], F32, kind="ExternalOutput").ap()
    x2scr = nc.dram_tensor("x2scr", [2, 2048, D], F32, kind="Internal").ap()

    x_half_v = x_d.rearrange("b (hh t p) d -> b hh p t d", hh=2, t=HALF, p=P)
    x_tile_v = x_d.rearrange("b (t p) d -> b t p d", t=NT, p=P)
    x2_tile_v = x2scr.rearrange("b (t p) d -> b t p d", t=NT, p=P)
    out_tile_v = out_d.rearrange("b (t p) d -> b t p d", t=NT, p=P)

    es = ExitStack()
    with es:
        sP = Sm(es, nc, "sP")
        sV = Sm(es, nc, "sV")
        sA = Sm(es, nc, "sA")
        sG = Sm(es, nc, "sG")
        dXA = Sm(es, nc, "dXA")
        dXp = [Sm(es, nc, "dX0"), Sm(es, nc, "dX1")]
        dCon = Sm(es, nc, "dCon")
        dPro = Sm(es, nc, "dPro")
        dWkv = Sm(es, nc, "dWkv")
        dWqo = Sm(es, nc, "dWqo")
        dE = [Sm(es, nc, "dE0"), Sm(es, nc, "dE1")]
        dSp = [Sm(es, nc, "dS0"), Sm(es, nc, "dS1")]
        dSh = [Sm(es, nc, "dSh0"), Sm(es, nc, "dSh1")]
        dO = Sm(es, nc, "dO")
        dOp = [Sm(es, nc, "dO0"), Sm(es, nc, "dO1")]
        dDbg = Sm(es, nc, "dDbg")

        ptrb_t = es.enter_context(nc.psum_tensor("ptrb", [P, 2048], BF16))
        ptrb = ptrb_t.ap()[:, 0:1024]
        ptrbB = ptrb_t.ap()[:, 1024:2048]
        pA = es.enter_context(nc.psum_tensor("pA", [P, 512], F32))
        pB = es.enter_context(nc.psum_tensor("pB", [P, 512], F32))
        pC = es.enter_context(nc.psum_tensor("pC", [P, 512], F32))
        pD = es.enter_context(nc.psum_tensor("pD", [P, 512], F32))
        pE = es.enter_context(nc.psum_tensor("pE", [P, 1024], F32))

        identb = es.enter_context(nc.sbuf_tensor("identb", [128, 128], BF16))
        identf = es.enter_context(nc.sbuf_tensor("identf", [128, 128], F32))
        onesb = es.enter_context(nc.sbuf_tensor("onesb", [1, 128], BF16))
        ones1f = es.enter_context(nc.sbuf_tensor("ones1f", [1, 128], F32))
        onescolb = es.enter_context(nc.sbuf_tensor("onescolb", [128, 1], BF16))
        negA = es.enter_context(nc.sbuf_tensor("negA", [64, 1024], BF16))
        negB = es.enter_context(nc.sbuf_tensor("negB", [8, 1024], BF16))
        planes = es.enter_context(nc.sbuf_tensor("planes", [128, 32], BF16))
        PbigF = es.enter_context(nc.sbuf_tensor("PbigF", [32, 128], F32))
        PbigS = es.enter_context(nc.sbuf_tensor("PbigS", [128, 128], BF16))
        bo_s = es.enter_context(nc.sbuf_tensor("bo_s", [1, D], BF16))
        lt_s = es.enter_context(nc.sbuf_tensor("lt_s", [1, 8], F32))
        nh8_s = es.enter_context(nc.sbuf_tensor("nh8_s", [128, 8], F32))
        nh1_s = es.enter_context(nc.sbuf_tensor("nh1_s", [128, 1], F32))
        ph1_s = es.enter_context(nc.sbuf_tensor("ph1_s", [128, 1], F32))
        invs32 = es.enter_context(nc.sbuf_tensor("invs32", [32, 1], F32))

        block = es.enter_context(nc.Block())
        T = block.tensor
        V = block.vector
        A = block.scalar
        G = block.gpsimd
        Q = block.sync

        def dump(name, ap, sem, val):
            if name not in dbg_names:
                return
            dd = nc.dram_tensor("dbg_" + name, list(ap.shape), F32, kind="ExternalOutput").ap()
            dbg_outs[name] = dd

            def _(g):
                g.wait_ge(sem.h, val)
                dDbg.inc(g.dma_start(out=dd, in_=ap), 16)
            G(_)

        def _(g):
            g.dma_start(out=identb[:], in_=ident_d).then_inc(dCon.h, 16)
            g.dma_start(out=identf[:], in_=ident_d).then_inc(dCon.h, 16)
            g.dma_start(out=onesb[:], in_=ones1_d).then_inc(dCon.h, 16)
            g.dma_start(out=ones1f[:], in_=ones1_d).then_inc(dCon.h, 16)
            g.dma_start(out=onescolb[:], in_=onescol_d).then_inc(dCon.h, 16)
            g.dma_start(out=negA[:], in_=negA_d).then_inc(dCon.h, 16)
            g.dma_start(out=negB[:], in_=negB_d).then_inc(dCon.h, 16)
            g.dma_start(out=planes[:], in_=planes_d).then_inc(dCon.h, 16)
            g.dma_start(out=bo_s[:], in_=bo_d).then_inc(dCon.h, 16)
            g.dma_start(out=lt_s[0:1, 0:1], in_=ltemp_d).then_inc(dCon.h, 16)
            g.dma_start(out=nh8_s[:], in_=nh8_d).then_inc(dCon.h, 16)
            g.dma_start(out=nh1_s[:], in_=nh1_d).then_inc(dCon.h, 16)
            g.dma_start(out=ph1_s[:], in_=ph1_d).then_inc(dCon.h, 16)
            dCon.n += 208
        G(_)
        w_pre = dCon.n

        def _(v):
            sV.inc(v.memset(PbigF[:], 0.0))
        V(_)
        v_pz = sV.n

        def _(g):
            g.wait_ge(sV.h, v_pz)
            for l in range(LTAB):
                g.dma_start(out=PbigF[4 * l:4 * l + 4, 16 * l:16 * l + 16], in_=protos_d).then_inc(dPro.h, 16)
            dPro.n += 16 * LTAB
        G(_)
        w_pbig = dPro.n

        def _(s):
            s.wait_ge(dCon.h, w_pre)
            sA.inc(s.activation(lt_s[0:1, 1:2], lt_s[0:1, 0:1], AF.Exp))
        A(_)
        a_e = sA.n

        def _(v):
            v.wait_ge(sA.h, a_e)
            sV.inc(v.tensor_scalar(lt_s[0:1, 2:3], lt_s[0:1, 1:2], 20.0, 0.01, op0=ALU.min, op1=ALU.max))
            v.wait_ge(sV.h, sV.n)
            sV.inc(v.reciprocal(lt_s[0:1, 3:4], lt_s[0:1, 2:3]))
        V(_)
        v_invs = sV.n

        def _(t):
            t.wait_ge(sV.h, v_invs)
            sP.inc(t.matmul(pD[0:32, 0:1], ones1f[0:1, 0:32], lt_s[0:1, 3:4], start=True, stop=True))
        T(_)
        p_inv = sP.n

        def _(s):
            s.wait_ge(sP.h, p_inv)
            sA.inc(s.copy(invs32[:], pD[0:32, 0:1]))
        A(_)
        a_inv = sA.n

        def _(v):
            v.wait_ge(sA.h, a_inv)
            v.wait_ge(dPro.h, w_pbig)
            sV.inc(v.tensor_scalar(PbigS[0:32, :], PbigF[:], invs32[:], None, op0=ALU.mult))
            sV.inc(v.tensor_scalar(PbigS[64:96, :], PbigF[:], invs32[:], None, op0=ALU.mult))
        V(_)
        v_pbig = sV.n
        dump("PbigS", PbigS[0:32, :], sV, v_pbig)

        wq_s = es.enter_context(nc.sbuf_tensor("wq_s", [128, 8192], BF16))
        wo_s = es.enter_context(nc.sbuf_tensor("wo_s", [128, 8192], BF16))

        def _(g):
            g.dma_start(out=wq_s[:], in_=wq_d).then_inc(dWqo.h, 16)
            g.dma_start(out=wo_s[:], in_=wo_d).then_inc(dWqo.h, 16)
            dWqo.n += 32
        G(_)

        stats = es.enter_context(nc.sbuf_tensor("stats", [P, 64], F32))
        negm1 = es.enter_context(nc.sbuf_tensor("negm1", [P, 32], F32))
        r1s = es.enter_context(nc.sbuf_tensor("r1s", [P, 32], F32))

        esAB = es.enter_context(ExitStack())
        bKTa = [esAB.enter_context(nc.sbuf_tensor(f"bKTa{b}", [64, 2048], F32)) for b in range(2)]
        bVa = [esAB.enter_context(nc.sbuf_tensor(f"bVa{b}", [P, 1024], F32)) for b in range(2)]

        esA = ExitStack()
        xn1 = esA.enter_context(nc.sbuf_tensor("xn1", [P, D], BF16))
        xn1f = esA.enter_context(nc.sbuf_tensor("xn1f", [P, D], BF16))
        wk_s = esA.enter_context(nc.sbuf_tensor("wk_s", [128, 8192], BF16))
        wv_s = esA.enter_context(nc.sbuf_tensor("wv_s", [128, 8192], BF16))
        x_sb = esA.enter_context(nc.sbuf_tensor("x_sb", [P, HALF, D], F32))
        Ktm = esA.enter_context(nc.sbuf_tensor("Ktm", [P, HALF, D], BF16))
        Vtm = esA.enter_context(nc.sbuf_tensor("Vtm", [P, HALF, D], BF16))
        probs = esA.enter_context(nc.sbuf_tensor("probs", [P, HALF, 2048], BF16))
        bnbuf = esA.enter_context(nc.sbuf_tensor("bnbuf", [P, 96], F32))
        tsb = esA.enter_context(nc.sbuf_tensor("tsb", [P, 8], F32))
        tsb2 = esA.enter_context(nc.sbuf_tensor("tsb2", [P, 8], F32))
        Kfm = esA.enter_context(nc.sbuf_tensor("Kfm", [P, D], BF16))
        tanhp = esA.enter_context(nc.sbuf_tensor("tanhp", [P, D], BF16))
        nmaxA = esA.enter_context(nc.sbuf_tensor("nmaxA", [P, 64], F32))
        cTA = esA.enter_context(nc.sbuf_tensor("cTA", [64, 128], BF16))
        expA = esA.enter_context(nc.sbuf_tensor("expA", [P, 1024], F32))
        ZA = esA.enter_context(nc.sbuf_tensor("ZA", [P, 64], F32))
        RA = esA.enter_context(nc.sbuf_tensor("RA", [P, 64], F32))

        def _(g):
            g.dma_start(out=wk_s[:], in_=wk_d).then_inc(dWkv.h, 16)
            g.dma_start(out=wv_s[:], in_=wv_d).then_inc(dWkv.h, 16)
            dWkv.n += 32
        G(_)
        w_attn = dWkv.n

        trk = {
            "x_free": 0,
            "ptrb_free": 0,
            "ptrf_free": 0,
            "pAB_free_pe": 0,
            "pCD_free": 0,
            "pE_free": 0,
            "prod_free_pe": 0,
            "bk_ready": 0,
            "x2_store": 0,
        }

        def emit_A(b):
            for half in A_CFG["halves"]:
                g0 = half * HALF
                x_free = trk["x_free"]
                prod_free = trk["prod_free_pe"]

                def _(q):
                    if x_free:
                        q.wait_ge(sV.h, x_free)
                    dXA.inc(q.dma_start(out=x_sb[:], in_=x_half_v[b, half]), 16)
                Q(_)
                x_ld = dXA.n

                def _(v):
                    v.wait_ge(dXA.h, x_ld)
                    for tt in range(HALF):
                        sV.inc(v.bn_stats(bnbuf[:, tt * 12:tt * 12 + 6], x_sb[:, tt, 0:512]))
                        sV.inc(v.bn_stats(bnbuf[:, tt * 12 + 6:tt * 12 + 12], x_sb[:, tt, 512:1024]))
                    v.wait_ge(sV.h, sV.n)
                    for tt in range(HALF):
                        col = b * 32 + (g0 + tt) * 2
                        sV.inc(v.bn_aggr(stats[:, col:col + 2], bnbuf[:, tt * 12:tt * 12 + 12]))
                    v.wait_ge(sV.h, sV.n)
                    vsl = stats[:, b * 32 + g0 * 2 + 1: b * 32 + (g0 + HALF) * 2: 2]
                    msl = stats[:, b * 32 + g0 * 2: b * 32 + (g0 + HALF) * 2: 2]
                    sV.inc(v.tensor_scalar(tsb[:], vsl, EPS, None, op0=ALU.add))
                    sV.inc(v.tensor_scalar(negm1[:, b * 16 + g0: b * 16 + g0 + HALF], msl, -1.0, None, op0=ALU.mult))
                V(_)
                v_stats = sV.n
                dump(f"xsb_b{b}h{half}", x_sb[:, 0, :], dXA, x_ld)
                dump(f"stats_b{b}h{half}", stats[:, b * 32 + g0 * 2: b * 32 + (g0 + HALF) * 2], sV, v_stats)

                def _(g):
                    g.wait_ge(sV.h, v_stats)
                    g.wait_ge(dCon.h, 208)
                    sG.inc(g.tensor_tensor(r1s[:, b * 16 + g0: b * 16 + g0 + HALF], tsb[:], nh8_s[:], op=ALU.pow))
                G(_)
                g_r1 = sG.n
                dump(f"r1s_b{b}h{half}", r1s[:, b * 16 + g0:b * 16 + g0 + HALF], sG, g_r1)

                for tt in range(A_CFG["tiles"]):
                    gt = g0 + tt
                    ncol = b * 16 + gt

                    p_xn1T_prev = trk.get("p_xn1T", 0)

                    def _(v, tt=tt, ncol=ncol, p_xn1T_prev=p_xn1T_prev, g_r1=g_r1):
                        v.wait_ge(sG.h, g_r1)
                        if p_xn1T_prev:
                            v.wait_ge(sP.h, p_xn1T_prev)
                        sV.inc(v.tensor_scalar(xn1[:], x_sb[:, tt, :], negm1[:, ncol:ncol + 1],
                                               r1s[:, ncol:ncol + 1], op0=ALU.add, op1=ALU.mult))
                    V(_)
                    v_xn1 = sV.n

                    ptrb_free = trk["ptrb_free"]

                    def _(t, v_xn1=v_xn1, ptrb_free=ptrb_free, prod_free=prod_free):
                        t.wait_ge(sV.h, v_xn1)
                        t.wait_ge(dCon.h, 208)
                        t.wait_ge(dWkv.h, 32)
                        if ptrb_free:
                            t.wait_ge(sA.h, ptrb_free)
                        for kc in range(8):
                            ii = t.transpose(ptrb[:, kc * 128:kc * 128 + 128], xn1[:, kc * 128:kc * 128 + 128], identb[:])
                        sP.inc(ii)
                    T(_)
                    trk["p_xn1T"] = sP.n
                    p_xn1T = sP.n

                    def _(s, p_xn1T=p_xn1T):
                        s.wait_ge(sP.h, p_xn1T)
                        sA.inc(s.copy(xn1f[:], ptrb[:]))
                    A(_)
                    a_xn1f = sA.n

                    pab_free = trk["pAB_free_pe"]
                    pa_ct = trk.get("pA_ct", 0)

                    def _(t, a_xn1f=a_xn1f, pab_free=pab_free, pa_ct=pa_ct):
                        t.wait_ge(sA.h, a_xn1f)
                        if pab_free:
                            t.wait_ge(sA.h, pab_free)
                        if pa_ct:
                            t.wait_ge(sA.h, pa_ct)
                        for kc in range(8):
                            for oc in range(8):
                                bank = pA if oc < 4 else pB
                                ii = t.matmul(bank[:, (oc % 4) * 128:(oc % 4) * 128 + 128],
                                              wk_s[:, kc * 1024 + oc * 128:kc * 1024 + oc * 128 + 128],
                                              xn1f[:, kc * 128:kc * 128 + 128],
                                              start=(kc == 0 and oc % 4 == 0),
                                              stop=(kc == 7 and oc % 4 == 3),
                                              skip_group_check=True)
                        sP.inc(ii)
                    T(_)
                    p_kfm = sP.n

                    def _(s, p_kfm=p_kfm):
                        s.wait_ge(sP.h, p_kfm)
                        sA.inc(s.copy(Kfm[:, 0:512], pA[:]))
                        sA.inc(s.copy(Kfm[:, 512:1024], pB[:]))
                    A(_)
                    a_kfm = sA.n

                    def _(t, a_kfm=a_kfm):
                        t.wait_ge(sA.h, a_kfm)
                        for oc in range(8):
                            ii = t.transpose(ptrb[:, oc * 128:oc * 128 + 128], Kfm[:, oc * 128:oc * 128 + 128], identb[:])
                        sP.inc(ii)
                    T(_)
                    p_ktT = sP.n

                    def _(s, p_ktT=p_ktT, tt=tt):
                        s.wait_ge(sP.h, p_ktT)
                        sA.inc(s.copy(Ktm[:, tt, :], ptrb[:]))
                    A(_)
                    a_ktm = sA.n
                    trk["ptrb_free"] = a_ktm

                    def _(t, a_kfm=a_kfm, v_pbig=v_pbig):
                        t.wait_ge(sV.h, v_pbig)
                        t.wait_ge(sA.h, a_kfm)
                        for h in range(H):
                            bank = pA if h < 8 else pB
                            base = 64 * (h % 2)
                            cs = (h % 8) // 2
                            ii = t.matmul(bank[base:base + 32, cs * 128:cs * 128 + 128],
                                          planes[base:base + 64, :],
                                          Kfm[base:base + 64, (h // 2) * 128:(h // 2) * 128 + 128],
                                          start=(h % 8 in (0, 1)), stop=(h % 8 in (6, 7)),
                                          skip_group_check=True)
                        sP.inc(ii)
                    T(_)
                    p_pj = sP.n

                    def _(s, p_pj=p_pj):
                        s.wait_ge(sP.h, p_pj)
                        sA.inc(s.activation(tanhp[:, 0:512], pA[:], AF.Tanh))
                        sA.inc(s.activation(tanhp[:, 512:1024], pB[:], AF.Tanh))
                    A(_)
                    a_tanh = sA.n
                    trk["pAB_free_pe"] = a_tanh

                    pcd_free = trk["pCD_free"]

                    def _(t, a_xn1f=a_xn1f, pcd_free=pcd_free):
                        t.wait_ge(sA.h, a_xn1f)
                        if pcd_free:
                            t.wait_ge(sA.h, pcd_free)
                        for dh in range(2):
                            bank = pC if dh == 0 else pD
                            for kc in range(8):
                                ii = t.matmul(bank[:, 0:512],
                                              xn1f[:, kc * 128:kc * 128 + 128],
                                              wv_s[:, kc * 1024 + dh * 512:kc * 1024 + dh * 512 + 512],
                                              start=(kc == 0), stop=(kc == 7))
                        sP.inc(ii)
                    T(_)
                    p_v = sP.n

                    def _(s, p_v=p_v, tt=tt):
                        s.wait_ge(sP.h, p_v)
                        sA.inc(s.copy(Vtm[:, tt, 0:512], pC[:]))
                        sA.inc(s.copy(Vtm[:, tt, 512:1024], pD[:]))
                    A(_)
                    a_v = sA.n
                    trk["pCD_free"] = a_v

                    for grp in range(2):
                        pE_free = trk["pE_free"]

                        def _(t, grp=grp, a_tanh=a_tanh, pE_free=pE_free):
                            t.wait_ge(sA.h, a_tanh)
                            if pE_free:
                                t.wait_ge(sV.h, pE_free)
                            for hl in range(8):
                                h = grp * 8 + hl
                                base = 64 * (h % 2)
                                col = (h // 8) * 512 + ((h % 8) // 2) * 128
                                oc = (hl // 2) * 128 + (hl % 2) * 512
                                ii = t.matmul(pE[:, oc:oc + 128],
                                              tanhp[base:base + 32, col:col + 128],
                                              PbigS[base:base + 32, :],
                                              start=(hl in (0, 1)), stop=(hl in (6, 7)),
                                              skip_group_check=True)
                            sP.inc(ii)
                        T(_)
                        p_lg = sP.n

                        def _(v, p_lg=p_lg):
                            v.wait_ge(sP.h, p_lg)
                            sV.inc(v.tensor_reduce(nmaxA[:], pE[:].rearrange("p (s r) -> p s r", r=RB),
                                                   axis=AX.X, op=ALU.max, negate=False))
                        V(_)
                        v_nm = sV.n

                        ptrf_free = trk["ptrf_free"]

                        def _(t, v_nm=v_nm, ptrf_free=ptrf_free):
                            t.wait_ge(sV.h, v_nm)
                            if ptrf_free:
                                t.wait_ge(sA.h, ptrf_free)
                            sP.inc(t.transpose(pA[0:64, 256:384], nmaxA[:], identf[:]))
                        T(_)
                        p_nt = sP.n

                        def _(s, p_nt=p_nt):
                            s.wait_ge(sP.h, p_nt)
                            sA.inc(s.copy(cTA[:], pA[0:64, 256:384]))
                        A(_)
                        a_ct = sA.n
                        trk["ptrf_free"] = a_ct
                        trk["pA_ct"] = a_ct

                        def _(t, a_ct=a_ct):
                            t.wait_ge(sA.h, a_ct)
                            for j in range(2):
                                ii = t.matmul(pE[:, j * 512:j * 512 + 512], cTA[:], negA[:, j * 512:j * 512 + 512],
                                              start=False, stop=True, skip_group_check=True)
                            sP.inc(ii)
                        T(_)
                        p_bm = sP.n

                        def _(s, p_bm=p_bm):
                            s.wait_ge(sP.h, p_bm)
                            sA.inc(s.activation(expA[:], pE[:], AF.Exp))
                        A(_)
                        a_exp = sA.n

                        def _(v, a_exp=a_exp):
                            v.wait_ge(sA.h, a_exp)
                            sV.inc(v.tensor_reduce(ZA[:], expA[:].rearrange("p (s r) -> p s r", r=RB),
                                                   axis=AX.X, op=ALU.add))
                            v.wait_ge(sV.h, sV.n)
                            sV.inc(v.reciprocal(RA[:], ZA[:]))
                        V(_)
                        v_r = sV.n

                        def _(v, v_r=v_r, tt=tt, grp=grp):
                            v.wait_ge(sV.h, v_r)
                            dst = probs[:, tt, grp * 1024:grp * 1024 + 1024].rearrange("p (s r) -> p s r", r=RB)
                            src = expA[:].rearrange("p (s r) -> p s r", r=RB)
                            bc = RA[:].unsqueeze(2).broadcast_to([P, 64, RB])
                            sV.inc(v.tensor_tensor(dst, src, bc, op=ALU.mult))
                        V(_)
                        trk["pE_free"] = sV.n
                        v_pm = sV.n

                    if b == 0 and half == 0 and tt == 0 and dbg_names:
                        dump("xn1_t0", xn1[:], sV, v_xn1)
                        dump("xn1f_t0", xn1f[:], sA, a_xn1f)
                        dump("Kfm_t0", Kfm[:], sA, a_kfm)
                        dump("tanhp_t0", tanhp[:], sA, a_tanh)
                        dump("expA_t0", expA[:], sA, a_exp)
                        dump("probs_t0", probs[:, 0, :], sV, v_pm)
                        dump("Ktm_t0", Ktm[:, 0, :], sA, a_ktm)
                        dump("Vtm_t0", Vtm[:, 0, :], sA, a_v)

                trk["x_free"] = sV.n

                a_tanh_last = trk["pAB_free_pe"]
                a_v_last = trk["pCD_free"]
                v_prod = sV.n
                a_prod = sA.n
                prev_bk = 0
                for h in (range(H) if A_CFG["buckets"] else ()):
                    kb = pA if h % 2 == 0 else pC
                    vb = pB if h % 2 == 0 else pD

                    def _(t, h=h, kb=kb, vb=vb, v_prod=v_prod, a_prod=a_prod, prev_bk=prev_bk):
                        if h == 0:
                            t.wait_ge(sV.h, v_prod)
                            t.wait_ge(sA.h, a_prod)
                        if prev_bk:
                            t.wait_ge(sV.h, prev_bk)
                        pcol = (h // 8) * 1024 + ((h % 8) // 2) * 128 + (h % 2) * 512
                        for tt in range(HALF):
                            t.matmul(kb[0:64, 0:128], Ktm[:, tt, h * 64:h * 64 + 64],
                                     probs[:, tt, pcol:pcol + 128],
                                     start=(tt == 0), stop=(tt == HALF - 1), skip_group_check=True)
                        for tt in range(HALF):
                            ii = t.matmul(vb[0:128, 0:64], probs[:, tt, pcol:pcol + 128],
                                          Vtm[:, tt, h * 64:h * 64 + 64],
                                          start=(tt == 0), stop=(tt == HALF - 1), skip_group_check=True)
                        sP.inc(ii)
                    T(_)
                    p_bk = sP.n

                    def _(v, h=h, kb=kb, vb=vb, half=half, p_bk=p_bk, b=b):
                        v.wait_ge(sP.h, p_bk)
                        if half == 0:
                            sV.inc(v.tensor_copy(bKTa[b][:, h * 128:h * 128 + 128], kb[0:64, 0:128]))
                            sV.inc(v.tensor_copy(bVa[b][:, h * 64:h * 64 + 64], vb[0:128, 0:64]))
                        else:
                            sV.inc(v.tensor_tensor(bKTa[b][:, h * 128:h * 128 + 128], bKTa[b][:, h * 128:h * 128 + 128],
                                                   kb[0:64, 0:128], op=ALU.add))
                            sV.inc(v.tensor_tensor(bVa[b][:, h * 64:h * 64 + 64], bVa[b][:, h * 64:h * 64 + 64],
                                                   vb[0:128, 0:64], op=ALU.add))
                    V(_)
                    prev_bk = sV.n
                trk["pAB_free_pe"] = 0
                trk["pCD_free"] = 0
                trk["bk_evac"] = prev_bk
                trk["prod_free_pe"] = sP.n

                def _(t, prev_bk=prev_bk):
                    t.wait_ge(sV.h, prev_bk)
                    sP.inc(t.nop())
                T(_)

            trk[f"bk_evac{b}"] = trk["bk_evac"]
            dump(f"bKT_b{b}", bKTa[b][:], sV, trk["bk_evac"])
            dump(f"bV_b{b}", bVa[b][:], sV, trk["bk_evac"])

        def emit_B(b):
            def _(v, b=b):
                v.wait_ge(sV.h, trk[f"bk_evac{b}"])
                sV.inc(v.tensor_copy(bKTb[b][0:64, :], bKTa[b][:]))
                sV.inc(v.tensor_copy(bKTb[b][64:128, :], bKTa[b][:]))
                sV.inc(v.tensor_copy(bVb[b][:], bVa[b][:]))
            V(_)
            bk_ready = sV.n

            for tt in range(NT):
                par = tt % 2
                xin_free = trk.get(f"xinfree_{par}", 0)

                def _(q, b=b, tt=tt, par=par, xin_free=xin_free):
                    if xin_free:
                        q.wait_ge(sV.h, xin_free)
                    dXp[par].inc(q.dma_start(out=xin[par][:], in_=x_tile_v[b, tt]), 16)
                Q(_)
                x_ld = dXp[par].n

                p_trs_prev = trk.get(f"b0trs_{par}", 0)

                def _(v, b=b, tt=tt, par=par, x_ld=x_ld, p_trs_prev=p_trs_prev):
                    v.wait_ge(dXp[par].h, x_ld)
                    if p_trs_prev:
                        v.wait_ge(sP.h, p_trs_prev)
                    ncol = b * 16 + tt
                    sV.inc(v.tensor_scalar(xnB[par][:], xin[par][:], negm1[:, ncol:ncol + 1],
                                           r1s[:, ncol:ncol + 1], op0=ALU.add, op1=ALU.mult))
                V(_)
                v_xn1 = sV.n
                trk[f"xinfree_{par}"] = v_xn1

                pt = ptrb if par == 0 else ptrbB
                a_evac_prev = trk.get(f"b0ev_{par}", 0)

                def _(t, par=par, pt=pt, v_xn1=v_xn1, a_evac_prev=a_evac_prev):
                    t.wait_ge(sV.h, v_xn1)
                    if a_evac_prev:
                        t.wait_ge(sA.h, a_evac_prev)
                    for kc in range(8):
                        ii = t.transpose(pt[:, kc * 128:kc * 128 + 128], xnB[par][:, kc * 128:kc * 128 + 128], identb[:])
                    sP.inc(ii)
                T(_)
                trk[f"b0trs_{par}"] = sP.n
                p_trs = sP.n

                def _(s, tt=tt, pt=pt, p_trs=p_trs):
                    s.wait_ge(sP.h, p_trs)
                    sA.inc(s.copy(xn1f_all[:, tt, :], pt[:]))
                A(_)
                trk[f"b0ev_{par}"] = sA.n
            a_xn1f_all = sA.n

            qb = [pA, pB, pC, pD]
            prev_evac = [0, 0, 0, 0]
            for oc in range(8):
                def _(t, oc=oc, a_xn1f_all=a_xn1f_all, pe=tuple(prev_evac)):
                    t.wait_ge(sA.h, a_xn1f_all)
                    t.wait_ge(dWqo.h, 32)
                    for tc in range(4):
                        if pe[tc]:
                            t.wait_ge(sA.h, pe[tc])
                    for kc in range(8):
                        for tc in range(4):
                            ii = t.matmul(qb[tc][:, 0:512],
                                          wq_s[:, kc * 1024 + oc * 128:kc * 1024 + oc * 128 + 128],
                                          xn1f_all[:, tc * 4:tc * 4 + 4, kc * 128:kc * 128 + 128],
                                          start=(kc == 0), stop=(kc == 7), skip_group_check=True)
                    sP.inc(ii)
                T(_)
                p_q = sP.n

                for tc in range(4):
                    def _(s, oc=oc, tc=tc, p_q=p_q):
                        s.wait_ge(sP.h, p_q)
                        sA.inc(s.copy(Q_all[:, tc * 4:tc * 4 + 4, oc * 128:oc * 128 + 128],
                                      qb[tc][:, 0:512].rearrange("p (t c) -> p t c", c=128)))
                    A(_)
                    prev_evac[tc] = sA.n
            a_q_all = sA.n

            SCB = [(pE[:, 0:512], pE[:, 512:1024]), (pA[:, 0:512], pB[:, 0:512])]
            POB = [pC, pD]
            PTB = [ptrb, ptrbB]
            nsl = NT * 2
            st_sc = [0] * nsl
            st_ex = [0] * nsl
            st_at = [0] * nsl
            st_ae = [0] * nsl
            st_oe = [0] * nsl
            st_nm = [0] * nsl
            st_ct = [0] * nsl
            st_bm = [0] * nsl

            def SC(i):
                tt, grp = divmod(i, 2)
                par = i % 2
                banks = SCB[par]
                w_ex = st_ex[i - 2] if i >= 2 else 0
                w_oe = st_oe[i - 2] if i >= 2 else 0

                def _(t, b=b, tt=tt, grp=grp, par=par, banks=banks, w_ex=w_ex, w_oe=w_oe,
                      a_q_all=a_q_all, bk_ready=bk_ready):
                    t.wait_ge(sA.h, a_q_all)
                    t.wait_ge(sV.h, bk_ready)
                    if w_ex:
                        t.wait_ge(sA.h, w_ex)
                    if w_oe:
                        t.wait_ge(sA.h, w_oe)
                    for hl in range(8):
                        h = grp * 8 + hl
                        base = 64 * (h % 2)
                        bank = banks[hl % 2]
                        ii = t.matmul(bank[:, (hl // 2) * 128:(hl // 2) * 128 + 128],
                                      Q_all[base:base + 64, tt, (h // 2) * 128:(h // 2) * 128 + 128],
                                      bKTb[b][base:base + 64, h * 128:h * 128 + 128],
                                      start=(hl in (0, 1)), stop=(hl in (6, 7)),
                                      skip_group_check=True)
                    sP.inc(ii)
                T(_)
                st_sc[i] = sP.n

            def NM(i):
                par = i % 2
                banks = SCB[par]

                def _(v, par=par, banks=banks, w=st_sc[i]):
                    v.wait_ge(sP.h, w)
                    v.tensor_reduce(nmax2[par][:, 0:4], banks[0].rearrange("p (h s) -> p h s", s=128),
                                    axis=AX.X, op=ALU.max, negate=False)
                    sV.inc(v.tensor_reduce(nmax2[par][:, 4:8], banks[1].rearrange("p (h s) -> p h s", s=128),
                                           axis=AX.X, op=ALU.max, negate=False))
                V(_)
                st_nm[i] = sV.n

            def NTCEBM(i):
                par = i % 2
                banks = SCB[par]
                w_oe = st_oe[i - 2] if i >= 2 else 0

                def _(t, par=par, w=st_nm[i], w_oe=w_oe):
                    t.wait_ge(sV.h, w)
                    if w_oe:
                        t.wait_ge(sA.h, w_oe)
                    sP.inc(t.transpose(POB[par][0:8, 384:512], nmax2[par][:], identf[:]))
                T(_)
                p_nt = sP.n

                def _(s, par=par, p_nt=p_nt):
                    s.wait_ge(sP.h, p_nt)
                    sA.inc(s.copy(cTB2[par][:], POB[par][0:8, 384:512]))
                A(_)
                st_ct[i] = sA.n

                def _(t, par=par, banks=banks, a_ct=st_ct[i]):
                    t.wait_ge(sA.h, a_ct)
                    for j in range(2):
                        ii = t.matmul(banks[j][:, 0:512], cTB2[par][:],
                                      negB[:, j * 512:j * 512 + 512],
                                      start=False, stop=True, skip_group_check=True)
                    sP.inc(ii)
                T(_)
                st_bm[i] = sP.n

            def EXZS(i):
                par = i % 2
                banks = SCB[par]

                def _(s, par=par, banks=banks, w=st_bm[i]):
                    s.wait_ge(sP.h, w)
                    s.activation(attn2[par][:, 0:512], banks[0], AF.Exp, scale=0.125)
                    sA.inc(s.activation(attn2[par][:, 512:1024], banks[1], AF.Exp, scale=0.125))
                A(_)
                st_ex[i] = sA.n

                def _(v, par=par, w=st_ex[i]):
                    v.wait_ge(sA.h, w)
                    sV.inc(v.tensor_reduce(ZB2[par][:], attn2[par][:].rearrange("p (h s) -> p h s", s=128),
                                           axis=AX.X, op=ALU.add))
                    v.wait_ge(sV.h, sV.n)
                    sV.inc(v.reciprocal(RB2[par][:], ZB2[par][:]))
                    v.wait_ge(sV.h, sV.n)
                    sV.inc(v.tensor_tensor(attn2[par][:].rearrange("p (h s) -> p h s", s=128),
                                           attn2[par][:].rearrange("p (h s) -> p h s", s=128),
                                           RB2[par][:].unsqueeze(2).broadcast_to([P, 8, 128]), op=ALU.mult))
                V(_)
                st_at[i] = sV.n

            def ATOM(i):
                tt, grp = divmod(i, 2)
                par = i % 2
                w_ae = st_ae[i - 2] if i >= 2 else 0

                def _(t, par=par, w=st_at[i], w_ae=w_ae):
                    t.wait_ge(sV.h, w)
                    if w_ae:
                        t.wait_ge(sA.h, w_ae)
                    for hl in range(8):
                        ii = t.transpose(PTB[par][:, hl * 128:hl * 128 + 128],
                                         attn2[par][:, hl * 128:hl * 128 + 128], identb[:])
                    sP.inc(ii)
                T(_)
                p_at = sP.n

                def _(s, par=par, p_at=p_at):
                    s.wait_ge(sP.h, p_at)
                    sA.inc(s.copy(expT2[par][:], PTB[par][:]))
                A(_)
                st_ae[i] = sA.n

                def _(t, b=b, grp=grp, par=par, a_ae=st_ae[i], w_ct=st_ct[i]):
                    t.wait_ge(sA.h, a_ae)
                    for hl in range(8):
                        h = grp * 8 + hl
                        base = 64 * (h % 2)
                        cs = (h // 2) % 4
                        ec = (hl // 2) * 128 + (hl % 2) * 512
                        ii = t.matmul(POB[par][base:base + 64, cs * 128:cs * 128 + 128],
                                      bVb[b][:, h * 64:h * 64 + 64],
                                      expT2[par][:, ec:ec + 128],
                                      start=(hl in (0, 1)), stop=(hl in (6, 7)), skip_group_check=True)
                    sP.inc(ii)
                T(_)
                p_om = sP.n

                def _(s, tt=tt, grp=grp, par=par, p_om=p_om):
                    s.wait_ge(sP.h, p_om)
                    sA.inc(s.copy(aout_all[:, tt, grp * 512:grp * 512 + 512], POB[par][:, 0:512]))
                A(_)
                st_oe[i] = sA.n

            for i in range(nsl + 3):
                if 0 <= i - 2 < nsl:
                    EXZS(i - 2)
                if 0 <= i - 3 < nsl:
                    ATOM(i - 3)
                if 0 <= i - 1 < nsl:
                    NM(i - 1)
                    NTCEBM(i - 1)
                if i < nsl:
                    SC(i)
            a_aout_all = sA.n

            wob = [(pE[:, 0:512], pE[:, 512:1024]), (pA[:, 0:512], pB[:, 0:512])]
            for tt in range(NT):
                par = tt % 2
                banks = wob[par]
                w_prev = trk.get(f"b3free_{par}", 0)

                def _(t, tt=tt, banks=banks, w_prev=w_prev, a_aout_all=a_aout_all):
                    t.wait_ge(sA.h, a_aout_all)
                    if w_prev:
                        t.wait_ge(sV.h, w_prev)
                    for dh in range(2):
                        for kc in range(8):
                            t.matmul(banks[dh], aout_all[:, tt, kc * 128:kc * 128 + 128],
                                     wo_s[:, kc * 1024 + dh * 512:kc * 1024 + dh * 512 + 512],
                                     start=(kc == 0), stop=False, skip_group_check=True)
                        ii = t.matmul(banks[dh], onesb[:], bo_s[0:1, dh * 512:dh * 512 + 512],
                                      start=False, stop=True, skip_group_check=True)
                    sP.inc(ii)
                T(_)
                p_wo = sP.n

                def _(v, par=par, banks=banks, p_wo=p_wo):
                    v.wait_ge(sP.h, p_wo)
                    sV.inc(v.bn_stats(bnb2[par][:, 0:6], banks[0]))
                    sV.inc(v.bn_stats(bnb2[par][:, 6:12], banks[1]))
                    v.wait_ge(sV.h, sV.n)
                    sV.inc(v.bn_aggr(mv2[par][:], bnb2[par][:]))
                    v.wait_ge(sV.h, sV.n)
                    sV.inc(v.tensor_scalar(sc2[par][:, 0:1], mv2[par][:, 1:2], EPS, None, op0=ALU.add))
                V(_)
                v_u2 = sV.n

                def _(g, par=par, v_u2=v_u2):
                    g.wait_ge(sV.h, v_u2)
                    sG.inc(g.tensor_tensor(sc2[par][:, 3:4], sc2[par][:, 0:1], nh1_s[:], op=ALU.pow))
                G(_)
                g_r2 = sG.n

                xr_free = max(trk.get(f"xr_free_{par}", 0), trk.get(f"xinfree_{par}", 0))

                def _(q, b=b, tt=tt, par=par, xr_free=xr_free):
                    if xr_free:
                        q.wait_ge(sV.h, xr_free)
                    dXp[par].inc(q.dma_start(out=xin[par][:], in_=x_tile_v[b, tt]), 16)
                Q(_)
                xr_ld = dXp[par].n

                x2_free = trk.get(f"x2free_{par}", 0)

                def _(v, b=b, par=par, banks=banks, g_r2=g_r2, x2_free=x2_free, xr_ld=xr_ld):
                    v.wait_ge(dXp[par].h, xr_ld)
                    v.wait_ge(sG.h, g_r2)
                    if x2_free:
                        v.wait_ge(dSp[par].h, x2_free)
                    sV.inc(v.tensor_scalar(sc2[par][:, 1:2], mv2[par][:, 0:1], sc2[par][:, 3:4], -1.0,
                                           op0=ALU.mult, op1=ALU.mult))
                    v.wait_ge(sV.h, sV.n)
                    sV.inc(v.tensor_scalar(x2t[par][:, 0:512], banks[0], sc2[par][:, 3:4], sc2[par][:, 1:2],
                                           op0=ALU.mult, op1=ALU.add))
                    sV.inc(v.tensor_scalar(x2t[par][:, 512:1024], banks[1], sc2[par][:, 3:4], sc2[par][:, 1:2],
                                           op0=ALU.mult, op1=ALU.add))
                    v.wait_ge(sV.h, sV.n)
                    sV.inc(v.tensor_tensor(x2t[par][:], x2t[par][:], xin[par][:], op=ALU.add))
                V(_)
                v_x2 = sV.n
                trk[f"b3free_{par}"] = v_x2
                trk[f"xr_free_{par}"] = v_x2
                trk[f"xinfree_{par}"] = v_x2

                def _(q, b=b, tt=tt, par=par, v_x2=v_x2):
                    q.wait_ge(sV.h, v_x2)
                    dSp[par].inc(q.dma_start(out=x2_tile_v[b, tt], in_=x2t[par][:]), 16)
                Q(_)
                trk[f"x2free_{par}"] = dSp[par].n


        def barrier():
            vals = [(sP, sP.n), (sV, sV.n), (sA, sA.n), (sG, sG.n),
                    (dXA, dXA.n), (dXp[0], dXp[0].n), (dXp[1], dXp[1].n),
                    (dCon, dCon.n), (dPro, dPro.n), (dWkv, dWkv.n), (dWqo, dWqo.n),
                    (dSp[0], dSp[0].n), (dSp[1], dSp[1].n)]

            def _(e):
                for s, vv in vals:
                    if vv:
                        e.wait_ge(s.h, vv)
            for body in (T, V, A, G, Q):
                body(_)

        if "A" in phases:
            for _b in A_CFG["bs"]:
                emit_A(_b)
        barrier()
        esA.close()

        esB = ExitStack()
        xn1f_all = esB.enter_context(nc.sbuf_tensor("xn1f_all", [P, NT, D], BF16))
        Q_all = esB.enter_context(nc.sbuf_tensor("Q_all", [P, NT, D], BF16))
        aout_all = esB.enter_context(nc.sbuf_tensor("aout_all", [P, NT, D], BF16))
        xnB = [esB.enter_context(nc.sbuf_tensor(f"xnB{i}", [P, D], BF16)) for i in range(2)]
        nmax2 = [esB.enter_context(nc.sbuf_tensor(f"nmax2_{i}", [P, 8], F32)) for i in range(2)]
        cTB2 = [esB.enter_context(nc.sbuf_tensor(f"cTB2_{i}", [8, 128], BF16)) for i in range(2)]
        ZB2 = [esB.enter_context(nc.sbuf_tensor(f"ZB2_{i}", [P, 8], F32)) for i in range(2)]
        RB2 = [esB.enter_context(nc.sbuf_tensor(f"RB2_{i}", [P, 8], F32)) for i in range(2)]
        attn2 = [esB.enter_context(nc.sbuf_tensor(f"attn2_{i}", [P, 1024], BF16)) for i in range(2)]
        expT2 = [esB.enter_context(nc.sbuf_tensor(f"expT2_{i}", [P, 1024], BF16)) for i in range(2)]
        bnb2 = [esB.enter_context(nc.sbuf_tensor(f"bnb2_{i}", [P, 12], F32)) for i in range(2)]
        mv2 = [esB.enter_context(nc.sbuf_tensor(f"mv2_{i}", [P, 2], F32)) for i in range(2)]
        sc2 = [esB.enter_context(nc.sbuf_tensor(f"sc2_{i}", [P, 4], F32)) for i in range(2)]
        bKTb = [esB.enter_context(nc.sbuf_tensor(f"bKTb{b}", [P, 2048], BF16)) for b in range(2)]
        bVb = [esB.enter_context(nc.sbuf_tensor(f"bVb{b}", [P, 1024], BF16)) for b in range(2)]
        xin = [esB.enter_context(nc.sbuf_tensor(f"xin{i}", [P, D], F32)) for i in range(2)]
        x2t = [esB.enter_context(nc.sbuf_tensor(f"x2t{i}", [P, D], F32)) for i in range(2)]

        if "B" in phases:
            emit_B(0)
            emit_B(1)
        barrier()
        esB.close()
        esAB.close()

        if "F" in phases:
            dFc = Sm(es, nc, "dFc")
            esF = ExitStack()
            muu = esF.enter_context(nc.sbuf_tensor("muu", [P, 2048], BF16))
            gam_s = esF.enter_context(nc.sbuf_tensor("gam_s", [P, D], F32))
            bet_s = esF.enter_context(nc.sbuf_tensor("bet_s", [P, D], F32))
            scl_s = esF.enter_context(nc.sbuf_tensor("scl_s", [P, 1], F32))
            b1f_s = esF.enter_context(nc.sbuf_tensor("b1f_s", [P, 32], F32))
            b2r_s = esF.enter_context(nc.sbuf_tensor("b2r_s", [P, 2, D], BF16))
            esFpre = ExitStack()
            mu_sb = esFpre.enter_context(nc.sbuf_tensor("mu_sb", [P, 2048], F32))
            musq = esFpre.enter_context(nc.sbuf_tensor("musq", [P, 2048], BF16))
            ssmu = esFpre.enter_context(nc.sbuf_tensor("ssmu", [1, 1024], F32))
            rmub = esFpre.enter_context(nc.sbuf_tensor("rmub", [P, 256], F32))
            def _(g):
                g.dma_start(out=mu_sb[:], in_=mu_d).then_inc(dFc.h, 16)
                g.dma_start(out=gam_s[:], in_=gam_d).then_inc(dFc.h, 16)
                g.dma_start(out=bet_s[:], in_=bet_d).then_inc(dFc.h, 16)
                g.dma_start(out=scl_s[:], in_=scl_d).then_inc(dFc.h, 16)
                g.dma_start(out=b1f_s[:], in_=b1_d).then_inc(dFc.h, 16)
                g.dma_start(out=b2r_s[:], in_=b2_d.rearrange("(c p) d -> p c d", p=P)).then_inc(dFc.h, 16)
                dFc.n += 96
            G(_)
            w_fc = dFc.n

            def _(s):
                s.wait_ge(dFc.h, w_fc)
                sA.inc(s.activation(musq[:], mu_sb[:], AF.Square))
            A(_)
            a_musq = sA.n

            def _(t):
                t.wait_ge(sA.h, a_musq)
                for kc in range(8):
                    ii = t.matmul(pD[0:1, 0:256], onescolb[:], musq[:, kc * 256:kc * 256 + 256],
                                  start=(kc == 0), stop=(kc == 7))
                sP.inc(ii)
            T(_)
            p_mu = sP.n

            def _(s):
                s.wait_ge(sP.h, p_mu)
                sA.inc(s.copy(ssmu[0:1, 0:256], pD[0:1, 0:256]))
                s.wait_ge(sA.h, sA.n)
                sA.inc(s.activation(ssmu[0:1, 256:512], ssmu[0:1, 0:256], AF.Ln))
                s.wait_ge(sA.h, sA.n)
                sA.inc(s.activation(ssmu[0:1, 512:768], ssmu[0:1, 256:512], AF.Exp, scale=0.5))
            A(_)
            a_nmu = sA.n

            def _(v):
                v.wait_ge(sA.h, a_nmu)
                sV.inc(v.tensor_scalar(ssmu[0:1, 768:1024], ssmu[0:1, 512:768], 1e-12, None, op0=ALU.max))
                v.wait_ge(sV.h, sV.n)
                sV.inc(v.reciprocal(ssmu[0:1, 0:256], ssmu[0:1, 768:1024]))
            V(_)
            v_rmu = sV.n

            def _(t):
                t.wait_ge(sV.h, v_rmu)
                sP.inc(t.matmul(pD[0:128, 0:256], ones1f[:], ssmu[0:1, 0:256], start=True, stop=True))
            T(_)
            p_rmub = sP.n

            def _(s):
                s.wait_ge(sP.h, p_rmub)
                sA.inc(s.copy(rmub[:], pD[0:128, 0:256]))
            A(_)
            a_rmub = sA.n

            def _(v):
                v.wait_ge(sA.h, a_rmub)
                for kc in range(8):
                    sV.inc(v.tensor_tensor(muu[:, kc * 256:kc * 256 + 256], mu_sb[:, kc * 256:kc * 256 + 256],
                                           rmub[:], op=ALU.mult))
            V(_)
            v_muu = sV.n

            esFpre.close()
            xnn_sb = esF.enter_context(nc.sbuf_tensor("xnn_sb", [P, NT, D], BF16))
            lgsm = esF.enter_context(nc.sbuf_tensor("lgsm", [P, 2, 2048], BF16))
            comb = esF.enter_context(nc.sbuf_tensor("comb", [P, 2, NT, 256], BF16))
            si_sb = esF.enter_context(nc.sbuf_tensor("si_sb", [P, 2, 8, 256], BF16))
            so_stg = esF.enter_context(nc.sbuf_tensor("so_stg", [P, 2, 2, D], BF16))
            so_bf = esF.enter_context(nc.sbuf_tensor("so_bf", [P, 2, 2, D], BF16))
            w1buf = [esF.enter_context(nc.sbuf_tensor(f"w1b{i}", [P, 2048], BF16)) for i in range(2)]
            w2buf = [esF.enter_context(nc.sbuf_tensor(f"w2b{i}", [P, 2048], BF16)) for i in range(2)]
            so_st = [esF.enter_context(nc.sbuf_tensor(f"sost{i}", [32, 1024], F32)) for i in range(2)]
            h_bf = esF.enter_context(nc.sbuf_tensor("h_bf", [P, 64], BF16))
            x2f = [esF.enter_context(nc.sbuf_tensor(f"x2f{i}", [P, D], F32)) for i in range(2)]
            t1f = esF.enter_context(nc.sbuf_tensor("t1f", [P, D], F32))
            xu_t = esF.enter_context(nc.sbuf_tensor("xu_t", [P, D], BF16))
            xuf = esF.enter_context(nc.sbuf_tensor("xuf", [P, D], BF16))
            lg_t = esF.enter_context(nc.sbuf_tensor("lg_t", [P, 256], F32))
            expc = esF.enter_context(nc.sbuf_tensor("expc", [P, 256], F32))
            bnF = esF.enter_context(nc.sbuf_tensor("bnF", [P, 12], F32))
            mvF = esF.enter_context(nc.sbuf_tensor("mvF", [P, 2], F32))
            scF = esF.enter_context(nc.sbuf_tensor("scF", [P, 16], F32))
            scD = esF.enter_context(nc.sbuf_tensor("scD", [P, 4], F32))
            scC = esF.enter_context(nc.sbuf_tensor("scC", [P, 4], F32))
            dspb = esF.enter_context(nc.sbuf_tensor("dspb", [P, 2048], BF16))
            esFd = ExitStack()
            dtm = esFd.enter_context(nc.sbuf_tensor("dtm", [P, NT, 256], BF16))

            ftrk = {"ptrb": 0, "ptrf": 0, "pA": 0, "si": {}, "comb": {}}

            def emit_F1(b):
                for tt in range(NT):
                    par = tt % 2
                    xfree = ftrk.get(f"x2f{par}", 0)

                    def _(q, b=b, tt=tt, par=par, xfree=xfree):
                        if xfree:
                            q.wait_ge(sV.h, xfree)
                        if b == 0 and tt < 2:
                            q.wait_ge(sV.h, v_muu)
                        dXp[par].inc(q.dma_start(out=x2f[par][:], in_=x2_tile_v[b, tt]), 16)
                    Q(_)
                    x_ld = dXp[par].n

                    def _(v, par=par, x_ld=x_ld):
                        v.wait_ge(dXp[par].h, x_ld)
                        sV.inc(v.bn_stats(bnF[:, 0:6], x2f[par][:, 0:512]))
                        sV.inc(v.bn_stats(bnF[:, 6:12], x2f[par][:, 512:1024]))
                        v.wait_ge(sV.h, sV.n)
                        sV.inc(v.bn_aggr(mvF[:], bnF[:]))
                        v.wait_ge(sV.h, sV.n)
                        sV.inc(v.tensor_scalar(scF[:, 0:1], mvF[:, 1:2], EPS, None, op0=ALU.add))
                    V(_)
                    v_u3 = sV.n

                    def _(g, v_u3=v_u3):
                        g.wait_ge(sV.h, v_u3)
                        sG.inc(g.tensor_tensor(scF[:, 2:3], scF[:, 0:1], nh1_s[:], op=ALU.pow))
                    G(_)
                    g_r3 = sG.n

                    def _(v, g_r3=g_r3):
                        v.wait_ge(sG.h, g_r3)
                        sV.inc(v.tensor_scalar(scF[:, 3:4], mvF[:, 1:2], scF[:, 2:3], scF[:, 2:3], op0=ALU.mult, op1=ALU.mult))
                        v.wait_ge(sV.h, sV.n)
                        sV.inc(v.tensor_scalar(scF[:, 4:5], scF[:, 3:4], EPS, None, op0=ALU.add))
                    V(_)
                    v_v4 = sV.n

                    def _(g, v_v4=v_v4):
                        g.wait_ge(sV.h, v_v4)
                        sG.inc(g.tensor_tensor(scF[:, 6:7], scF[:, 4:5], nh1_s[:], op=ALU.pow))
                    G(_)
                    a_r4 = None
                    g_r4 = sG.n

                    p_xuT_prev = ftrk.get("p_xuT", 0)

                    si_mm = ftrk.get("si_mm_done", 0)

                    def _(v, par=par, tt=tt, g_r4=g_r4, p_xuT_prev=p_xuT_prev, si_mm=si_mm):
                        v.wait_ge(sG.h, g_r4)
                        if p_xuT_prev:
                            v.wait_ge(sP.h, p_xuT_prev)
                        sV.inc(v.tensor_tensor(scF[:, 7:8], scF[:, 2:3], scF[:, 6:7], op=ALU.mult))
                        v.wait_ge(sV.h, sV.n)
                        sV.inc(v.tensor_scalar(scF[:, 8:9], mvF[:, 0:1], scF[:, 7:8], -1.0, op0=ALU.mult, op1=ALU.mult))
                        v.wait_ge(sV.h, sV.n)
                        sV.inc(v.tensor_scalar(t1f[:], x2f[par][:], scF[:, 7:8], scF[:, 8:9], op0=ALU.mult, op1=ALU.add))
                        v.wait_ge(sV.h, sV.n)
                        sV.inc(v.tensor_tensor(t1f[:], t1f[:], gam_s[:], op=ALU.mult))
                        v.wait_ge(sV.h, sV.n)
                        if si_mm:
                            v.wait_ge(sP.h, si_mm)
                        sV.inc(v.tensor_tensor(xnn_sb[:, tt, :], t1f[:], bet_s[:], op=ALU.add))
                        v.wait_ge(sV.h, sV.n)
                        sV.inc(v.bn_stats(bnF[:, 0:6], xnn_sb[:, tt, 0:512]))
                        sV.inc(v.bn_stats(bnF[:, 6:12], xnn_sb[:, tt, 512:1024]))
                        v.wait_ge(sV.h, sV.n)
                        sV.inc(v.bn_aggr(mvF[:], bnF[:]))
                        v.wait_ge(sV.h, sV.n)
                        sV.inc(v.tensor_tensor(scF[:, 9:10], mvF[:, 0:1], mvF[:, 0:1], op=ALU.mult))
                        v.wait_ge(sV.h, sV.n)
                        sV.inc(v.tensor_tensor(scF[:, 10:11], scF[:, 9:10], mvF[:, 1:2], op=ALU.add))
                        v.wait_ge(sV.h, sV.n)
                        sV.inc(v.tensor_scalar(scF[:, 11:12], scF[:, 10:11], 1024.0, None, op0=ALU.mult))
                    V(_)
                    v_ss = sV.n

                    def _(g, v_ss=v_ss):
                        g.wait_ge(sV.h, v_ss)
                        sG.inc(g.tensor_tensor(scF[:, 12:13], scF[:, 11:12], ph1_s[:], op=ALU.pow))
                    G(_)
                    a_n5 = None
                    g_n5 = sG.n

                    def _(v, tt=tt, g_n5=g_n5):
                        v.wait_ge(sG.h, g_n5)
                        sV.inc(v.tensor_scalar(scF[:, 13:14], scF[:, 12:13], 1e-12, None, op0=ALU.max))
                        v.wait_ge(sV.h, sV.n)
                        sV.inc(v.reciprocal(scF[:, 14:15], scF[:, 13:14]))
                        v.wait_ge(sV.h, sV.n)
                        sV.inc(v.tensor_tensor(scF[:, 15:16], scF[:, 14:15], scl_s[:], op=ALU.mult))
                        v.wait_ge(sV.h, sV.n)
                        sV.inc(v.tensor_scalar(xu_t[:], xnn_sb[:, tt, :], scF[:, 15:16], None, op0=ALU.mult))
                    V(_)
                    v_xu = sV.n

                    g_xnn = sG.n
                    ftrk["g_xnn"] = g_xnn


                    ptrb_free = ftrk["ptrb"]

                    def _(t, v_xu=v_xu, ptrb_free=ptrb_free):
                        t.wait_ge(sV.h, v_xu)
                        if ptrb_free:
                            t.wait_ge(sA.h, ptrb_free)
                        for kc in range(8):
                            ii = t.transpose(ptrb[:, kc * 128:kc * 128 + 128], xu_t[:, kc * 128:kc * 128 + 128], identb[:])
                        sP.inc(ii)
                    T(_)
                    ftrk["p_xuT"] = sP.n
                    p_xuT = sP.n

                    def _(s, p_xuT=p_xuT):
                        s.wait_ge(sP.h, p_xuT)
                        sA.inc(s.copy(xuf[:], ptrb[:]))
                    A(_)
                    a_xuf = sA.n
                    ftrk["ptrb"] = a_xuf

                    pa_free = ftrk["pA"]

                    def _(t, a_xuf=a_xuf, pa_free=pa_free, v_muu=v_muu):
                        t.wait_ge(sA.h, a_xuf)
                        t.wait_ge(sV.h, v_muu)
                        if pa_free:
                            t.wait_ge(sA.h, pa_free)
                        for kc in range(8):
                            ii = t.matmul(pA[:, 0:256], xuf[:, kc * 128:kc * 128 + 128],
                                          muu[:, kc * 256:kc * 256 + 256],
                                          start=(kc == 0), stop=(kc == 7))
                        sP.inc(ii)
                    T(_)
                    p_lg2 = sP.n

                    def _(s, p_lg2=p_lg2):
                        s.wait_ge(sP.h, p_lg2)
                        sA.inc(s.copy(lg_t[:], pA[:, 0:256]))
                    A(_)
                    a_lg = sA.n

                    def _(v, a_lg=a_lg):
                        v.wait_ge(sA.h, a_lg)
                        sV.inc(v.tensor_reduce(scC[:, 0:1], lg_t[:], axis=AX.X, op=ALU.max, negate=True))
                    V(_)
                    v_nmc = sV.n

                    def _(s, v_nmc=v_nmc):
                        s.wait_ge(sV.h, v_nmc)
                        sA.inc(s.activation(expc[:], lg_t[:], AF.Exp, bias=scC[:, 0:1], accum_out=scC[:, 1:2]))
                    A(_)
                    a_expc = sA.n

                    def _(v, b=b, tt=tt, a_expc=a_expc):
                        v.wait_ge(sA.h, a_expc)
                        sV.inc(v.reciprocal(scC[:, 2:3], scC[:, 1:2]))
                        v.wait_ge(sV.h, sV.n)
                        sV.inc(v.tensor_scalar(comb[:, b, tt, :], expc[:], scC[:, 2:3], None, op0=ALU.mult))
                    V(_)
                    ftrk["comb"][b] = sV.n

                    ptrf_free = ftrk["ptrf"]

                    def _(t, a_lg=a_lg, ptrf_free=ptrf_free):
                        t.wait_ge(sA.h, a_lg)
                        if ptrf_free:
                            t.wait_ge(sA.h, ptrf_free)
                        t.transpose(pA[:, 256:384], lg_t[:, 0:128], identf[:])
                        sP.inc(t.transpose(pA[:, 384:512], lg_t[:, 128:256], identf[:]))
                    T(_)
                    p_lgT = sP.n

                    disp_done = ftrk.get("disp_done_v", 0)

                    def _(s, b=b, tt=tt, p_lgT=p_lgT, disp_done=disp_done):
                        s.wait_ge(sP.h, p_lgT)
                        if disp_done:
                            s.wait_ge(sV.h, disp_done)
                        sA.inc(s.copy(lgsm[:, :, tt * 128:tt * 128 + 128],
                                      pA[:, 256:512].rearrange("p (c t) -> p c t", t=128)))
                    A(_)
                    a_lgsm = sA.n
                    ftrk["ptrf"] = a_lgsm
                    ftrk["pA"] = a_lgsm
                    ftrk[f"x2f{par}"] = v_ss
                    if tt == 0 and dbg_names:
                        dump(f"xnnf_b{b}", xnn_sb[:, tt, :], sV, v_ss)
                        dump(f"lg_b{b}", lg_t[:], sA, a_lg)
                        dump(f"comb0_b{b}", comb[:, b, 0, :], sV, ftrk["comb"][b])

                ftrk[f"lgsm_done"] = sA.n
                ftrk[f"xnn_done"] = sV.n
                if dbg_names:
                    dump(f"xnn14_a_b{b}", xnn_sb[:, 14, :], sG, sG.n)

                for c in range(2):
                    prev_dsp = ftrk.get("a_dsp_last", 0)

                    def _(v, c=c, prev_dsp=prev_dsp):
                        v.wait_ge(sA.h, ftrk["lgsm_done"])
                        if prev_dsp:
                            v.wait_ge(sA.h, prev_dsp)
                        sV.inc(v.tensor_reduce(scD[:, 0:1], lgsm[:, c, :], axis=AX.X, op=ALU.max, negate=True))
                    V(_)
                    v_nmd = sV.n

                    dspb_free = ftrk.get("dspb_free", 0)

                    def _(s, c=c, v_nmd=v_nmd, dspb_free=dspb_free):
                        s.wait_ge(sV.h, v_nmd)
                        if dspb_free:
                            s.wait_ge(sP.h, dspb_free)
                        sA.inc(s.activation(dspb[:], lgsm[:, c, :], AF.Exp, bias=scD[:, 0:1], accum_out=scD[:, 1:2]))
                        s.wait_ge(sA.h, sA.n)
                        sA.inc(s.activation(scD[:, 2:3], scD[:, 1:2], AF.Ln))
                    A(_)
                    a_lnz = sA.n

                    def _(v, a_lnz=a_lnz):
                        v.wait_ge(sA.h, a_lnz)
                        sV.inc(v.tensor_scalar(scD[:, 3:4], scD[:, 2:3], -1.0, scD[:, 0:1], op0=ALU.mult, op1=ALU.add))
                    V(_)
                    v_b2x = sV.n
                    ftrk["v_b2x"] = v_b2x

                    def _(s, c=c, v_b2x=v_b2x):
                        s.wait_ge(sV.h, v_b2x)
                        sA.inc(s.activation(dspb[:], lgsm[:, c, :], AF.Exp, bias=scD[:, 3:4]))
                    A(_)
                    v_dsp = sV.n
                    a_dsp = sA.n
                    ftrk["a_dsp_last"] = a_dsp

                    for rnd in range(2):
                        ptrb_free = ftrk["ptrb"]

                        def _(t, c=c, rnd=rnd, a_dsp=a_dsp, ptrb_free=ptrb_free):
                            t.wait_ge(sA.h, a_dsp)
                            if ptrb_free:
                                t.wait_ge(sA.h, ptrb_free)
                            for j in range(8):
                                tt = rnd * 8 + j
                                ii = t.transpose(ptrb[:, j * 128:j * 128 + 128],
                                                 dspb[:, tt * 128:tt * 128 + 128], identb[:])
                            sP.inc(ii)
                        T(_)
                        p_dT = sP.n
                        ftrk["dspb_free"] = p_dT
                        si_mm_done = ftrk.get("si_mm_done", 0)

                        def _(s, c=c, rnd=rnd, p_dT=p_dT, si_mm_done=si_mm_done):
                            s.wait_ge(sP.h, p_dT)
                            if si_mm_done:
                                s.wait_ge(sP.h, si_mm_done)
                            sA.inc(s.copy(dtm[:, rnd * 8:rnd * 8 + 8, c * 128:c * 128 + 128],
                                          ptrb[:].rearrange("p (j t) -> p j t", t=128)))
                        A(_)
                        ftrk["ptrb"] = sA.n
                v_dtm = sA.n
                ftrk["disp_done_v"] = sV.n
                if dbg_names:
                    dump(f"dtmA_b{b}", dtm[:, 0, :], sA, v_dtm)

                prev_si = 0
                for dc in range(8):
                    bank = pB if dc % 2 == 0 else pC

                    def _(t, b=b, dc=dc, bank=bank, v_dtm=v_dtm, prev_si=prev_si):
                        t.wait_ge(sA.h, v_dtm)
                        t.wait_ge(sV.h, ftrk["xnn_done"])
                        if prev_si:
                            t.wait_ge(sA.h, prev_si)
                        for tt in range(NT):
                            ii = t.matmul(bank[:, 0:256], xnn_sb[:, tt, dc * 128:dc * 128 + 128],
                                          dtm[:, tt, :], start=(tt == 0), stop=(tt == NT - 1))
                        sP.inc(ii)
                    T(_)
                    p_si = sP.n

                    def _(s, b=b, dc=dc, bank=bank, p_si=p_si):
                        s.wait_ge(sP.h, p_si)
                        sA.inc(s.copy(si_sb[:, b, dc, :], bank[:, 0:256]))
                    A(_)
                    prev_si = sA.n if dc >= 1 else 0
                    if dc >= 1:
                        prev_si = ftrk["si"].get(dc - 1, 0)
                    ftrk["si"][dc] = sA.n
                ftrk[f"si_done{b}"] = sA.n
                ftrk["si_mm_done"] = sP.n
                if dbg_names:
                    dump(f"dtmB_b{b}", dtm[:, 0, :], sP, sP.n)
                if dbg_names:
                    dump(f"xnn14_b_b{b}", xnn_sb[:, 14, :], sP, sP.n)
                if dbg_names:
                    dump(f"si_b{b}", si_sb[:, b], sA, sA.n)
                    dump(f"dtm_b{b}", dtm[:, 0, :], sA, sA.n)
                    dump(f"dtm7_b{b}", dtm[:, 7, :], sA, sA.n)
                    dump(f"dtm8_b{b}", dtm[:, 8, :], sA, sA.n)
                    dump(f"dtm15_b{b}", dtm[:, 15, :], sA, sA.n)

            def emit_experts():
                si_done = max(ftrk["si_done0"], ftrk["si_done1"])
                for e in range(E):
                    par = e % 2
                    wfree = ftrk.get(f"wbuf{par}", 0)

                    def _(g, e=e, par=par, wfree=wfree):
                        if wfree:
                            g.wait_ge(sP.h, wfree)
                        if e < 2:
                            g.wait_ge(sV.h, v_muu)
                        g.dma_start(out=w1buf[par][:], in_=w1_d[e]).then_inc(dE[par].h, 16)
                        g.dma_start(out=w2buf[par][:], in_=w2_d[e]).then_inc(dE[par].h, 16)
                        dE[par].n += 32
                    G(_)
                    w_e = dE[par].n

                    pd_free = ftrk.get("pD_exp", 0)

                    def _(t, e=e, par=par, w_e=w_e, si_done=si_done, pd_free=pd_free):
                        t.wait_ge(dE[par].h, w_e)
                        t.wait_ge(sA.h, si_done)
                        if pd_free:
                            t.wait_ge(sA.h, pd_free)
                        for dc in range(8):
                            for hc in range(2):
                                ii = t.matmul(pD[:, hc * 32:hc * 32 + 32],
                                              w1buf[par][:, dc * 256 + hc * 128:dc * 256 + hc * 128 + 128],
                                              si_sb[:, :, dc, e * 16:e * 16 + 16],
                                              start=(dc == 0 and hc == 0), stop=(dc == 7 and hc == 1),
                                              skip_group_check=True)
                        sP.inc(ii)
                    T(_)
                    p_h = sP.n

                    def _(s, e=e, p_h=p_h):
                        s.wait_ge(sP.h, p_h)
                        sA.inc(s.activation(h_bf[:, 0:32], pD[:, 0:32], AF.Gelu, bias=b1f_s[:, 2 * e:2 * e + 1]))
                        sA.inc(s.activation(h_bf[:, 32:64], pD[:, 32:64], AF.Gelu, bias=b1f_s[:, 2 * e + 1:2 * e + 2]))
                    A(_)
                    a_h = sA.n
                    ftrk["pD_exp"] = a_h

                    pe_free = ftrk.get("pE_exp", 0)

                    def _(t, e=e, par=par, a_h=a_h, pe_free=pe_free):
                        t.wait_ge(sA.h, a_h)
                        if pe_free:
                            t.wait_ge(sA.h, pe_free)
                        for dh in range(2):
                            for hc in range(2):
                                ii = t.matmul(pE[0:32, dh * 512:dh * 512 + 512],
                                              h_bf[:, hc * 32:hc * 32 + 32],
                                              w2buf[par][:, hc * 1024 + dh * 512:hc * 1024 + dh * 512 + 512],
                                              start=(hc == 0), stop=(hc == 1), skip_group_check=True)
                        sP.inc(ii)
                    T(_)
                    p_so = sP.n
                    ftrk[f"wbuf{par}"] = p_so

                    spar = e % 2
                    stfree = ftrk.get(f"sost{spar}", 0)

                    def _(s, e=e, spar=spar, p_so=p_so, stfree=stfree):
                        s.wait_ge(sP.h, p_so)
                        if stfree:
                            s.wait_ge(dSh[spar].h, stfree)
                        sA.inc(s.copy(so_st[spar][0:32, 0:512], pE[0:32, 0:512]))
                        sA.inc(s.copy(so_st[spar][0:32, 512:1024], pE[0:32, 512:1024]))
                    A(_)
                    a_so = sA.n
                    ftrk["pE_exp"] = a_so

                    def _(g, e=e, spar=spar, a_so=a_so):
                        g.wait_ge(sA.h, a_so)
                        r0 = (e % 8) * 16
                        g.dma_start(out=so_stg[r0:r0 + 16, 0, e // 8, :], in_=so_st[spar][0:16, :]).then_inc(dSh[spar].h, 16)
                        g.dma_start(out=so_stg[r0:r0 + 16, 1, e // 8, :], in_=so_st[spar][16:32, :]).then_inc(dSh[spar].h, 16)
                        dSh[spar].n += 32
                    G(_)
                    ftrk[f"sost{spar}"] = dSh[spar].n

                def _(v):
                    v.wait_ge(dSh[0].h, dSh[0].n)
                    v.wait_ge(dSh[1].h, dSh[1].n)
                    for bb in range(2):
                        for c in range(2):
                            sV.inc(v.tensor_tensor(so_bf[:, bb, c, :], so_stg[:, bb, c, :],
                                                   b2r_s[:, c, :], op=ALU.add))
                V(_)
                ftrk["so_ready"] = sV.n
                if dbg_names:
                    dump("sobf_b0", so_bf[:, 0], sV, sV.n)
                    dump("sobf_b1", so_bf[:, 1], sV, sV.n)

            def emit_F3(b):
                for tt in range(NT):
                    par = tt % 2
                    xfree = max(ftrk.get(f"x2r{par}", 0), ftrk.get(f"x2f{par}", 0))

                    def _(q, b=b, tt=tt, par=par, xfree=xfree):
                        if xfree:
                            q.wait_ge(sV.h, xfree)
                        dXp[par].inc(q.dma_start(out=x2f[par][:], in_=x2_tile_v[b, tt]), 16)
                    Q(_)
                    x_ld = dXp[par].n

                    ptrb_free = ftrk["ptrb"]

                    def _(t, b=b, tt=tt, ptrb_free=ptrb_free):
                        t.wait_ge(sV.h, ftrk["comb"][b])
                        if ptrb_free:
                            t.wait_ge(sA.h, ptrb_free)
                        t.transpose(ptrb[:, 0:128], comb[:, b, tt, 0:128], identb[:])
                        sP.inc(t.transpose(ptrb[:, 128:256], comb[:, b, tt, 128:256], identb[:]))
                    T(_)
                    p_cT = sP.n

                    def _(s, p_cT=p_cT):
                        s.wait_ge(sP.h, p_cT)
                        sA.inc(s.copy(cb_sm[:], ptrb[:, 0:256]))
                    A(_)
                    a_cb = sA.n
                    ftrk["ptrb"] = a_cb

                    pbc_free = ftrk.get("pBC_F3", 0)

                    def _(t, b=b, a_cb=a_cb, pbc_free=pbc_free):
                        t.wait_ge(sA.h, a_cb)
                        t.wait_ge(sV.h, ftrk["so_ready"])
                        if pbc_free:
                            t.wait_ge(sV.h, pbc_free)
                        for dh in range(2):
                            bank = pB if dh == 0 else pC
                            for c in range(2):
                                ii = t.matmul(bank[:, 0:512], cb_sm[:, c * 128:c * 128 + 128],
                                              so_bf[:, b, c, dh * 512:dh * 512 + 512],
                                              start=(c == 0), stop=(c == 1), skip_group_check=True)
                        sP.inc(ii)
                    T(_)
                    p_o = sP.n

                    def _(v, p_o=p_o):
                        v.wait_ge(sP.h, p_o)
                        sV.inc(v.bn_stats(bnF[:, 0:6], pB[:, 0:512]))
                        sV.inc(v.bn_stats(bnF[:, 6:12], pC[:, 0:512]))
                        v.wait_ge(sV.h, sV.n)
                        sV.inc(v.bn_aggr(mvF[:], bnF[:]))
                        v.wait_ge(sV.h, sV.n)
                        sV.inc(v.tensor_scalar(scF[:, 0:1], mvF[:, 1:2], EPS, None, op0=ALU.add))
                    V(_)
                    v_u6 = sV.n

                    def _(g, v_u6=v_u6):
                        g.wait_ge(sV.h, v_u6)
                        sG.inc(g.tensor_tensor(scF[:, 3:4], scF[:, 0:1], nh1_s[:], op=ALU.pow))
                    G(_)
                    a_r6 = None
                    g_r6 = sG.n

                    ofree = ftrk.get(f"ostore{par}", 0)

                    def _(v, b=b, tt=tt, par=par, g_r6=g_r6, x_ld=x_ld, ofree=ofree):
                        v.wait_ge(sG.h, g_r6)
                        v.wait_ge(dXp[par].h, x_ld)
                        if ofree:
                            v.wait_ge(dOp[par].h, ofree)
                        sV.inc(v.tensor_scalar(scF[:, 4:5], mvF[:, 0:1], scF[:, 3:4], -1.0, op0=ALU.mult, op1=ALU.mult))
                        v.wait_ge(sV.h, sV.n)
                        sV.inc(v.tensor_scalar(ln5_t[:, 0:512], pB[:, 0:512], scF[:, 3:4], scF[:, 4:5], op0=ALU.mult, op1=ALU.add))
                        sV.inc(v.tensor_scalar(ln5_t[:, 512:1024], pC[:, 0:512], scF[:, 3:4], scF[:, 4:5], op0=ALU.mult, op1=ALU.add))
                        v.wait_ge(sV.h, sV.n)
                        sV.inc(v.tensor_tensor(out_t[par][:], x2f[par][:], ln5_t[:], op=ALU.add))
                    V(_)
                    v_out = sV.n
                    ftrk["pBC_F3"] = v_out
                    ftrk[f"x2r{par}"] = v_out

                    def _(q, b=b, tt=tt, par=par, v_out=v_out):
                        q.wait_ge(sV.h, v_out)
                        dOp[par].inc(q.dma_start(out=out_tile_v[b, tt], in_=out_t[par][:]), 16)
                    Q(_)
                    ftrk[f"ostore{par}"] = dOp[par].n

            emit_F1(0)
            emit_F1(1)
            esFd.close()
            cb_sm = esF.enter_context(nc.sbuf_tensor("cb_sm", [P, 256], BF16))
            ln5_t = esF.enter_context(nc.sbuf_tensor("ln5_t", [P, D], F32))
            out_t = [esF.enter_context(nc.sbuf_tensor(f"outt{i}", [P, D], F32)) for i in range(2)]
            emit_experts()
            emit_F3(0)
            emit_F3(1)
            esF.close()

        x2_done = (dSp[0].n, dSp[1].n)

        def _(q):
            if x2_done[0]:
                q.wait_ge(dSp[0].h, x2_done[0])
            if x2_done[1]:
                q.wait_ge(dSp[1].h, x2_done[1])
            if "B" in phases and "F" not in phases:
                for b in range(2):
                    dO.inc(q.dma_start(out=out_d[b], in_=x2scr[b]), 16)
            if dO.n:
                q.wait_ge(dO.h, dO.n)
            if dOp[0].n:
                q.wait_ge(dOp[0].h, dOp[0].n)
            if dOp[1].n:
                q.wait_ge(dOp[1].h, dOp[1].n)
            if dDbg.n:
                q.wait_ge(dDbg.h, dDbg.n)
        Q(_)

    f = nc.m.functions[0]
    blocks = f.blocks
    if len(blocks) > 2:
        def ekey(bb):
            engs = {str(getattr(ins, "engine", None)) for ins in bb.instructions}
            return sorted(engs)[0] if engs else "zz"
        f.blocks = [blocks[0]] + sorted(blocks[1:-1], key=ekey) + [blocks[-1]]
    return nc, dbg_outs


def _host_prep(inp):
    f = np.float32
    Wq, Wk, Wv, Wo = (np.asarray(inp[k], f) for k in ("L0_attn_Wq", "L0_attn_Wk", "L0_attn_Wv", "L0_attn_Wo"))

    def wlay(W):
        return np.ascontiguousarray(
            W.T.reshape(8, 128, 8, 128).transpose(1, 0, 2, 3).reshape(128, 8192))

    planes_T = np.asarray(inp["L0_attn_planes_T"], f)
    negA = np.zeros((64, 1024), f)
    for j in range(64):
        negA[j, j * 16:(j + 1) * 16] = -1.0
    negB = np.zeros((8, 1024), f)
    for j in range(8):
        negB[j, j * 128:(j + 1) * 128] = -1.0
    mu = np.asarray(inp["L0_ffn_mu"], f).reshape(1024, 256)
    w1 = np.asarray(inp["L0_ffn_w1"], f)
    w2 = np.asarray(inp["L0_ffn_w2"], f)
    b1 = np.asarray(inp["L0_ffn_b1"], f)
    com = {
        "wq": wlay(Wq), "wk": wlay(Wk), "wv": wlay(Wv), "wo": wlay(Wo),
        "bo": np.asarray(inp["L0_attn_bo"], f).reshape(1, 1024),
        "planes2": np.concatenate([planes_T, planes_T], 0),
        "protos": np.asarray(inp["L0_attn_protos_T"], f),
        "ltemp": np.asarray(inp["L0_attn_logit_temp"], f).reshape(1, 1),
        "negindA": negA, "negindB": negB,
        "ident": np.eye(128, dtype=f),
        "ones1": np.ones((1, 128), f),
        "onescol": np.ones((128, 1), f),
        "nh8": np.full((128, 8), -0.5, f),
        "nh1": np.full((128, 1), -0.5, f),
        "ph1": np.full((128, 1), 0.5, f),
        "mul": np.ascontiguousarray(mu.reshape(8, 128, 256).transpose(1, 0, 2).reshape(128, 2048)),
        "gam": np.ascontiguousarray(np.broadcast_to(np.asarray(inp["L0_ffn_ln_gamma"], f), (128, 1024))),
        "bet": np.ascontiguousarray(np.broadcast_to(np.asarray(inp["L0_ffn_ln_beta"], f), (128, 1024))),
        "scl": np.ascontiguousarray(np.broadcast_to(np.asarray(inp["L0_ffn_scale"], f).reshape(1, 1), (128, 1))),
        "w1l": np.ascontiguousarray(w1.reshape(E, 8, 128, 2, 128).transpose(0, 2, 1, 3, 4).reshape(E, 128, 2048)),
        "w2l": np.ascontiguousarray(w2.reshape(E, 2, 128, 2, 512).transpose(0, 2, 1, 3, 4).reshape(E, 128, 2048)),
        "b1f": np.ascontiguousarray(b1.reshape(16, 2, 128).transpose(2, 0, 1).reshape(128, 32)),
        "b2r": np.ascontiguousarray(np.repeat(np.asarray(inp["L0_ffn_b2"], f), 16, axis=0)),
    }
    return com


def kernel(**inputs):
    x = np.ascontiguousarray(np.asarray(inputs["L0_attn_x"], np.float32))
    com = _host_prep(inputs)
    nc, _ = build_program(DBG_NAMES)
    in_maps = []
    for c in range(8):
        m = dict(com)
        m["x"] = np.ascontiguousarray(x[2 * c:2 * c + 2])
        in_maps.append(m)
    res = run_bass_kernel_spmd(nc, in_maps, core_ids=list(range(8)))
    return np.concatenate([r["out"] for r in res.results], axis=0)
